# version 02
# baseline (speedup 1.0000x reference)
import contextlib
import math

import numpy as np
import concourse.bass as bass
import concourse.mybir as mybir
from concourse.bass_utils import run_bass_kernel_spmd

F32 = mybir.dt.float32
BF16 = mybir.dt.bfloat16
AF = mybir.ActivationFunctionType
ALU = mybir.AluOpType
AX = mybir.AxisListType

D = 4096
S = 4096
NH = 32
NG = 8
HD = 128
NIT = 86
TOWN = 1024
SCALE = 1.0 / math.sqrt(HD)
SHIFT = 20.0
EPS = 1e-5
FFN_BLOCKS = [16, 16, 16, 16, 16, 6]


def tiles_for_core(c):
    out = []
    for p in range(8):
        out.append(((31 - c) - 8 * (p // 2)) if p % 2 == 0 else ((24 + c) - 8 * (p // 2)))
    return out


def wA_of(j):
    return 128 * min(4, -(-(32 - j) // 4))


def wB_of(j):
    return 128 * min(4, -(-(16 - j) // 4))


class Cnt:
    def __init__(self):
        self.v = {}

    def __getitem__(self, k):
        return self.v.get(k, 0)

    def add(self, k, n):
        self.v[k] = self.v.get(k, 0) + n
        return self.v[k]


def build_nc(dump=False, upto="C"):
    nc = bass.Bass("TRN2", target_bir_lowering=False, detect_race_conditions=False)
    scr_kind = "ExternalOutput" if dump else "Internal"

    x_kv = nc.dram_tensor("x_kv", [S, D], F32, kind="ExternalInput").ap()
    x_own = nc.dram_tensor("x_own", [TOWN, D], F32, kind="ExternalInput").ap()
    consts = nc.dram_tensor("consts", [128, 4], F32, kind="ExternalInput").ap()
    ones_row = nc.dram_tensor("ones_row", [1, 128], F32, kind="ExternalInput").ap()
    ones_col = nc.dram_tensor("ones_col", [128, 1], F32, kind="ExternalInput").ap()
    ident_in = nc.dram_tensor("ident_in", [128, 128], F32, kind="ExternalInput").ap()
    masks_in = nc.dram_tensor("masks_in", [32, 128, 128], F32, kind="ExternalInput").ap()
    cosk_in = nc.dram_tensor("cosk_in", [128, S], F32, kind="ExternalInput").ap()
    sink_in = nc.dram_tensor("sink_in", [128, S], F32, kind="ExternalInput").ap()
    cosq_in = nc.dram_tensor("cosq_in", [128, TOWN], F32, kind="ExternalInput").ap()
    sinq_in = nc.dram_tensor("sinq_in", [128, TOWN], F32, kind="ExternalInput").ap()
    wk_lin = nc.dram_tensor("wk_lin", [NG, 128, D], F32, kind="ExternalInput").ap()
    wv_lin = nc.dram_tensor("wv_lin", [32, 128, 1024], F32, kind="ExternalInput").ap()
    wq_lin = nc.dram_tensor("wq_lin", [32, 128, D], F32, kind="ExternalInput").ap()
    wo_lin = nc.dram_tensor("wo_lin", [8, 32, 128, 512], F32, kind="ExternalInput").ap()
    wg_lin = nc.dram_tensor("wg_lin", [NIT, 128, D], F32, kind="ExternalInput").ap()
    wu_lin = nc.dram_tensor("wu_lin", [NIT, 128, D], F32, kind="ExternalInput").ap()
    wd_lin = nc.dram_tensor("wd_lin", [NIT, 128, D], F32, kind="ExternalInput").ap()

    out = nc.dram_tensor("out", [TOWN, D], F32, kind="ExternalOutput").ap()

    wkb = nc.dram_tensor("wkb", [NG, 128, D], BF16, kind="Internal").ap()
    wvb = nc.dram_tensor("wvb", [32, 128, 1024], BF16, kind="Internal").ap()
    ktr_scr = nc.dram_tensor("ktr_scr", [NG, 128, S], BF16, kind=scr_kind).ap()
    v_scr = nc.dram_tensor("v_scr", [32, 128, 1024], BF16, kind=scr_kind).ap()
    qtr_scr = nc.dram_tensor("qtr_scr", [NH, 128, TOWN], BF16, kind=scr_kind).ap()
    xnt_scr = nc.dram_tensor("xnt_scr", [32, 128, TOWN], BF16, kind=scr_kind).ap()
    if dump:
        attn_dump = nc.dram_tensor("attn_dump", [128, NH * TOWN], BF16,
                                   kind="ExternalOutput").ap()

    cnt = Cnt()

    with contextlib.ExitStack() as TOP:
        E = TOP.enter_context
        sems = {}
        semnames = ["yP0", "gP0", "gD2D", "qtb", "xsp", "xrl", "sSQ", "vVAR", "sSD",
                    "vXN", "tTR", "sTE", "tK", "sKE", "vKR", "tVg", "vVE", "tQ",
                    "sQE", "vQR", "tST", "sE", "vMK", "tPV", "sRC", "tBC", "sRB",
                    "vEV", "tY", "vY", "tG", "tU", "sSIL", "vIN", "tD", "vDE"]
        for extra in ["swL2", "swL3", "swL4", "swL5", "swL6", "swL7"]:
            semnames.append(extra)
        for pair in ["xaL", "tbL", "wkL", "wvL", "swL", "kspL", "x2L", "wqL",
                     "qswL", "qspL", "kgl", "vgl", "qgl", "woL", "yYL", "wgL",
                     "wuL", "wdL", "gAL"]:
            semnames += [pair + "0", pair + "1"]
        semnames += ["vspL0", "vspL1", "vspL2", "vspL3"]
        semnames += ["xoL0", "xoL1", "xoL2", "xoL3"]
        for name in semnames:
            sems[name] = E(nc.semaphore(name))

        def W(eng, name, val):
            if val > 0:
                eng.wait_ge(sems[name], val)

        consts_sb = E(nc.sbuf_tensor("consts_sb", [128, 4], F32))
        onesr_sb = E(nc.sbuf_tensor("onesr_sb", [1, 128], F32))
        onesc_sb = E(nc.sbuf_tensor("onesc_sb", [128, 1], BF16))
        ident_sb = E(nc.sbuf_tensor("ident_sb", [128, 128], BF16))
        masks_sb = E(nc.sbuf_tensor("masks_sb", [128, 32 * 128], BF16))
        c_rd = consts_sb[:, 0:1]
        c_eps = consts_sb[:, 1:2]
        c_m1 = consts_sb[:, 2:3]
        c_shift = consts_sb[:, 3:4]

        psf = E(nc.psum_tensor("psf", [128, 3072], F32))
        pst = E(nc.psum_tensor("pst", [128, 2048], BF16))

        def bank(i, w=512, row0=0, rows=128):
            return psf[row0:row0 + rows, i * 512:i * 512 + w]

        def bank2(i, w=1024):
            return psf[:, i * 512:i * 512 + w]

        with nc.Block("p0") as blk:
            @blk.sync
            def _(y):
                y.dma_start(consts_sb[:, :], consts).then_inc(sems["yP0"], 16)
                y.dma_start(onesr_sb[:, :], ones_row).then_inc(sems["yP0"], 16)
                cnt.add("yP0", 32)
                y.wait_ge(sems["yP0"], 32)

            @blk.gpsimd
            def _(g):
                g.dma_start(onesc_sb[:, :], ones_col).then_inc(sems["gP0"], 16)
                g.dma_start(ident_sb[:, :], ident_in).then_inc(sems["gP0"], 16)
                g.dma_start(masks_sb.ap().rearrange("p (j q) -> p j q", q=128),
                            masks_in.transpose([1, 0, 2])).then_inc(sems["gP0"], 16)
                cnt.add("gP0", 48)
                g.dma_start(wkb, wk_lin).then_inc(sems["gD2D"], 16)
                g.dma_start(wvb, wv_lin).then_inc(sems["gD2D"], 16)
                cnt.add("gD2D", 32)
                g.wait_ge(sems["gP0"], 48)

        with contextlib.ExitStack() as ST:
            EA = ST.enter_context
            xa = [EA(nc.sbuf_tensor(f"xa{i}", [128, D], F32)) for i in range(2)]
            xsq = [EA(nc.sbuf_tensor(f"xsq{i}", [128, D], F32)) for i in range(2)]
            xn = [EA(nc.sbuf_tensor(f"xn{i}", [128, D], BF16)) for i in range(2)]
            xnt = EA(nc.sbuf_tensor("xnt", [128, 32 * 512], BF16))
            t_s = EA(nc.sbuf_tensor("t_s", [128, 2], F32))
            t_q = EA(nc.sbuf_tensor("t_q", [128, 2], F32))
            t_mean = EA(nc.sbuf_tensor("t_mean", [128, 2], F32))
            t_msq = EA(nc.sbuf_tensor("t_msq", [128, 2], F32))
            t_v1 = EA(nc.sbuf_tensor("t_v1", [128, 2], F32))
            t_var = EA(nc.sbuf_tensor("t_var", [128, 2], F32))
            t_sd = EA(nc.sbuf_tensor("t_sd", [128, 2], F32))
            t_r = EA(nc.sbuf_tensor("t_r", [128, 2], F32))
            ck = [EA(nc.sbuf_tensor(f"ck{i}", [128, 512], F32)) for i in range(2)]
            sk = [EA(nc.sbuf_tensor(f"sk{i}", [128, 512], F32)) for i in range(2)]
            wkS = [EA(nc.sbuf_tensor(f"wkS{i}", [128, D], BF16)) for i in range(2)]
            wvS = [EA(nc.sbuf_tensor(f"wvS{i}", [128, 4 * 1024], BF16)) for i in range(2)]
            kqf = [EA(nc.sbuf_tensor(f"kqf{i}", [128, 512], F32)) for i in range(8)]
            kswp = [EA(nc.sbuf_tensor(f"kswp{i}", [128, 512], F32)) for i in range(8)]
            ropet = EA(nc.sbuf_tensor("ropet", [128, 512], F32))
            ropet2 = EA(nc.sbuf_tensor("ropet2", [128, 512], F32))
            ktr_t = [EA(nc.sbuf_tensor(f"ktr_t{i}", [128, 512], BF16)) for i in range(2)]
            v_bf = [EA(nc.sbuf_tensor(f"v_bf{i}", [128, 1024], BF16)) for i in range(4)]

            def wk_lidx(c, g):
                return c * 8 + g

            def wv_lidx(c, h2, grp):
                return c * 16 + h2 * 8 + grp

            def gw_A(c, g=None, h2=None, grp=None):
                if g is not None:
                    return c * 24 + g + 1
                return c * 24 + 8 + h2 * 8 + grp + 1

            with nc.Block("pA") as blk:
                @blk.gpsimd
                def _(g):
                    g.wait_ge(sems["gD2D"], 32)
                    for c in range(8):
                        for gg in range(8):
                            kidx = wk_lidx(c, gg)
                            W(g, "tK", kidx - 1)
                            g.dma_start(wkS[kidx % 2][:, :], wkb[gg]).then_inc(sems[f"wkL{kidx % 2}"], 16)
                            cnt.add(f"wkL{kidx % 2}", 16)
                        for h2 in range(2):
                            for grp in range(8):
                                vl = wv_lidx(c, h2, grp)
                                W(g, "tVg", vl - 1)
                                g.dma_start(
                                    wvS[vl % 2].ap().rearrange("p (d q) -> p d q", q=1024),
                                    wvb[4 * grp:4 * grp + 4].transpose([1, 0, 2]),
                                ).then_inc(sems[f"wvL{vl % 2}"], 16)
                                cnt.add(f"wvL{vl % 2}", 16)

                @blk.sync
                def _(y):
                    y.wait_ge(sems["yP0"], 32)
                    for c in range(9):
                        if c < 8:
                            W(y, "vKR", 8 * (c - 1))
                            y.dma_start(ck[c % 2][:, :], cosk_in[:, c * 512:(c + 1) * 512]).then_inc(sems[f"tbL{c % 2}"], 16)
                            y.dma_start(sk[c % 2][:, :], sink_in[:, c * 512:(c + 1) * 512]).then_inc(sems[f"tbL{c % 2}"], 16)
                            cnt.add(f"tbL{c % 2}", 32)
                            for tt in range(4):
                                gt = 4 * c + tt
                                W(y, "vXN", gt - 1)
                                W(y, "sSQ", gt - 1)
                                y.dma_start(xa[gt % 2][:, :], x_kv[gt * 128:(gt + 1) * 128, :]).then_inc(sems[f"xaL{gt % 2}"], 16)
                                cnt.add(f"xaL{gt % 2}", 16)
                        if c > 0:
                            cp = c - 1
                            for gg in range(8):
                                kidx = cp * 8 + gg
                                W(y, "sKE", kidx + 1)
                                W(y, "vKR", kidx - 7)
                                y.dma_start(kswp[kidx % 8][0:64, :], kqf[kidx % 8][64:128, :]).then_inc(sems[f"swL{kidx % 8}"], 16)
                                y.dma_start(kswp[kidx % 8][64:128, :], kqf[kidx % 8][0:64, :]).then_inc(sems[f"swL{kidx % 8}"], 16)
                                cnt.add(f"swL{kidx % 8}", 32)
                                W(y, "vKR", kidx + 1)
                                y.dma_start(ktr_scr[gg, :, cp * 512:(cp + 1) * 512], ktr_t[kidx % 2][:, :]).then_inc(sems[f"kspL{kidx % 2}"], 16)
                                cnt.add(f"kspL{kidx % 2}", 16)
                            for h2 in range(2):
                                for tp in range(2):
                                    half = cp * 2 + h2
                                    vidx = 2 * half + tp
                                    slot = (half % 2) * 2 + tp
                                    W(y, "vVE", vidx + 1)
                                    ttg = cp * 4 + 2 * h2 + tp
                                    y.dma_start(v_scr[ttg], v_bf[slot][:, :]).then_inc(sems[f"vspL{slot}"], 16)
                                    cnt.add(f"vspL{slot}", 16)
                    for nm in ["kspL0", "kspL1", "vspL0", "vspL1", "vspL2", "vspL3"]:
                        y.wait_ge(sems[nm], cnt[nm])

                @blk.vector
                def _(v):
                    v.wait_ge(sems["yP0"], 32)
                    for c in range(9):
                        if c < 8:
                            for tt in range(4):
                                gt = 4 * c + tt
                                par = gt % 2
                                W(v, f"xaL{par}", 16 * (gt // 2 + 1))
                                W(v, "sSD", 2 * gt - 3)
                                v.reduce_sum(t_s[:, par:par + 1], xa[par][:, :], axis=AX.X)
                                W(v, "sSQ", gt + 1)
                                v.reduce_sum(t_q[:, par:par + 1], xsq[par][:, :], axis=AX.X).then_inc(sems["vVAR"], 1)
                                cnt.add("vVAR", 1)
                                W(v, "sSD", gt * 2 + 1)
                                v.tensor_scalar(t_var[:, par:par + 1], t_v1[:, par:par + 1],
                                                t_msq[:, par:par + 1], None, ALU.subtract).then_inc(sems["vVAR"], 1)
                                cnt.add("vVAR", 1)
                                W(v, "sSD", gt * 2 + 2)
                                v.reciprocal(t_r[:, par:par + 1], t_sd[:, par:par + 1])
                                v.tensor_scalar(xsq[par][:, :], xa[par][:, :], t_mean[:, par:par + 1],
                                                None, ALU.subtract)
                                W(v, "tTR", 4 * (gt - 1))
                                v.tensor_scalar(xn[par][:, :], xsq[par][:, :], t_r[:, par:par + 1],
                                                None, ALU.mult).then_inc(sems["vXN"], 1)
                                cnt.add("vXN", 1)
                        if c > 0:
                            cp = c - 1
                            for gg in range(8):
                                kidx = cp * 8 + gg
                                W(v, "sKE", kidx + 1)
                                W(v, f"tbL{cp % 2}", 32 * (cp // 2 + 1))
                                v.tensor_mul(ropet[:, :], kqf[kidx % 8][:, :], ck[cp % 2][:, :])
                                W(v, f"swL{kidx % 8}", 32 * (kidx // 8 + 1))
                                v.tensor_mul(ropet2[:, :], kswp[kidx % 8][:, :], sk[cp % 2][:, :])
                                W(v, f"kspL{kidx % 2}", 16 * (kidx // 2))
                                v.tensor_add(ktr_t[kidx % 2][:, :], ropet[:, :], ropet2[:, :]).then_inc(sems["vKR"], 1)
                                cnt.add("vKR", 1)

                @blk.scalar
                def _(sc):
                    sc.wait_ge(sems["yP0"], 32)
                    for c in range(9):
                        if c < 8:
                            for tt in range(4):
                                gt = 4 * c + tt
                                par = gt % 2
                                W(sc, f"xaL{par}", 16 * (gt // 2 + 1))
                                W(sc, "vVAR", 2 * gt - 1)
                                W(sc, "vXN", gt - 1)
                                sc.activation(xsq[par][:, :], xa[par][:, :], AF.Square).then_inc(sems["sSQ"], 1)
                                cnt.add("sSQ", 1)
                                W(sc, "vVAR", 2 * gt + 1)
                                W(sc, "vXN", gt - 1)
                                sc.activation(t_mean[:, par:par + 1], t_s[:, par:par + 1], AF.Copy, scale=1.0 / D)
                                sc.activation(t_msq[:, par:par + 1], t_s[:, par:par + 1], AF.Square, scale=1.0 / D)
                                sc.activation(t_v1[:, par:par + 1], t_q[:, par:par + 1], AF.Copy, scale=1.0 / D).then_inc(sems["sSD"], 1)
                                cnt.add("sSD", 1)
                                W(sc, "vVAR", 2 * gt + 2)
                                sc.activation(t_sd[:, par:par + 1], t_var[:, par:par + 1], AF.Sqrt, bias=c_eps).then_inc(sems["sSD"], 1)
                                cnt.add("sSD", 1)
                                for q in range(4):
                                    idx = 4 * gt + q
                                    W(sc, "tTR", idx + 1)
                                    W(sc, "tVg", 16 * c)
                                    dst = xnt.ap().rearrange("p (d t) -> p d t", t=512)[:, 8 * q:8 * q + 8, tt * 128:(tt + 1) * 128]
                                    src = pst.ap().rearrange("p (d t) -> p d t", t=128)[:, 8 * (idx % 2):8 * (idx % 2) + 8, :]
                                    sc.copy(dst, src).then_inc(sems["sTE"], 1)
                                    cnt.add("sTE", 1)
                            for gg in range(8):
                                kidx = c * 8 + gg
                                W(sc, "tK", kidx + 1)
                                W(sc, "vKR", kidx - 7)
                                W(sc, f"swL{kidx % 8}", 32 * (kidx // 8))
                                sc.copy(kqf[kidx % 8][:, :], bank(gg % 2)).then_inc(sems["sKE"], 1)
                                cnt.add("sKE", 1)
                            for h2 in range(2):
                                for tp in range(2):
                                    half = c * 2 + h2
                                    slot = (half % 2) * 2 + tp
                                    W(sc, "tVg", 8 * (half + 1))
                                    W(sc, f"vspL{slot}", 16 * (half // 2))
                                    sc.copy(v_bf[slot][:, :], bank2(2 + 2 * tp)).then_inc(sems["vVE"], 1)
                                    cnt.add("vVE", 1)

                @blk.tensor
                def _(t):
                    t.wait_ge(sems["gP0"], 48)
                    for c in range(8):
                        for tt in range(4):
                            gt = 4 * c + tt
                            W(t, "vXN", gt + 1)
                            for q in range(4):
                                idx = 4 * gt + q
                                W(t, "sTE", idx - 1)
                                for k in range(8):
                                    dt = 8 * q + k
                                    ins = t.transpose(pst[:, (idx % 2) * 1024 + k * 128:(idx % 2) * 1024 + (k + 1) * 128],
                                                      xn[gt % 2][:, dt * 128:(dt + 1) * 128],
                                                      ident_sb[:, :])
                                ins.then_inc(sems["tTR"], 1)
                                cnt.add("tTR", 1)
                        W(t, "sTE", 16 * (c + 1))
                        for gg in range(8):
                            kidx = c * 8 + gg
                            W(t, f"wkL{kidx % 2}", 16 * (kidx // 2 + 1))
                            W(t, "sKE", kidx - 1)
                            for dt in range(32):
                                ins = t.matmul(bank(gg % 2), wkS[kidx % 2][:, dt * 128:(dt + 1) * 128],
                                               xnt[:, dt * 512:(dt + 1) * 512],
                                               start=(dt == 0), stop=(dt == 31))
                            ins.then_inc(sems["tK"], 1)
                            cnt.add("tK", 1)
                        for h2 in range(2):
                            half = c * 2 + h2
                            W(t, "vVE", 2 * half)
                            for dt in range(32):
                                vl = wv_lidx(c, h2, dt // 4)
                                if dt % 4 == 0:
                                    W(t, f"wvL{vl % 2}", 16 * (vl // 2 + 1))
                                for tp in range(2):
                                    ttl = 2 * h2 + tp
                                    for colh in range(2):
                                        ins = t.matmul(bank(2 + 2 * tp + colh),
                                                       xnt[:, dt * 512 + ttl * 128:dt * 512 + (ttl + 1) * 128],
                                                       wvS[vl % 2][:, (dt % 4) * 1024 + colh * 512:(dt % 4) * 1024 + (colh + 1) * 512],
                                                       start=(dt == 0), stop=(dt == 31))
                                if dt % 4 == 3:
                                    ins.then_inc(sems["tVg"], 1)
                                    cnt.add("tVg", 1)

        with contextlib.ExitStack() as ST:
            EA = ST.enter_context
            xa = [EA(nc.sbuf_tensor(f"x2a{i}", [128, D], F32)) for i in range(2)]
            xsq = [EA(nc.sbuf_tensor(f"x2sq{i}", [128, D], F32)) for i in range(2)]
            xn = [EA(nc.sbuf_tensor(f"x2n{i}", [128, D], BF16)) for i in range(2)]
            xnt = EA(nc.sbuf_tensor("xnt_own", [128, 32 * 1024], BF16))
            t_s = EA(nc.sbuf_tensor("t2_s", [128, 2], F32))
            t_q = EA(nc.sbuf_tensor("t2_q", [128, 2], F32))
            t_mean = EA(nc.sbuf_tensor("t2_mean", [128, 2], F32))
            t_msq = EA(nc.sbuf_tensor("t2_msq", [128, 2], F32))
            t_v1 = EA(nc.sbuf_tensor("t2_v1", [128, 2], F32))
            t_var = EA(nc.sbuf_tensor("t2_var", [128, 2], F32))
            t_sd = EA(nc.sbuf_tensor("t2_sd", [128, 2], F32))
            t_r = EA(nc.sbuf_tensor("t2_r", [128, 2], F32))
            cq = EA(nc.sbuf_tensor("cq", [128, TOWN], F32))
            sq = EA(nc.sbuf_tensor("sq", [128, TOWN], F32))
            wqS = [EA(nc.sbuf_tensor(f"wqS{i}", [128, D], BF16)) for i in range(2)]
            qf = [EA(nc.sbuf_tensor(f"qf{i}", [128, TOWN], F32)) for i in range(2)]
            qswp = [EA(nc.sbuf_tensor(f"qswp{i}", [128, TOWN], F32)) for i in range(2)]
            ropet = EA(nc.sbuf_tensor("ropetq", [128, TOWN], F32))
            ropet2 = EA(nc.sbuf_tensor("ropetq2", [128, TOWN], F32))
            qtr_t = [EA(nc.sbuf_tensor(f"qtr_t{i}", [128, TOWN], BF16)) for i in range(2)]

            base = {k: cnt[k] for k in ["sSQ", "vVAR", "sSD", "vXN", "tTR", "sTE"]}

            with nc.Block("pA2") as blk:
                @blk.gpsimd
                def _(g):
                    for h in range(NH):
                        W(g, "tQ", h - 1)
                        g.dma_start(wqS[h % 2][:, :], wq_lin[h]).then_inc(sems[f"wqL{h % 2}"], 16)
                        cnt.add(f"wqL{h % 2}", 16)

                @blk.sync
                def _(y):
                    y.dma_start(cq[:, :], cosq_in).then_inc(sems["qtb"], 16)
                    y.dma_start(sq[:, :], sinq_in).then_inc(sems["qtb"], 16)
                    cnt.add("qtb", 32)
                    for gt in range(8):
                        W(y, "vXN", base["vXN"] + gt - 1)
                        W(y, "sSQ", base["sSQ"] + gt - 1)
                        y.dma_start(xa[gt % 2][:, :], x_own[gt * 128:(gt + 1) * 128, :]).then_inc(sems[f"x2L{gt % 2}"], 16)
                        cnt.add(f"x2L{gt % 2}", 16)
                    for h in range(NH):
                        W(y, "sQE", h + 1)
                        y.dma_start(qswp[h % 2][0:64, :], qf[h % 2][64:128, :]).then_inc(sems[f"qswL{h % 2}"], 16)
                        y.dma_start(qswp[h % 2][64:128, :], qf[h % 2][0:64, :]).then_inc(sems[f"qswL{h % 2}"], 16)
                        cnt.add(f"qswL{h % 2}", 32)
                        W(y, "vQR", h + 1)
                        y.dma_start(qtr_scr[h], qtr_t[h % 2][:, :]).then_inc(sems[f"qspL{h % 2}"], 16)
                        cnt.add(f"qspL{h % 2}", 16)
                    W(y, "sTE", base["sTE"] + 32)
                    y.dma_start(xnt_scr.transpose([1, 0, 2]),
                                xnt.ap().rearrange("p (d t) -> p d t", t=1024)).then_inc(sems["xsp"], 16)
                    cnt.add("xsp", 16)
                    y.wait_ge(sems["qspL0"], cnt["qspL0"])
                    y.wait_ge(sems["qspL1"], cnt["qspL1"])
                    y.wait_ge(sems["xsp"], cnt["xsp"])

                @blk.vector
                def _(v):
                    for gt in range(8):
                        par = gt % 2
                        v.wait_ge(sems[f"x2L{par}"], 16 * (gt // 2 + 1))
                        W(v, "sSD", base["sSD"] + 2 * gt - 3)
                        v.reduce_sum(t_s[:, par:par + 1], xa[par][:, :], axis=AX.X)
                        W(v, "sSQ", base["sSQ"] + gt + 1)
                        v.reduce_sum(t_q[:, par:par + 1], xsq[par][:, :], axis=AX.X).then_inc(sems["vVAR"], 1)
                        cnt.add("vVAR", 1)
                        W(v, "sSD", base["sSD"] + gt * 2 + 1)
                        v.tensor_scalar(t_var[:, par:par + 1], t_v1[:, par:par + 1],
                                        t_msq[:, par:par + 1], None, ALU.subtract).then_inc(sems["vVAR"], 1)
                        cnt.add("vVAR", 1)
                        W(v, "sSD", base["sSD"] + gt * 2 + 2)
                        v.reciprocal(t_r[:, par:par + 1], t_sd[:, par:par + 1])
                        v.tensor_scalar(xsq[par][:, :], xa[par][:, :], t_mean[:, par:par + 1],
                                        None, ALU.subtract)
                        W(v, "tTR", base["tTR"] + 4 * (gt - 1))
                        v.tensor_scalar(xn[par][:, :], xsq[par][:, :], t_r[:, par:par + 1],
                                        None, ALU.mult).then_inc(sems["vXN"], 1)
                        cnt.add("vXN", 1)
                    for h in range(NH):
                        W(v, "sQE", h + 1)
                        W(v, "qtb", 32)
                        v.tensor_mul(ropet[:, :], qf[h % 2][:, :], cq[:, :])
                        W(v, f"qswL{h % 2}", 32 * (h // 2 + 1))
                        v.tensor_mul(ropet2[:, :], qswp[h % 2][:, :], sq[:, :])
                        W(v, f"qspL{h % 2}", 16 * (h // 2))
                        v.tensor_add(qtr_t[h % 2][:, :], ropet[:, :], ropet2[:, :]).then_inc(sems["vQR"], 1)
                        cnt.add("vQR", 1)

                @blk.scalar
                def _(sc):
                    for gt in range(8):
                        par = gt % 2
                        sc.wait_ge(sems[f"x2L{par}"], 16 * (gt // 2 + 1))
                        W(sc, "vVAR", base["vVAR"] + 2 * gt - 1)
                        W(sc, "vXN", base["vXN"] + gt - 1)
                        sc.activation(xsq[par][:, :], xa[par][:, :], AF.Square).then_inc(sems["sSQ"], 1)
                        cnt.add("sSQ", 1)
                        W(sc, "vVAR", base["vVAR"] + 2 * gt + 1)
                        W(sc, "vXN", base["vXN"] + gt - 1)
                        sc.activation(t_mean[:, par:par + 1], t_s[:, par:par + 1], AF.Copy, scale=1.0 / D)
                        sc.activation(t_msq[:, par:par + 1], t_s[:, par:par + 1], AF.Square, scale=1.0 / D)
                        sc.activation(t_v1[:, par:par + 1], t_q[:, par:par + 1], AF.Copy, scale=1.0 / D).then_inc(sems["sSD"], 1)
                        cnt.add("sSD", 1)
                        W(sc, "vVAR", base["vVAR"] + 2 * gt + 2)
                        sc.activation(t_sd[:, par:par + 1], t_var[:, par:par + 1], AF.Sqrt, bias=c_eps).then_inc(sems["sSD"], 1)
                        cnt.add("sSD", 1)
                        for q in range(4):
                            idx = 4 * gt + q
                            W(sc, "tTR", base["tTR"] + idx + 1)
                            dst = xnt.ap().rearrange("p (d t) -> p d t", t=1024)[:, 8 * q:8 * q + 8, gt * 128:(gt + 1) * 128]
                            src = pst.ap().rearrange("p (d t) -> p d t", t=128)[:, 8 * (idx % 2):8 * (idx % 2) + 8, :]
                            sc.copy(dst, src).then_inc(sems["sTE"], 1)
                            cnt.add("sTE", 1)
                    for h in range(NH):
                        W(sc, "tQ", h + 1)
                        W(sc, "vQR", h - 1)
                        W(sc, f"qswL{h % 2}", 32 * (h // 2))
                        sc.copy(qf[h % 2][:, :], bank2((h % 2) * 2)).then_inc(sems["sQE"], 1)
                        cnt.add("sQE", 1)

                @blk.tensor
                def _(t):
                    for gt in range(8):
                        W(t, "vXN", base["vXN"] + gt + 1)
                        for q in range(4):
                            idx = 4 * gt + q
                            W(t, "sTE", base["sTE"] + idx - 1)
                            for k in range(8):
                                dt = 8 * q + k
                                ins = t.transpose(pst[:, (idx % 2) * 1024 + k * 128:(idx % 2) * 1024 + (k + 1) * 128],
                                                  xn[gt % 2][:, dt * 128:(dt + 1) * 128],
                                                  ident_sb[:, :])
                            ins.then_inc(sems["tTR"], 1)
                            cnt.add("tTR", 1)
                    W(t, "sTE", base["sTE"] + 32)
                    for h in range(NH):
                        W(t, f"wqL{h % 2}", 16 * (h // 2 + 1))
                        W(t, "sQE", h - 1)
                        for dt in range(32):
                            for colh in range(2):
                                ins = t.matmul(bank((h % 2) * 2 + colh),
                                               wqS[h % 2][:, dt * 128:(dt + 1) * 128],
                                               xnt[:, dt * 1024 + colh * 512:dt * 1024 + (colh + 1) * 512],
                                               start=(dt == 0), stop=(dt == 31))
                        ins.then_inc(sems["tQ"], 1)
                        cnt.add("tQ", 1)

        if upto == "A":
            return nc

        with contextlib.ExitStack() as STB:
            EB = STB.enter_context
            attn = EB(nc.sbuf_tensor("attn", [128, NH * TOWN], BF16))
            with contextlib.ExitStack() as ST:
                EA = ST.enter_context
                ktr_g = [EA(nc.sbuf_tensor(f"ktr_g{i}", [128, S], BF16)) for i in range(2)]
                v_g = [EA(nc.sbuf_tensor(f"v_g{i}", [128, S], BF16)) for i in range(2)]
                qtr = [EA(nc.sbuf_tensor(f"qtr{i}", [128, TOWN], BF16)) for i in range(2)]
                ptb = [EA(nc.sbuf_tensor(f"ptb{i}", [128, 512], BF16)) for i in range(2)]
                lnt = EA(nc.sbuf_tensor("lnt", [1, 512], F32))
                rcp = EA(nc.sbuf_tensor("rcp", [1, 512], F32))
                rb = EA(nc.sbuf_tensor("rb", [128, 512], F32))

                PASS = []
                for h in range(NH):
                    PASS.append((h, 0, 32))
                    PASS.append((h, 1, 16))
                e_of_pass = [0]
                for (_, _, nj) in PASS:
                    e_of_pass.append(e_of_pass[-1] + nj)

                def o_bank(pl):
                    return 2 if pl == 0 else 4

                def s_bank(pl):
                    return 3 if pl == 0 else 5

                def wof(pl, j):
                    return wA_of(j) if pl == 0 else wB_of(j)

                def masked(pl, j):
                    return (j >= 16) if pl == 0 else True

                def mask_count_upto(p, j):
                    n = 16 * p
                    h, pl, nj = PASS[p]
                    for jj in range(j + 1):
                        if masked(pl, jj):
                            n += 1
                    return n

                bSE = cnt["sE"]; bTST = cnt["tST"]; bTPV = cnt["tPV"]; bVMK = cnt["vMK"]
                bSRC = cnt["sRC"]; bTBC = cnt["tBC"]; bSRB = cnt["sRB"]; bVEV = cnt["vEV"]

                with nc.Block("pB") as blk:
                    @blk.sync
                    def _(y):
                        for gg in range(NG):
                            if gg >= 2:
                                W(y, "tPV", bTPV + 192 * (gg - 1))
                            y.dma_start(ktr_g[gg % 2][:, :], ktr_scr[gg]).then_inc(sems[f"kgl{gg % 2}"], 16)
                            cnt.add(f"kgl{gg % 2}", 16)
                            y.dma_start(v_g[gg % 2].ap().rearrange("p (j q) -> p j q", q=128),
                                        v_scr[:, :, gg * 128:(gg + 1) * 128].transpose([1, 0, 2])).then_inc(sems[f"vgl{gg % 2}"], 16)
                            cnt.add(f"vgl{gg % 2}", 16)
                            for hq in range(4):
                                h = gg * 4 + hq
                                if h >= 2:
                                    W(y, "tST", bTST + e_of_pass[2 * (h - 1)])
                                y.dma_start(qtr[h % 2][:, :], qtr_scr[h]).then_inc(sems[f"qgl{h % 2}"], 16)
                                cnt.add(f"qgl{h % 2}", 16)

                    @blk.scalar
                    def _(sc):
                        for p, (h, pl, nj) in enumerate(PASS):
                            for j in range(nj):
                                e = e_of_pass[p] + j
                                w = wof(pl, j)
                                W(sc, "tST", bTST + e + 1)
                                W(sc, "tPV", bTPV + e - 1)
                                sc.activation(ptb[e % 2][:, 0:w], bank(e % 2, w), AF.Exp,
                                              bias=c_shift, scale=SCALE).then_inc(sems["sE"], 1)
                                cnt.add("sE", 1)
                            W(sc, "tPV", bTPV + e_of_pass[p + 1])
                            sc.activation(lnt[:, :], bank(s_bank(pl), 512, 0, 1), AF.Ln)
                            sc.activation(rcp[:, :], lnt[:, :], AF.Exp, scale=-1.0).then_inc(sems["sRC"], 1)
                            cnt.add("sRC", 1)
                            W(sc, "tBC", bTBC + p + 1)
                            W(sc, "vEV", bVEV + p)
                            sc.copy(rb[:, :], bank(s_bank(pl))).then_inc(sems["sRB"], 1)
                            cnt.add("sRB", 1)

                    @blk.vector
                    def _(v):
                        v.wait_ge(sems["gP0"], 48)
                        for p, (h, pl, nj) in enumerate(PASS):
                            for j in range(nj):
                                if not masked(pl, j):
                                    continue
                                e = e_of_pass[p] + j
                                w = wof(pl, j)
                                W(v, "sE", bSE + e + 1)
                                v.tensor_mul(ptb[e % 2][:, w - 128:w], ptb[e % 2][:, w - 128:w],
                                             masks_sb[:, j * 128:(j + 1) * 128]).then_inc(sems["vMK"], 1)
                                cnt.add("vMK", 1)
                            W(v, "sRB", bSRB + p + 1)
                            v.tensor_mul(attn[:, h * 1024 + pl * 512:h * 1024 + (pl + 1) * 512],
                                         bank2(o_bank(pl), 512), rb[:, :]).then_inc(sems["vEV"], 1)
                            cnt.add("vEV", 1)

                    @blk.tensor
                    def _(t):
                        t.wait_ge(sems["gP0"], 48)

                        def emit_ST(p, j):
                            h, pl, nj = PASS[p]
                            gg = h // 4
                            e = e_of_pass[p] + j
                            w = wof(pl, j)
                            W(t, "sE", bSE + e - 1)
                            if j == 0:
                                W(t, f"kgl{gg % 2}", 16 * (gg // 2 + 1))
                                W(t, f"qgl{h % 2}", 16 * (h // 2 + 1))
                            t.matmul(bank(e % 2, w), ktr_g[gg % 2][:, j * 128:(j + 1) * 128],
                                     qtr[h % 2][:, pl * 512:pl * 512 + w],
                                     start=True, stop=True).then_inc(sems["tST"], 1)
                            cnt.add("tST", 1)

                        def emit_PV(p, j):
                            h, pl, nj = PASS[p]
                            gg = h // 4
                            e = e_of_pass[p] + j
                            w = wof(pl, j)
                            W(t, "sE", bSE + e + 1)
                            if masked(pl, j):
                                W(t, "vMK", bVMK + mask_count_upto(p, j))
                            if j == 0:
                                W(t, "vEV", bVEV + p - 1)
                                W(t, "sRB", bSRB + p - 1)
                                W(t, f"vgl{gg % 2}", 16 * (gg // 2 + 1))
                            t.matmul(bank2(o_bank(pl), w), v_g[gg % 2][:, j * 128:(j + 1) * 128],
                                     ptb[e % 2][:, 0:w], start=(j == 0), stop=(j == nj - 1))
                            t.matmul(bank(s_bank(pl), w, 0, 1), onesc_sb[:, :],
                                     ptb[e % 2][:, 0:w],
                                     start=(j == 0), stop=(j == nj - 1)).then_inc(sems["tPV"], 1)
                            cnt.add("tPV", 1)

                        def emit_BC(p):
                            h, pl, nj = PASS[p]
                            W(t, "sRC", bSRC + p + 1)
                            t.matmul(bank(s_bank(pl)), onesr_sb[:, :], rcp[:, :],
                                     start=True, stop=True).then_inc(sems["tBC"], 1)
                            cnt.add("tBC", 1)

                        pend_bc = None
                        for p in range(len(PASS)):
                            h, pl, nj = PASS[p]
                            emit_ST(p, 0)
                            if pend_bc is not None:
                                emit_BC(pend_bc)
                                pend_bc = None
                            emit_ST(p, 1)
                            for j in range(2, nj):
                                emit_PV(p, j - 2)
                                emit_ST(p, j)
                            emit_PV(p, nj - 2)
                            emit_PV(p, nj - 1)
                            pend_bc = p
                        emit_BC(pend_bc)

                    if dump:
                        @blk.gpsimd
                        def _(g):
                            g.wait_ge(sems["vEV"], bVEV + 64)
                            g.dma_start(attn_dump, attn[:, :]).then_inc(sems["gAL0"], 16)
                            cnt.add("gAL0", 16)
                            g.wait_ge(sems["gAL0"], cnt["gAL0"])

            if upto == "B":
                return nc

            with contextlib.ExitStack() as ST:
                EA = ST.enter_context
                woS = [EA(nc.sbuf_tensor(f"woS{i}", [128, 32 * 512], BF16)) for i in range(2)]
                xo = [EA(nc.sbuf_tensor(f"xo{i}", [128, 512], F32)) for i in range(4)]
                yf = [EA(nc.sbuf_tensor(f"yf{i}", [128, 512], F32)) for i in range(2)]
                bTY = cnt["tY"]; bVY = cnt["vY"]

                with nc.Block("pC1") as blk:
                    @blk.gpsimd
                    def _(g):
                        for dc in range(8):
                            W(g, "tY", bTY + 8 * (dc - 1))
                            g.dma_start(woS[dc % 2].ap().rearrange("p (a q) -> p a q", q=512),
                                        wo_lin[dc].transpose([1, 0, 2])).then_inc(sems[f"woL{dc % 2}"], 16)
                            cnt.add(f"woL{dc % 2}", 16)

                    @blk.sync
                    def _(y):
                        for dc in range(8):
                            for tt in range(8):
                                qi = dc * 8 + tt
                                W(y, "vY", bVY + qi - 3)
                                y.dma_start(xo[qi % 4][:, :], x_own[tt * 128:(tt + 1) * 128, dc * 512:(dc + 1) * 512]).then_inc(sems[f"xoL{qi % 4}"], 16)
                                cnt.add(f"xoL{qi % 4}", 16)
                                W(y, "vY", bVY + qi + 1)
                                y.dma_start(out[tt * 128:(tt + 1) * 128, dc * 512:(dc + 1) * 512], yf[qi % 2][:, :]).then_inc(sems[f"yYL{qi % 2}"], 16)
                                cnt.add(f"yYL{qi % 2}", 16)
                        y.wait_ge(sems["yYL0"], cnt["yYL0"])
                        y.wait_ge(sems["yYL1"], cnt["yYL1"])

                    @blk.tensor
                    def _(t):
                        for dc in range(8):
                            W(t, f"woL{dc % 2}", 16 * (dc // 2 + 1))
                            for tt in range(8):
                                qi = dc * 8 + tt
                                W(t, "vY", bVY + qi - 1)
                                for at in range(32):
                                    ins = t.matmul(bank(qi % 2), attn[:, at * 1024 + tt * 128:at * 1024 + (tt + 1) * 128],
                                                   woS[dc % 2][:, at * 512:(at + 1) * 512],
                                                   start=(at == 0), stop=(at == 31))
                                ins.then_inc(sems["tY"], 1)
                                cnt.add("tY", 1)

                    @blk.vector
                    def _(v):
                        for dc in range(8):
                            for tt in range(8):
                                qi = dc * 8 + tt
                                W(v, "tY", bTY + qi + 1)
                                v.wait_ge(sems[f"xoL{qi % 4}"], 16 * (qi // 4 + 1))
                                W(v, f"yYL{qi % 2}", 16 * (qi // 2))
                                v.tensor_add(yf[qi % 2][:, :], bank(qi % 2), xo[qi % 4][:, :]).then_inc(sems["vY"], 1)
                                cnt.add("vY", 1)

        with contextlib.ExitStack() as ST:
            EA = ST.enter_context
            xnt = EA(nc.sbuf_tensor("xnt_c2", [128, 32 * 1024], BF16))
            inter = EA(nc.sbuf_tensor("inter", [128, 16 * 1024], BF16))
            wgS = [EA(nc.sbuf_tensor(f"wgS{i}", [128, D], BF16)) for i in range(2)]
            wuS = [EA(nc.sbuf_tensor(f"wuS{i}", [128, D], BF16)) for i in range(2)]
            wdS = [EA(nc.sbuf_tensor(f"wdS{i}", [128, 16 * 512], BF16)) for i in range(2)]
            gsil = [EA(nc.sbuf_tensor(f"gsil{i}", [128, 1024], BF16)) for i in range(2)]
            ydf = [EA(nc.sbuf_tensor(f"ydf{i}", [128, 512], F32)) for i in range(2)]
            bYY0 = cnt["yYL0"]; bYY1 = cnt["yYL1"]
            bGA0 = cnt["gAL0"]; bGA1 = cnt["gAL1"]
            bTG = cnt["tG"]; bTU = cnt["tU"]; bSIL = cnt["sSIL"]; bVIN = cnt["vIN"]
            bTD = cnt["tD"]; bVDE = cnt["vDE"]

            it0_of = [0]
            for nb in FFN_BLOCKS:
                it0_of.append(it0_of[-1] + nb)
            NB = len(FFN_BLOCKS)

            with nc.Block("pC2") as blk:
                def wgu_loads(g, b, lo=0, hi=None):
                    for li in range(lo, FFN_BLOCKS[b] if hi is None else hi):
                        gi = it0_of[b] + li
                        W(g, "sSIL", bSIL + gi - 1)
                        g.dma_start(wgS[gi % 2][:, :], wg_lin[gi]).then_inc(sems[f"wgL{gi % 2}"], 16)
                        cnt.add(f"wgL{gi % 2}", 16)
                        W(g, "vIN", bVIN + gi - 1)
                        g.dma_start(wuS[gi % 2][:, :], wu_lin[gi]).then_inc(sems[f"wuL{gi % 2}"], 16)
                        cnt.add(f"wuL{gi % 2}", 16)

                def emit_accums(g, b, dcg):
                    if b == 0 and dcg == 0:
                        g.wait_ge(sems["yYL0"], bYY0)
                        g.wait_ge(sems["yYL1"], bYY1)
                    for k in range(8):
                        r = dcg * 8 + k
                        dc, tt = divmod(r, 8)
                        W(g, "vDE", bVDE + 64 * b + r + 1)
                        g.dma_start(out[tt * 128:(tt + 1) * 128, dc * 512:(dc + 1) * 512],
                                    ydf[r % 2][:, :], accum_op=ALU.add).then_inc(sems[f"gAL{r % 2}"], 16)
                        cnt.add(f"gAL{r % 2}", 16)

                @blk.gpsimd
                def _(g):
                    wgu_loads(g, 0)
                    for b, nb in enumerate(FFN_BLOCKS):
                        for dc in range(8):
                            if dc >= 2:
                                emit_accums(g, b, dc - 2)
                            W(g, "tD", bTD + 64 * b + 8 * (dc - 1))
                            g.dma_start(wdS[dc % 2].ap().rearrange("p (i q) -> p i q", q=512)[:, 0:nb, :],
                                        wd_lin[it0_of[b]:it0_of[b] + nb, :, dc * 512:(dc + 1) * 512].transpose([1, 0, 2])).then_inc(sems[f"wdL{dc % 2}"], 16)
                            cnt.add(f"wdL{dc % 2}", 16)
                        if b + 1 < NB:
                            wgu_loads(g, b + 1, 0, 2)
                        emit_accums(g, b, 6)
                        emit_accums(g, b, 7)
                        if b + 1 < NB:
                            wgu_loads(g, b + 1, 2)
                    g.wait_ge(sems["gAL0"], cnt["gAL0"])
                    g.wait_ge(sems["gAL1"], cnt["gAL1"])

                @blk.sync
                def _(y):
                    y.dma_start(xnt.ap().rearrange("p (d t) -> p d t", t=1024),
                                xnt_scr.transpose([1, 0, 2])).then_inc(sems["xrl"], 16)
                    cnt.add("xrl", 16)
                    y.wait_ge(sems["gAL0"], cnt["gAL0"])
                    y.wait_ge(sems["gAL1"], cnt["gAL1"])

                @blk.tensor
                def _(t):
                    W(t, "xrl", 16)
                    for b, nb in enumerate(FFN_BLOCKS):
                        for li in range(nb):
                            gi = it0_of[b] + li
                            gbk = (gi % 2) * 2
                            W(t, f"wgL{gi % 2}", 16 * (gi // 2 + 1))
                            W(t, "sSIL", bSIL + gi - 1)
                            if li == 0 and b > 0:
                                W(t, "vDE", bVDE + 64 * b)
                            for dt in range(32):
                                for colh in range(2):
                                    ins = t.matmul(bank(gbk + colh), wgS[gi % 2][:, dt * 128:(dt + 1) * 128],
                                                   xnt[:, dt * 1024 + colh * 512:dt * 1024 + (colh + 1) * 512],
                                                   start=(dt == 0), stop=(dt == 31))
                            ins.then_inc(sems["tG"], 1)
                            cnt.add("tG", 1)
                            W(t, f"wuL{gi % 2}", 16 * (gi // 2 + 1))
                            W(t, "vIN", bVIN + gi)
                            for dt in range(32):
                                for colh in range(2):
                                    ins = t.matmul(bank(4 + colh), wuS[gi % 2][:, dt * 128:(dt + 1) * 128],
                                                   xnt[:, dt * 1024 + colh * 512:dt * 1024 + (colh + 1) * 512],
                                                   start=(dt == 0), stop=(dt == 31))
                            ins.then_inc(sems["tU"], 1)
                            cnt.add("tU", 1)
                        W(t, "vIN", bVIN + it0_of[b] + nb)
                        for r in range(64):
                            dc, tt = divmod(r, 8)
                            if tt == 0:
                                W(t, f"wdL{dc % 2}", 16 * (4 * b + dc // 2 + 1))
                            W(t, "vDE", bVDE + 64 * b + r - 1)
                            if r == 0:
                                W(t, "sSIL", bSIL + it0_of[b] + nb)
                            for li in range(nb):
                                ins = t.matmul(bank(r % 2), inter[:, li * 1024 + tt * 128:li * 1024 + (tt + 1) * 128],
                                               wdS[dc % 2][:, li * 512:(li + 1) * 512],
                                               start=(li == 0), stop=(li == nb - 1))
                            ins.then_inc(sems["tD"], 1)
                            cnt.add("tD", 1)

                @blk.scalar
                def _(sc):
                    for b, nb in enumerate(FFN_BLOCKS):
                        for li in range(nb):
                            gi = it0_of[b] + li
                            W(sc, "tG", bTG + gi + 1)
                            W(sc, "vIN", bVIN + gi - 1)
                            sc.activation(gsil[gi % 2][:, :], bank2((gi % 2) * 2), AF.Silu).then_inc(sems["sSIL"], 1)
                            cnt.add("sSIL", 1)

                @blk.vector
                def _(v):
                    for b, nb in enumerate(FFN_BLOCKS):
                        for li in range(nb):
                            gi = it0_of[b] + li
                            W(v, "tU", bTU + gi + 1)
                            W(v, "sSIL", bSIL + gi + 1)
                            v.tensor_mul(inter[:, li * 1024:(li + 1) * 1024], bank2(4),
                                         gsil[gi % 2][:, :]).then_inc(sems["vIN"], 1)
                            cnt.add("vIN", 1)
                        for r in range(64):
                            W(v, "tD", bTD + 64 * b + r + 1)
                            W(v, f"gAL{r % 2}", (bGA0 if r % 2 == 0 else bGA1) + 16 * (32 * b + r // 2))
                            v.tensor_copy(ydf[r % 2][:, :], bank(r % 2)).then_inc(sems["vDE"], 1)
                            cnt.add("vDE", 1)

    return nc



def _host_inputs(L0_attn_x, L0_attn_w_qkv, L0_attn_w_o, L0_ffn_w_gate, L0_ffn_w_up, L0_ffn_w_down):
    x = np.asarray(L0_attn_x, dtype=np.float32)
    w_qkv = np.asarray(L0_attn_w_qkv, dtype=np.float32)
    w_o = np.asarray(L0_attn_w_o, dtype=np.float32)
    w_gate = np.asarray(L0_ffn_w_gate, dtype=np.float32)
    w_up = np.asarray(L0_ffn_w_up, dtype=np.float32)
    w_down = np.asarray(L0_ffn_w_down, dtype=np.float32)

    perm = np.concatenate([np.arange(0, 128, 2), np.arange(1, 128, 2)])
    w_q = np.ascontiguousarray(w_qkv[:, :4096].reshape(D, 32, 128)[:, :, perm].reshape(D, 4096))
    w_k = np.ascontiguousarray(w_qkv[:, 4096:5120].reshape(D, 8, 128)[:, :, perm].reshape(D, 1024))
    w_v = np.ascontiguousarray(w_qkv[:, 5120:6144])

    wq_lin = np.ascontiguousarray(w_q.reshape(32, 128, 32, 128).transpose(2, 1, 0, 3).reshape(32, 128, D))
    wk_lin = np.ascontiguousarray(w_k.reshape(32, 128, 8, 128).transpose(2, 1, 0, 3).reshape(8, 128, D))
    wv_lin = np.ascontiguousarray(w_v.reshape(32, 128, 1024))
    wo_lin = np.ascontiguousarray(w_o.reshape(32, 128, 8, 512).transpose(2, 0, 1, 3))
    wg_lin = np.ascontiguousarray(w_gate.reshape(NIT, 128, 32, 128).transpose(0, 3, 2, 1).reshape(NIT, 128, D))
    wu_lin = np.ascontiguousarray(w_up.reshape(NIT, 128, 32, 128).transpose(0, 3, 2, 1).reshape(NIT, 128, D))
    wd_lin = np.ascontiguousarray(np.ascontiguousarray(w_down.T).reshape(NIT, 128, D))

    consts = np.zeros((128, 4), np.float32)
    consts[:, 0] = 1.0 / D
    consts[:, 1] = EPS
    consts[:, 2] = -1.0
    consts[:, 3] = -SHIFT
    ones_row = np.ones((1, 128), np.float32)
    ones_col = np.ones((128, 1), np.float32)
    ident = np.eye(128, dtype=np.float32)

    inv_freq = (1.0 / (10000.0 ** (np.arange(0, HD, 2, dtype=np.float64) / HD)))

    def tables(pos):
        ang = pos[None, :].astype(np.float64) * inv_freq[:, None]
        cos = np.cos(ang).astype(np.float32)
        sin = np.sin(ang).astype(np.float32)
        cos_d = np.concatenate([cos, cos], axis=0)
        sin_d = np.concatenate([-sin, sin], axis=0)
        return np.ascontiguousarray(cos_d), np.ascontiguousarray(sin_d)

    cosk, sink = tables(np.arange(S))

    shared = dict(consts=consts, ones_row=ones_row, ones_col=ones_col, ident_in=ident,
                  cosk_in=cosk, sink_in=sink, wk_lin=wk_lin, wv_lin=wv_lin, wq_lin=wq_lin,
                  wo_lin=wo_lin, wg_lin=wg_lin, wu_lin=wu_lin, wd_lin=wd_lin)

    in_maps = []
    for core in range(8):
        b, c = divmod(core, 4)
        tl = tiles_for_core(c)
        rows = np.concatenate([np.arange(t * 128, (t + 1) * 128) for t in tl])
        x_own = np.ascontiguousarray(x[b][rows])
        cosq, sinq = tables(rows)
        masks = np.zeros((32, 128, 128), np.float32)
        kk = np.arange(128)[:, None]
        qq = np.arange(128)[None, :]
        for j in range(32):
            p = (31 - j) // 4
            t = tl[p]
            masks[j] = (128 * j + kk <= 128 * t + qq).astype(np.float32)
        m = dict(shared)
        m.update(x_kv=np.ascontiguousarray(x[b]), x_own=x_own, cosq_in=cosq,
                 sinq_in=sinq, masks_in=masks)
        in_maps.append(m)
    return in_maps


def kernel(L0_attn_x, L0_attn_w_qkv, L0_attn_w_o, L0_ffn_w_gate, L0_ffn_w_up, L0_ffn_w_down):
    in_maps = _host_inputs(L0_attn_x, L0_attn_w_qkv, L0_attn_w_o,
                           L0_ffn_w_gate, L0_ffn_w_up, L0_ffn_w_down)
    nc = build_nc()
    res = run_bass_kernel_spmd(nc, in_maps, core_ids=list(range(8)))
    y = np.empty((2, S, D), np.float32)
    for core in range(8):
        b, c = divmod(core, 4)
        tl = tiles_for_core(c)
        o = res.results[core]["out"]
        for p, t in enumerate(tl):
            y[b, t * 128:(t + 1) * 128, :] = o[p * 128:(p + 1) * 128, :]
    return y


# version 03
# speedup vs baseline: 1.0119x; 1.0119x over previous
import contextlib
import math

import numpy as np
import concourse.bass as bass
import concourse.mybir as mybir
from concourse.bass_utils import run_bass_kernel_spmd

F32 = mybir.dt.float32
BF16 = mybir.dt.bfloat16
AF = mybir.ActivationFunctionType
ALU = mybir.AluOpType
AX = mybir.AxisListType

D = 4096
S = 4096
NH = 32
NG = 8
HD = 128
NIT = 86
TOWN = 1024
SCALE = 1.0 / math.sqrt(HD)
SHIFT = 20.0
EPS = 1e-5
FFN_BLOCKS = [16, 16, 16, 16, 16, 6]


def tiles_for_core(c):
    out = []
    for p in range(8):
        out.append(((31 - c) - 8 * (p // 2)) if p % 2 == 0 else ((24 + c) - 8 * (p // 2)))
    return out


def wA_of(j):
    return 128 * min(4, -(-(32 - j) // 4))


def wB_of(j):
    return 128 * min(4, -(-(16 - j) // 4))


class Cnt:
    def __init__(self):
        self.v = {}

    def __getitem__(self, k):
        return self.v.get(k, 0)

    def add(self, k, n):
        self.v[k] = self.v.get(k, 0) + n
        return self.v[k]


def build_nc(dump=False, upto="C"):
    nc = bass.Bass("TRN2", target_bir_lowering=False, detect_race_conditions=False)
    scr_kind = "ExternalOutput" if dump else "Internal"

    x_kv = nc.dram_tensor("x_kv", [S, D], F32, kind="ExternalInput").ap()
    x_own = nc.dram_tensor("x_own", [TOWN, D], F32, kind="ExternalInput").ap()
    consts = nc.dram_tensor("consts", [128, 4], F32, kind="ExternalInput").ap()
    ones_row = nc.dram_tensor("ones_row", [1, 128], F32, kind="ExternalInput").ap()
    ones_col = nc.dram_tensor("ones_col", [128, 1], F32, kind="ExternalInput").ap()
    ident_in = nc.dram_tensor("ident_in", [128, 128], F32, kind="ExternalInput").ap()
    masks_in = nc.dram_tensor("masks_in", [32, 128, 128], F32, kind="ExternalInput").ap()
    cosk_in = nc.dram_tensor("cosk_in", [128, S], F32, kind="ExternalInput").ap()
    sink_in = nc.dram_tensor("sink_in", [128, S], F32, kind="ExternalInput").ap()
    cosq_in = nc.dram_tensor("cosq_in", [128, TOWN], F32, kind="ExternalInput").ap()
    sinq_in = nc.dram_tensor("sinq_in", [128, TOWN], F32, kind="ExternalInput").ap()
    wk_lin = nc.dram_tensor("wk_lin", [NG, 128, D], F32, kind="ExternalInput").ap()
    wv_lin = nc.dram_tensor("wv_lin", [32, 128, 1024], F32, kind="ExternalInput").ap()
    wq_lin = nc.dram_tensor("wq_lin", [32, 128, D], F32, kind="ExternalInput").ap()
    wo_lin = nc.dram_tensor("wo_lin", [8, 32, 128, 512], F32, kind="ExternalInput").ap()
    wg_lin = nc.dram_tensor("wg_lin", [NIT, 128, D], F32, kind="ExternalInput").ap()
    wu_lin = nc.dram_tensor("wu_lin", [NIT, 128, D], F32, kind="ExternalInput").ap()
    wd_lin = nc.dram_tensor("wd_lin", [NIT, 128, D], F32, kind="ExternalInput").ap()

    out = nc.dram_tensor("out", [TOWN, D], F32, kind="ExternalOutput").ap()

    wkb = nc.dram_tensor("wkb", [NG, 128, D], BF16, kind="Internal").ap()
    wvb = nc.dram_tensor("wvb", [32, 128, 1024], BF16, kind="Internal").ap()
    ktr_scr = nc.dram_tensor("ktr_scr", [NG, 128, S], BF16, kind=scr_kind).ap()
    v_scr = nc.dram_tensor("v_scr", [32, 128, 1024], BF16, kind=scr_kind).ap()
    qtr_scr = nc.dram_tensor("qtr_scr", [NH, 128, TOWN], BF16, kind=scr_kind).ap()
    xnt_scr = nc.dram_tensor("xnt_scr", [32, 128, TOWN], BF16, kind=scr_kind).ap()
    if dump:
        attn_dump = nc.dram_tensor("attn_dump", [128, NH * TOWN], BF16,
                                   kind="ExternalOutput").ap()

    cnt = Cnt()

    with contextlib.ExitStack() as TOP:
        E = TOP.enter_context
        sems = {}
        semnames = ["yP0", "gP0", "gD2D", "qtb", "xsp", "xrl", "sSQ", "vVAR", "sSD",
                    "vXN", "tTR", "sTE", "tK", "sKE", "vKR", "tVg", "vVE", "tQ",
                    "sQE", "vQR", "tST", "sE", "vMK", "tPV", "sRC", "tBC", "sRB",
                    "vEV", "tY", "vY", "tG", "tU", "sSIL", "vIN", "tD", "vDE"]
        for extra in ["swL2", "swL3", "swL4", "swL5", "swL6", "swL7"]:
            semnames.append(extra)
        for pair in ["xaL", "tbL", "wkL", "wvL", "swL", "kspL", "x2L", "wqL",
                     "qswL", "qspL", "kgl", "vgl", "qgl", "woL", "yYL", "wgL",
                     "wuL", "wdL", "gAL"]:
            semnames += [pair + "0", pair + "1"]
        semnames += ["vspL0", "vspL1", "vspL2", "vspL3"]
        semnames += ["xoL0", "xoL1", "xoL2", "xoL3"]
        for name in semnames:
            sems[name] = E(nc.semaphore(name))

        def W(eng, name, val):
            if val > 0:
                eng.wait_ge(sems[name], val)

        consts_sb = E(nc.sbuf_tensor("consts_sb", [128, 4], F32))
        onesr_sb = E(nc.sbuf_tensor("onesr_sb", [1, 128], F32))
        onesc_sb = E(nc.sbuf_tensor("onesc_sb", [128, 1], BF16))
        ident_sb = E(nc.sbuf_tensor("ident_sb", [128, 128], BF16))
        masks_sb = E(nc.sbuf_tensor("masks_sb", [128, 32 * 128], BF16))
        c_rd = consts_sb[:, 0:1]
        c_eps = consts_sb[:, 1:2]
        c_m1 = consts_sb[:, 2:3]
        c_shift = consts_sb[:, 3:4]

        psf = E(nc.psum_tensor("psf", [128, 3072], F32))
        pst = E(nc.psum_tensor("pst", [128, 2048], BF16))

        def bank(i, w=512, row0=0, rows=128):
            return psf[row0:row0 + rows, i * 512:i * 512 + w]

        def bank2(i, w=1024):
            return psf[:, i * 512:i * 512 + w]

        with nc.Block("p0") as blk:
            @blk.sync
            def _(y):
                y.dma_start(consts_sb[:, :], consts).then_inc(sems["yP0"], 16)
                y.dma_start(onesr_sb[:, :], ones_row).then_inc(sems["yP0"], 16)
                cnt.add("yP0", 32)
                y.wait_ge(sems["yP0"], 32)

            @blk.gpsimd
            def _(g):
                g.dma_start(onesc_sb[:, :], ones_col).then_inc(sems["gP0"], 16)
                g.dma_start(ident_sb[:, :], ident_in).then_inc(sems["gP0"], 16)
                g.dma_start(masks_sb.ap().rearrange("p (j q) -> p j q", q=128),
                            masks_in.transpose([1, 0, 2])).then_inc(sems["gP0"], 16)
                cnt.add("gP0", 48)
                g.dma_start(wkb, wk_lin).then_inc(sems["gD2D"], 16)
                g.dma_start(wvb, wv_lin).then_inc(sems["gD2D"], 16)
                cnt.add("gD2D", 32)
                g.wait_ge(sems["gP0"], 48)

        with contextlib.ExitStack() as ST:
            EA = ST.enter_context
            xa = [EA(nc.sbuf_tensor(f"xa{i}", [128, D], F32)) for i in range(2)]
            xsq = [EA(nc.sbuf_tensor(f"xsq{i}", [128, D], F32)) for i in range(2)]
            xn = [EA(nc.sbuf_tensor(f"xn{i}", [128, D], BF16)) for i in range(2)]
            xnt = EA(nc.sbuf_tensor("xnt", [128, 32 * 512], BF16))
            t_s = EA(nc.sbuf_tensor("t_s", [128, 2], F32))
            t_q = EA(nc.sbuf_tensor("t_q", [128, 2], F32))
            t_mean = EA(nc.sbuf_tensor("t_mean", [128, 2], F32))
            t_msq = EA(nc.sbuf_tensor("t_msq", [128, 2], F32))
            t_v1 = EA(nc.sbuf_tensor("t_v1", [128, 2], F32))
            t_var = EA(nc.sbuf_tensor("t_var", [128, 2], F32))
            t_sd = EA(nc.sbuf_tensor("t_sd", [128, 2], F32))
            t_r = EA(nc.sbuf_tensor("t_r", [128, 2], F32))
            ck = [EA(nc.sbuf_tensor(f"ck{i}", [128, 512], F32)) for i in range(2)]
            sk = [EA(nc.sbuf_tensor(f"sk{i}", [128, 512], F32)) for i in range(2)]
            wkS = [EA(nc.sbuf_tensor(f"wkS{i}", [128, D], BF16)) for i in range(2)]
            wvS = [EA(nc.sbuf_tensor(f"wvS{i}", [128, 4 * 1024], BF16)) for i in range(2)]
            kqf = [EA(nc.sbuf_tensor(f"kqf{i}", [128, 512], F32)) for i in range(8)]
            kswp = [EA(nc.sbuf_tensor(f"kswp{i}", [128, 512], F32)) for i in range(8)]
            ropet = EA(nc.sbuf_tensor("ropet", [128, 512], F32))
            ropet2 = EA(nc.sbuf_tensor("ropet2", [128, 512], F32))
            ktr_t = [EA(nc.sbuf_tensor(f"ktr_t{i}", [128, 512], BF16)) for i in range(2)]
            v_bf = [EA(nc.sbuf_tensor(f"v_bf{i}", [128, 1024], BF16)) for i in range(4)]

            def wk_lidx(c, g):
                return c * 8 + g

            def wv_lidx(c, h2, grp):
                return c * 16 + h2 * 8 + grp

            def gw_A(c, g=None, h2=None, grp=None):
                if g is not None:
                    return c * 24 + g + 1
                return c * 24 + 8 + h2 * 8 + grp + 1

            with nc.Block("pA") as blk:
                @blk.gpsimd
                def _(g):
                    g.wait_ge(sems["gD2D"], 32)
                    for c in range(8):
                        for gg in range(8):
                            kidx = wk_lidx(c, gg)
                            W(g, "tK", kidx - 1)
                            g.dma_start(wkS[kidx % 2][:, :], wkb[gg]).then_inc(sems[f"wkL{kidx % 2}"], 16)
                            cnt.add(f"wkL{kidx % 2}", 16)
                        for h2 in range(2):
                            for grp in range(8):
                                vl = wv_lidx(c, h2, grp)
                                W(g, "tVg", vl - 1)
                                g.dma_start(
                                    wvS[vl % 2].ap().rearrange("p (d q) -> p d q", q=1024),
                                    wvb[4 * grp:4 * grp + 4].transpose([1, 0, 2]),
                                ).then_inc(sems[f"wvL{vl % 2}"], 16)
                                cnt.add(f"wvL{vl % 2}", 16)

                @blk.sync
                def _(y):
                    y.wait_ge(sems["yP0"], 32)
                    for c in range(9):
                        if c < 8:
                            W(y, "vKR", 8 * (c - 1))
                            y.dma_start(ck[c % 2][:, :], cosk_in[:, c * 512:(c + 1) * 512]).then_inc(sems[f"tbL{c % 2}"], 16)
                            y.dma_start(sk[c % 2][:, :], sink_in[:, c * 512:(c + 1) * 512]).then_inc(sems[f"tbL{c % 2}"], 16)
                            cnt.add(f"tbL{c % 2}", 32)
                            for tt in range(4):
                                gt = 4 * c + tt
                                W(y, "vXN", gt - 1)
                                W(y, "sSQ", gt - 1)
                                y.dma_start(xa[gt % 2][:, :], x_kv[gt * 128:(gt + 1) * 128, :]).then_inc(sems[f"xaL{gt % 2}"], 16)
                                cnt.add(f"xaL{gt % 2}", 16)
                        if c > 0:
                            cp = c - 1
                            for gg in range(8):
                                kidx = cp * 8 + gg
                                W(y, "sKE", kidx + 1)
                                W(y, "vKR", kidx - 7)
                                y.dma_start(kswp[kidx % 8][0:64, :], kqf[kidx % 8][64:128, :]).then_inc(sems[f"swL{kidx % 8}"], 16)
                                y.dma_start(kswp[kidx % 8][64:128, :], kqf[kidx % 8][0:64, :]).then_inc(sems[f"swL{kidx % 8}"], 16)
                                cnt.add(f"swL{kidx % 8}", 32)
                                W(y, "vKR", kidx + 1)
                                y.dma_start(ktr_scr[gg, :, cp * 512:(cp + 1) * 512], ktr_t[kidx % 2][:, :]).then_inc(sems[f"kspL{kidx % 2}"], 16)
                                cnt.add(f"kspL{kidx % 2}", 16)
                            for h2 in range(2):
                                for tp in range(2):
                                    half = cp * 2 + h2
                                    vidx = 2 * half + tp
                                    slot = (half % 2) * 2 + tp
                                    W(y, "vVE", vidx + 1)
                                    ttg = cp * 4 + 2 * h2 + tp
                                    y.dma_start(v_scr[ttg], v_bf[slot][:, :]).then_inc(sems[f"vspL{slot}"], 16)
                                    cnt.add(f"vspL{slot}", 16)
                    for nm in ["kspL0", "kspL1", "vspL0", "vspL1", "vspL2", "vspL3"]:
                        y.wait_ge(sems[nm], cnt[nm])

                @blk.vector
                def _(v):
                    v.wait_ge(sems["yP0"], 32)
                    for c in range(9):
                        if c < 8:
                            for tt in range(4):
                                gt = 4 * c + tt
                                par = gt % 2
                                W(v, f"xaL{par}", 16 * (gt // 2 + 1))
                                W(v, "sSD", 2 * gt - 3)
                                v.reduce_sum(t_s[:, par:par + 1], xa[par][:, :], axis=AX.X)
                                W(v, "sSQ", gt + 1)
                                v.reduce_sum(t_q[:, par:par + 1], xsq[par][:, :], axis=AX.X).then_inc(sems["vVAR"], 1)
                                cnt.add("vVAR", 1)
                                W(v, "sSD", gt * 2 + 1)
                                v.tensor_scalar(t_var[:, par:par + 1], t_v1[:, par:par + 1],
                                                t_msq[:, par:par + 1], None, ALU.subtract).then_inc(sems["vVAR"], 1)
                                cnt.add("vVAR", 1)
                                W(v, "sSD", gt * 2 + 2)
                                v.reciprocal(t_r[:, par:par + 1], t_sd[:, par:par + 1])
                                v.tensor_scalar(xsq[par][:, :], xa[par][:, :], t_mean[:, par:par + 1],
                                                None, ALU.subtract)
                                W(v, "tTR", 4 * (gt - 1))
                                v.tensor_scalar(xn[par][:, :], xsq[par][:, :], t_r[:, par:par + 1],
                                                None, ALU.mult).then_inc(sems["vXN"], 1)
                                cnt.add("vXN", 1)
                        if c > 0:
                            cp = c - 1
                            for gg in range(8):
                                kidx = cp * 8 + gg
                                W(v, "sKE", kidx + 1)
                                W(v, f"tbL{cp % 2}", 32 * (cp // 2 + 1))
                                v.tensor_mul(ropet[:, :], kqf[kidx % 8][:, :], ck[cp % 2][:, :])
                                W(v, f"swL{kidx % 8}", 32 * (kidx // 8 + 1))
                                v.tensor_mul(ropet2[:, :], kswp[kidx % 8][:, :], sk[cp % 2][:, :])
                                W(v, f"kspL{kidx % 2}", 16 * (kidx // 2))
                                v.tensor_add(ktr_t[kidx % 2][:, :], ropet[:, :], ropet2[:, :]).then_inc(sems["vKR"], 1)
                                cnt.add("vKR", 1)

                @blk.scalar
                def _(sc):
                    sc.wait_ge(sems["yP0"], 32)
                    for c in range(9):
                        if c < 8:
                            for tt in range(4):
                                gt = 4 * c + tt
                                par = gt % 2
                                W(sc, f"xaL{par}", 16 * (gt // 2 + 1))
                                W(sc, "vVAR", 2 * gt - 1)
                                W(sc, "vXN", gt - 1)
                                sc.activation(xsq[par][:, :], xa[par][:, :], AF.Square).then_inc(sems["sSQ"], 1)
                                cnt.add("sSQ", 1)
                                W(sc, "vVAR", 2 * gt + 1)
                                W(sc, "vXN", gt - 1)
                                sc.activation(t_mean[:, par:par + 1], t_s[:, par:par + 1], AF.Copy, scale=1.0 / D)
                                sc.activation(t_msq[:, par:par + 1], t_s[:, par:par + 1], AF.Square, scale=1.0 / D)
                                sc.activation(t_v1[:, par:par + 1], t_q[:, par:par + 1], AF.Copy, scale=1.0 / D).then_inc(sems["sSD"], 1)
                                cnt.add("sSD", 1)
                                W(sc, "vVAR", 2 * gt + 2)
                                sc.activation(t_sd[:, par:par + 1], t_var[:, par:par + 1], AF.Sqrt, bias=c_eps).then_inc(sems["sSD"], 1)
                                cnt.add("sSD", 1)
                                for q in range(4):
                                    idx = 4 * gt + q
                                    W(sc, "tTR", idx + 1)
                                    W(sc, "tVg", 16 * c)
                                    dst = xnt.ap().rearrange("p (d t) -> p d t", t=512)[:, 8 * q:8 * q + 8, tt * 128:(tt + 1) * 128]
                                    src = pst.ap().rearrange("p (d t) -> p d t", t=128)[:, 8 * (idx % 2):8 * (idx % 2) + 8, :]
                                    sc.copy(dst, src).then_inc(sems["sTE"], 1)
                                    cnt.add("sTE", 1)
                            for gg in range(8):
                                kidx = c * 8 + gg
                                W(sc, "tK", kidx + 1)
                                W(sc, "vKR", kidx - 7)
                                W(sc, f"swL{kidx % 8}", 32 * (kidx // 8))
                                sc.copy(kqf[kidx % 8][:, :], bank(gg % 2)).then_inc(sems["sKE"], 1)
                                cnt.add("sKE", 1)
                            for h2 in range(2):
                                for tp in range(2):
                                    half = c * 2 + h2
                                    slot = (half % 2) * 2 + tp
                                    W(sc, "tVg", 8 * (half + 1))
                                    W(sc, f"vspL{slot}", 16 * (half // 2))
                                    sc.copy(v_bf[slot][:, :], bank2(2 + 2 * tp)).then_inc(sems["vVE"], 1)
                                    cnt.add("vVE", 1)

                @blk.tensor
                def _(t):
                    t.wait_ge(sems["gP0"], 48)
                    for c in range(8):
                        for tt in range(4):
                            gt = 4 * c + tt
                            W(t, "vXN", gt + 1)
                            for q in range(4):
                                idx = 4 * gt + q
                                W(t, "sTE", idx - 1)
                                for k in range(8):
                                    dt = 8 * q + k
                                    ins = t.transpose(pst[:, (idx % 2) * 1024 + k * 128:(idx % 2) * 1024 + (k + 1) * 128],
                                                      xn[gt % 2][:, dt * 128:(dt + 1) * 128],
                                                      ident_sb[:, :])
                                ins.then_inc(sems["tTR"], 1)
                                cnt.add("tTR", 1)
                        W(t, "sTE", 16 * (c + 1))
                        for gg in range(8):
                            kidx = c * 8 + gg
                            W(t, f"wkL{kidx % 2}", 16 * (kidx // 2 + 1))
                            W(t, "sKE", kidx - 1)
                            for dt in range(32):
                                ins = t.matmul(bank(gg % 2), wkS[kidx % 2][:, dt * 128:(dt + 1) * 128],
                                               xnt[:, dt * 512:(dt + 1) * 512],
                                               start=(dt == 0), stop=(dt == 31))
                            ins.then_inc(sems["tK"], 1)
                            cnt.add("tK", 1)
                        for h2 in range(2):
                            half = c * 2 + h2
                            W(t, "vVE", 2 * half)
                            for dt in range(32):
                                vl = wv_lidx(c, h2, dt // 4)
                                if dt % 4 == 0:
                                    W(t, f"wvL{vl % 2}", 16 * (vl // 2 + 1))
                                for tp in range(2):
                                    ttl = 2 * h2 + tp
                                    for colh in range(2):
                                        ins = t.matmul(bank(2 + 2 * tp + colh),
                                                       xnt[:, dt * 512 + ttl * 128:dt * 512 + (ttl + 1) * 128],
                                                       wvS[vl % 2][:, (dt % 4) * 1024 + colh * 512:(dt % 4) * 1024 + (colh + 1) * 512],
                                                       start=(dt == 0), stop=(dt == 31))
                                if dt % 4 == 3:
                                    ins.then_inc(sems["tVg"], 1)
                                    cnt.add("tVg", 1)

        with contextlib.ExitStack() as ST:
            EA = ST.enter_context
            xa = [EA(nc.sbuf_tensor(f"x2a{i}", [128, D], F32)) for i in range(2)]
            xsq = [EA(nc.sbuf_tensor(f"x2sq{i}", [128, D], F32)) for i in range(2)]
            xn = [EA(nc.sbuf_tensor(f"x2n{i}", [128, D], BF16)) for i in range(2)]
            xnt = EA(nc.sbuf_tensor("xnt_own", [128, 32 * 1024], BF16))
            t_s = EA(nc.sbuf_tensor("t2_s", [128, 2], F32))
            t_q = EA(nc.sbuf_tensor("t2_q", [128, 2], F32))
            t_mean = EA(nc.sbuf_tensor("t2_mean", [128, 2], F32))
            t_msq = EA(nc.sbuf_tensor("t2_msq", [128, 2], F32))
            t_v1 = EA(nc.sbuf_tensor("t2_v1", [128, 2], F32))
            t_var = EA(nc.sbuf_tensor("t2_var", [128, 2], F32))
            t_sd = EA(nc.sbuf_tensor("t2_sd", [128, 2], F32))
            t_r = EA(nc.sbuf_tensor("t2_r", [128, 2], F32))
            cq = EA(nc.sbuf_tensor("cq", [128, TOWN], F32))
            sq = EA(nc.sbuf_tensor("sq", [128, TOWN], F32))
            wqS = [EA(nc.sbuf_tensor(f"wqS{i}", [128, D], BF16)) for i in range(2)]
            qf = [EA(nc.sbuf_tensor(f"qf{i}", [128, TOWN], F32)) for i in range(2)]
            qswp = [EA(nc.sbuf_tensor(f"qswp{i}", [128, TOWN], F32)) for i in range(2)]
            ropet = EA(nc.sbuf_tensor("ropetq", [128, TOWN], F32))
            ropet2 = EA(nc.sbuf_tensor("ropetq2", [128, TOWN], F32))
            qtr_t = [EA(nc.sbuf_tensor(f"qtr_t{i}", [128, TOWN], BF16)) for i in range(2)]

            base = {k: cnt[k] for k in ["sSQ", "vVAR", "sSD", "vXN", "tTR", "sTE"]}

            with nc.Block("pA2") as blk:
                @blk.gpsimd
                def _(g):
                    for h in range(NH):
                        W(g, "tQ", h - 1)
                        g.dma_start(wqS[h % 2][:, :], wq_lin[h]).then_inc(sems[f"wqL{h % 2}"], 16)
                        cnt.add(f"wqL{h % 2}", 16)

                @blk.sync
                def _(y):
                    y.dma_start(cq[:, :], cosq_in).then_inc(sems["qtb"], 16)
                    y.dma_start(sq[:, :], sinq_in).then_inc(sems["qtb"], 16)
                    cnt.add("qtb", 32)
                    for gt in range(8):
                        W(y, "vXN", base["vXN"] + gt - 1)
                        W(y, "sSQ", base["sSQ"] + gt - 1)
                        y.dma_start(xa[gt % 2][:, :], x_own[gt * 128:(gt + 1) * 128, :]).then_inc(sems[f"x2L{gt % 2}"], 16)
                        cnt.add(f"x2L{gt % 2}", 16)
                    for h in range(NH):
                        W(y, "sQE", h + 1)
                        y.dma_start(qswp[h % 2][0:64, :], qf[h % 2][64:128, :]).then_inc(sems[f"qswL{h % 2}"], 16)
                        y.dma_start(qswp[h % 2][64:128, :], qf[h % 2][0:64, :]).then_inc(sems[f"qswL{h % 2}"], 16)
                        cnt.add(f"qswL{h % 2}", 32)
                        W(y, "vQR", h + 1)
                        y.dma_start(qtr_scr[h], qtr_t[h % 2][:, :]).then_inc(sems[f"qspL{h % 2}"], 16)
                        cnt.add(f"qspL{h % 2}", 16)
                    W(y, "sTE", base["sTE"] + 32)
                    y.dma_start(xnt_scr.transpose([1, 0, 2]),
                                xnt.ap().rearrange("p (d t) -> p d t", t=1024)).then_inc(sems["xsp"], 16)
                    cnt.add("xsp", 16)
                    y.wait_ge(sems["qspL0"], cnt["qspL0"])
                    y.wait_ge(sems["qspL1"], cnt["qspL1"])
                    y.wait_ge(sems["xsp"], cnt["xsp"])

                @blk.vector
                def _(v):
                    for gt in range(8):
                        par = gt % 2
                        v.wait_ge(sems[f"x2L{par}"], 16 * (gt // 2 + 1))
                        W(v, "sSD", base["sSD"] + 2 * gt - 3)
                        v.reduce_sum(t_s[:, par:par + 1], xa[par][:, :], axis=AX.X)
                        W(v, "sSQ", base["sSQ"] + gt + 1)
                        v.reduce_sum(t_q[:, par:par + 1], xsq[par][:, :], axis=AX.X).then_inc(sems["vVAR"], 1)
                        cnt.add("vVAR", 1)
                        W(v, "sSD", base["sSD"] + gt * 2 + 1)
                        v.tensor_scalar(t_var[:, par:par + 1], t_v1[:, par:par + 1],
                                        t_msq[:, par:par + 1], None, ALU.subtract).then_inc(sems["vVAR"], 1)
                        cnt.add("vVAR", 1)
                        W(v, "sSD", base["sSD"] + gt * 2 + 2)
                        v.reciprocal(t_r[:, par:par + 1], t_sd[:, par:par + 1])
                        v.tensor_scalar(xsq[par][:, :], xa[par][:, :], t_mean[:, par:par + 1],
                                        None, ALU.subtract)
                        W(v, "tTR", base["tTR"] + 4 * (gt - 1))
                        v.tensor_scalar(xn[par][:, :], xsq[par][:, :], t_r[:, par:par + 1],
                                        None, ALU.mult).then_inc(sems["vXN"], 1)
                        cnt.add("vXN", 1)
                    for h in range(NH):
                        W(v, "sQE", h + 1)
                        W(v, "qtb", 32)
                        v.tensor_mul(ropet[:, :], qf[h % 2][:, :], cq[:, :])
                        W(v, f"qswL{h % 2}", 32 * (h // 2 + 1))
                        v.tensor_mul(ropet2[:, :], qswp[h % 2][:, :], sq[:, :])
                        W(v, f"qspL{h % 2}", 16 * (h // 2))
                        v.tensor_add(qtr_t[h % 2][:, :], ropet[:, :], ropet2[:, :]).then_inc(sems["vQR"], 1)
                        cnt.add("vQR", 1)

                @blk.scalar
                def _(sc):
                    for gt in range(8):
                        par = gt % 2
                        sc.wait_ge(sems[f"x2L{par}"], 16 * (gt // 2 + 1))
                        W(sc, "vVAR", base["vVAR"] + 2 * gt - 1)
                        W(sc, "vXN", base["vXN"] + gt - 1)
                        sc.activation(xsq[par][:, :], xa[par][:, :], AF.Square).then_inc(sems["sSQ"], 1)
                        cnt.add("sSQ", 1)
                        W(sc, "vVAR", base["vVAR"] + 2 * gt + 1)
                        W(sc, "vXN", base["vXN"] + gt - 1)
                        sc.activation(t_mean[:, par:par + 1], t_s[:, par:par + 1], AF.Copy, scale=1.0 / D)
                        sc.activation(t_msq[:, par:par + 1], t_s[:, par:par + 1], AF.Square, scale=1.0 / D)
                        sc.activation(t_v1[:, par:par + 1], t_q[:, par:par + 1], AF.Copy, scale=1.0 / D).then_inc(sems["sSD"], 1)
                        cnt.add("sSD", 1)
                        W(sc, "vVAR", base["vVAR"] + 2 * gt + 2)
                        sc.activation(t_sd[:, par:par + 1], t_var[:, par:par + 1], AF.Sqrt, bias=c_eps).then_inc(sems["sSD"], 1)
                        cnt.add("sSD", 1)
                        for q in range(4):
                            idx = 4 * gt + q
                            W(sc, "tTR", base["tTR"] + idx + 1)
                            dst = xnt.ap().rearrange("p (d t) -> p d t", t=1024)[:, 8 * q:8 * q + 8, gt * 128:(gt + 1) * 128]
                            src = pst.ap().rearrange("p (d t) -> p d t", t=128)[:, 8 * (idx % 2):8 * (idx % 2) + 8, :]
                            sc.copy(dst, src).then_inc(sems["sTE"], 1)
                            cnt.add("sTE", 1)
                    for h in range(NH):
                        W(sc, "tQ", h + 1)
                        W(sc, "vQR", h - 1)
                        W(sc, f"qswL{h % 2}", 32 * (h // 2))
                        sc.copy(qf[h % 2][:, :], bank2((h % 2) * 2)).then_inc(sems["sQE"], 1)
                        cnt.add("sQE", 1)

                @blk.tensor
                def _(t):
                    for gt in range(8):
                        W(t, "vXN", base["vXN"] + gt + 1)
                        for q in range(4):
                            idx = 4 * gt + q
                            W(t, "sTE", base["sTE"] + idx - 1)
                            for k in range(8):
                                dt = 8 * q + k
                                ins = t.transpose(pst[:, (idx % 2) * 1024 + k * 128:(idx % 2) * 1024 + (k + 1) * 128],
                                                  xn[gt % 2][:, dt * 128:(dt + 1) * 128],
                                                  ident_sb[:, :])
                            ins.then_inc(sems["tTR"], 1)
                            cnt.add("tTR", 1)
                    W(t, "sTE", base["sTE"] + 32)
                    for h in range(NH):
                        W(t, f"wqL{h % 2}", 16 * (h // 2 + 1))
                        W(t, "sQE", h - 1)
                        for dt in range(32):
                            for colh in range(2):
                                ins = t.matmul(bank((h % 2) * 2 + colh),
                                               wqS[h % 2][:, dt * 128:(dt + 1) * 128],
                                               xnt[:, dt * 1024 + colh * 512:dt * 1024 + (colh + 1) * 512],
                                               start=(dt == 0), stop=(dt == 31))
                        ins.then_inc(sems["tQ"], 1)
                        cnt.add("tQ", 1)

        if upto == "A":
            return nc

        with contextlib.ExitStack() as STB:
            EB = STB.enter_context
            attn = EB(nc.sbuf_tensor("attn", [128, NH * TOWN], BF16))
            with contextlib.ExitStack() as ST:
                EA = ST.enter_context
                ktr_g = [EA(nc.sbuf_tensor(f"ktr_g{i}", [128, S], BF16)) for i in range(2)]
                v_g = [EA(nc.sbuf_tensor(f"v_g{i}", [128, S], BF16)) for i in range(2)]
                qtr = [EA(nc.sbuf_tensor(f"qtr{i}", [128, TOWN], BF16)) for i in range(2)]
                ptb = [EA(nc.sbuf_tensor(f"ptb{i}", [128, 512], BF16)) for i in range(4)]
                lnt = EA(nc.sbuf_tensor("lnt", [1, 512], F32))
                rcp = EA(nc.sbuf_tensor("rcp", [1, 512], F32))
                rb = EA(nc.sbuf_tensor("rb", [128, 512], F32))

                PASS = []
                for h in range(NH):
                    PASS.append((h, 0, 32))
                    PASS.append((h, 1, 16))
                e_of_pass = [0]
                for (_, _, nj) in PASS:
                    e_of_pass.append(e_of_pass[-1] + nj)

                def o_bank(pl):
                    return 2 if pl == 0 else 4

                def s_bank(pl):
                    return 3 if pl == 0 else 5

                def wof(pl, j):
                    return wA_of(j) if pl == 0 else wB_of(j)

                def masked(pl, j):
                    return (j >= 16) if pl == 0 else True

                def stbank(pl, j):
                    return (0, 1, 4)[j % 3] if pl == 0 else (0, 1, 2)[j % 3]

                def mask_count_upto(p, j):
                    n = 16 * p
                    h, pl, nj = PASS[p]
                    for jj in range(j + 1):
                        if masked(pl, jj):
                            n += 1
                    return n

                bSE = cnt["sE"]; bTST = cnt["tST"]; bTPV = cnt["tPV"]; bVMK = cnt["vMK"]
                bSRC = cnt["sRC"]; bTBC = cnt["tBC"]; bSRB = cnt["sRB"]; bVEV = cnt["vEV"]

                with nc.Block("pB") as blk:
                    @blk.sync
                    def _(y):
                        for gg in range(NG):
                            if gg >= 2:
                                W(y, "tPV", bTPV + 192 * (gg - 1))
                            y.dma_start(ktr_g[gg % 2][:, :], ktr_scr[gg]).then_inc(sems[f"kgl{gg % 2}"], 16)
                            cnt.add(f"kgl{gg % 2}", 16)
                            y.dma_start(v_g[gg % 2].ap().rearrange("p (j q) -> p j q", q=128),
                                        v_scr[:, :, gg * 128:(gg + 1) * 128].transpose([1, 0, 2])).then_inc(sems[f"vgl{gg % 2}"], 16)
                            cnt.add(f"vgl{gg % 2}", 16)
                            for hq in range(4):
                                h = gg * 4 + hq
                                if h >= 2:
                                    W(y, "tST", bTST + e_of_pass[2 * (h - 1)])
                                y.dma_start(qtr[h % 2][:, :], qtr_scr[h]).then_inc(sems[f"qgl{h % 2}"], 16)
                                cnt.add(f"qgl{h % 2}", 16)

                    @blk.scalar
                    def _(sc):
                        for p, (h, pl, nj) in enumerate(PASS):
                            for j in range(nj):
                                e = e_of_pass[p] + j
                                w = wof(pl, j)
                                W(sc, "tST", bTST + e + 1)
                                W(sc, "tPV", bTPV + e - 3)
                                sc.activation(ptb[e % 4][:, 0:w], bank(stbank(pl, j), w), AF.Exp,
                                              bias=c_shift, scale=SCALE).then_inc(sems["sE"], 1)
                                cnt.add("sE", 1)
                            W(sc, "tPV", bTPV + e_of_pass[p + 1])
                            sc.activation(lnt[:, :], bank(s_bank(pl), 512, 0, 1), AF.Ln)
                            sc.activation(rcp[:, :], lnt[:, :], AF.Exp, scale=-1.0).then_inc(sems["sRC"], 1)
                            cnt.add("sRC", 1)
                            W(sc, "tBC", bTBC + p + 1)
                            W(sc, "vEV", bVEV + p)
                            sc.copy(rb[:, :], bank(s_bank(pl))).then_inc(sems["sRB"], 1)
                            cnt.add("sRB", 1)

                    @blk.vector
                    def _(v):
                        v.wait_ge(sems["gP0"], 48)
                        for p, (h, pl, nj) in enumerate(PASS):
                            for j in range(nj):
                                if not masked(pl, j):
                                    continue
                                e = e_of_pass[p] + j
                                w = wof(pl, j)
                                W(v, "sE", bSE + e + 1)
                                v.tensor_mul(ptb[e % 4][:, w - 128:w], ptb[e % 4][:, w - 128:w],
                                             masks_sb[:, j * 128:(j + 1) * 128]).then_inc(sems["vMK"], 1)
                                cnt.add("vMK", 1)
                            W(v, "sRB", bSRB + p + 1)
                            v.tensor_mul(attn[:, h * 1024 + pl * 512:h * 1024 + (pl + 1) * 512],
                                         bank2(o_bank(pl), 512), rb[:, :]).then_inc(sems["vEV"], 1)
                            cnt.add("vEV", 1)

                    @blk.tensor
                    def _(t):
                        t.wait_ge(sems["gP0"], 48)
                        ledger = {}

                        def bank_wait(b):
                            if b in ledger:
                                W(t, ledger[b][0], ledger[b][1])

                        def emit_ST(p, j):
                            h, pl, nj = PASS[p]
                            gg = h // 4
                            e = e_of_pass[p] + j
                            w = wof(pl, j)
                            b = stbank(pl, j)
                            bank_wait(b)
                            ledger[b] = ("sE", bSE + e + 1)
                            if j == 0:
                                W(t, f"kgl{gg % 2}", 16 * (gg // 2 + 1))
                                W(t, f"qgl{h % 2}", 16 * (h // 2 + 1))
                            t.matmul(bank(b, w), ktr_g[gg % 2][:, j * 128:(j + 1) * 128],
                                     qtr[h % 2][:, pl * 512:pl * 512 + w],
                                     start=True, stop=True).then_inc(sems["tST"], 1)
                            cnt.add("tST", 1)

                        def emit_PV(p, j):
                            h, pl, nj = PASS[p]
                            gg = h // 4
                            e = e_of_pass[p] + j
                            w = wof(pl, j)
                            W(t, "sE", bSE + e + 1)
                            if masked(pl, j):
                                W(t, "vMK", bVMK + mask_count_upto(p, j))
                            if j == 0:
                                bank_wait(o_bank(pl))
                                bank_wait(s_bank(pl))
                                ledger[o_bank(pl)] = ("vEV", bVEV + p + 1)
                                ledger[s_bank(pl)] = ("sRB", bSRB + p + 1)
                                W(t, f"vgl{gg % 2}", 16 * (gg // 2 + 1))
                            t.matmul(bank2(o_bank(pl), w), v_g[gg % 2][:, j * 128:(j + 1) * 128],
                                     ptb[e % 4][:, 0:w], start=(j == 0), stop=(j == nj - 1))
                            t.matmul(bank(s_bank(pl), w, 0, 1), onesc_sb[:, :],
                                     ptb[e % 4][:, 0:w],
                                     start=(j == 0), stop=(j == nj - 1)).then_inc(sems["tPV"], 1)
                            cnt.add("tPV", 1)

                        def emit_BC(p):
                            h, pl, nj = PASS[p]
                            W(t, "sRC", bSRC + p + 1)
                            t.matmul(bank(s_bank(pl)), onesr_sb[:, :], rcp[:, :],
                                     start=True, stop=True).then_inc(sems["tBC"], 1)
                            cnt.add("tBC", 1)

                        LAG = 3
                        pend_bc = None
                        for p in range(len(PASS)):
                            h, pl, nj = PASS[p]
                            for j in range(nj):
                                emit_ST(p, j)
                                if j == 0 and pend_bc is not None:
                                    emit_BC(pend_bc)
                                    pend_bc = None
                                if j >= LAG:
                                    emit_PV(p, j - LAG)
                            for j in range(nj - LAG, nj):
                                emit_PV(p, j)
                            pend_bc = p
                        emit_BC(pend_bc)

                    if dump:
                        @blk.gpsimd
                        def _(g):
                            g.wait_ge(sems["vEV"], bVEV + 64)
                            g.dma_start(attn_dump, attn[:, :]).then_inc(sems["gAL0"], 16)
                            cnt.add("gAL0", 16)
                            g.wait_ge(sems["gAL0"], cnt["gAL0"])

            if upto == "B":
                return nc

            with contextlib.ExitStack() as ST:
                EA = ST.enter_context
                woS = [EA(nc.sbuf_tensor(f"woS{i}", [128, 32 * 512], BF16)) for i in range(2)]
                xo = [EA(nc.sbuf_tensor(f"xo{i}", [128, 512], F32)) for i in range(4)]
                yf = [EA(nc.sbuf_tensor(f"yf{i}", [128, 512], F32)) for i in range(2)]
                bTY = cnt["tY"]; bVY = cnt["vY"]

                with nc.Block("pC1") as blk:
                    @blk.gpsimd
                    def _(g):
                        for dc in range(8):
                            W(g, "tY", bTY + 8 * (dc - 1))
                            g.dma_start(woS[dc % 2].ap().rearrange("p (a q) -> p a q", q=512),
                                        wo_lin[dc].transpose([1, 0, 2])).then_inc(sems[f"woL{dc % 2}"], 16)
                            cnt.add(f"woL{dc % 2}", 16)

                    @blk.sync
                    def _(y):
                        for dc in range(8):
                            for tt in range(8):
                                qi = dc * 8 + tt
                                W(y, "vY", bVY + qi - 3)
                                y.dma_start(xo[qi % 4][:, :], x_own[tt * 128:(tt + 1) * 128, dc * 512:(dc + 1) * 512]).then_inc(sems[f"xoL{qi % 4}"], 16)
                                cnt.add(f"xoL{qi % 4}", 16)
                                W(y, "vY", bVY + qi + 1)
                                y.dma_start(out[tt * 128:(tt + 1) * 128, dc * 512:(dc + 1) * 512], yf[qi % 2][:, :]).then_inc(sems[f"yYL{qi % 2}"], 16)
                                cnt.add(f"yYL{qi % 2}", 16)
                        y.wait_ge(sems["yYL0"], cnt["yYL0"])
                        y.wait_ge(sems["yYL1"], cnt["yYL1"])

                    @blk.tensor
                    def _(t):
                        for dc in range(8):
                            W(t, f"woL{dc % 2}", 16 * (dc // 2 + 1))
                            for tt in range(8):
                                qi = dc * 8 + tt
                                W(t, "vY", bVY + qi - 1)
                                for at in range(32):
                                    ins = t.matmul(bank(qi % 2), attn[:, at * 1024 + tt * 128:at * 1024 + (tt + 1) * 128],
                                                   woS[dc % 2][:, at * 512:(at + 1) * 512],
                                                   start=(at == 0), stop=(at == 31))
                                ins.then_inc(sems["tY"], 1)
                                cnt.add("tY", 1)

                    @blk.vector
                    def _(v):
                        for dc in range(8):
                            for tt in range(8):
                                qi = dc * 8 + tt
                                W(v, "tY", bTY + qi + 1)
                                v.wait_ge(sems[f"xoL{qi % 4}"], 16 * (qi // 4 + 1))
                                W(v, f"yYL{qi % 2}", 16 * (qi // 2))
                                v.tensor_add(yf[qi % 2][:, :], bank(qi % 2), xo[qi % 4][:, :]).then_inc(sems["vY"], 1)
                                cnt.add("vY", 1)

        with contextlib.ExitStack() as ST:
            EA = ST.enter_context
            xnt = EA(nc.sbuf_tensor("xnt_c2", [128, 32 * 1024], BF16))
            inter = EA(nc.sbuf_tensor("inter", [128, 16 * 1024], BF16))
            wgS = [EA(nc.sbuf_tensor(f"wgS{i}", [128, D], BF16)) for i in range(2)]
            wuS = [EA(nc.sbuf_tensor(f"wuS{i}", [128, D], BF16)) for i in range(2)]
            wdS = [EA(nc.sbuf_tensor(f"wdS{i}", [128, 16 * 512], BF16)) for i in range(2)]
            gsil = [EA(nc.sbuf_tensor(f"gsil{i}", [128, 1024], BF16)) for i in range(2)]
            ydf = [EA(nc.sbuf_tensor(f"ydf{i}", [128, 512], F32)) for i in range(2)]
            bYY0 = cnt["yYL0"]; bYY1 = cnt["yYL1"]
            bGA0 = cnt["gAL0"]; bGA1 = cnt["gAL1"]
            bTG = cnt["tG"]; bTU = cnt["tU"]; bSIL = cnt["sSIL"]; bVIN = cnt["vIN"]
            bTD = cnt["tD"]; bVDE = cnt["vDE"]

            it0_of = [0]
            for nb in FFN_BLOCKS:
                it0_of.append(it0_of[-1] + nb)
            NB = len(FFN_BLOCKS)

            with nc.Block("pC2") as blk:
                def wgu_loads(g, b, lo=0, hi=None):
                    for li in range(lo, FFN_BLOCKS[b] if hi is None else hi):
                        gi = it0_of[b] + li
                        W(g, "sSIL", bSIL + gi - 1)
                        g.dma_start(wgS[gi % 2][:, :], wg_lin[gi]).then_inc(sems[f"wgL{gi % 2}"], 16)
                        cnt.add(f"wgL{gi % 2}", 16)
                        W(g, "vIN", bVIN + gi - 1)
                        g.dma_start(wuS[gi % 2][:, :], wu_lin[gi]).then_inc(sems[f"wuL{gi % 2}"], 16)
                        cnt.add(f"wuL{gi % 2}", 16)

                def emit_accums(g, b, dcg):
                    if b == 0 and dcg == 0:
                        g.wait_ge(sems["yYL0"], bYY0)
                        g.wait_ge(sems["yYL1"], bYY1)
                    for k in range(8):
                        r = dcg * 8 + k
                        dc, tt = divmod(r, 8)
                        W(g, "vDE", bVDE + 64 * b + r + 1)
                        g.dma_start(out[tt * 128:(tt + 1) * 128, dc * 512:(dc + 1) * 512],
                                    ydf[r % 2][:, :], accum_op=ALU.add).then_inc(sems[f"gAL{r % 2}"], 16)
                        cnt.add(f"gAL{r % 2}", 16)

                @blk.gpsimd
                def _(g):
                    wgu_loads(g, 0)
                    for b, nb in enumerate(FFN_BLOCKS):
                        for dc in range(8):
                            if dc >= 2:
                                emit_accums(g, b, dc - 2)
                            W(g, "tD", bTD + 64 * b + 8 * (dc - 1))
                            g.dma_start(wdS[dc % 2].ap().rearrange("p (i q) -> p i q", q=512)[:, 0:nb, :],
                                        wd_lin[it0_of[b]:it0_of[b] + nb, :, dc * 512:(dc + 1) * 512].transpose([1, 0, 2])).then_inc(sems[f"wdL{dc % 2}"], 16)
                            cnt.add(f"wdL{dc % 2}", 16)
                            if dc == 3 and b + 1 < NB:
                                wgu_loads(g, b + 1, 0, 2)
                        emit_accums(g, b, 6)
                        emit_accums(g, b, 7)
                        if b + 1 < NB:
                            wgu_loads(g, b + 1, 2)
                    g.wait_ge(sems["gAL0"], cnt["gAL0"])
                    g.wait_ge(sems["gAL1"], cnt["gAL1"])

                @blk.sync
                def _(y):
                    y.dma_start(xnt.ap().rearrange("p (d t) -> p d t", t=1024),
                                xnt_scr.transpose([1, 0, 2])).then_inc(sems["xrl"], 16)
                    cnt.add("xrl", 16)
                    y.wait_ge(sems["gAL0"], cnt["gAL0"])
                    y.wait_ge(sems["gAL1"], cnt["gAL1"])

                @blk.tensor
                def _(t):
                    W(t, "xrl", 16)
                    for b, nb in enumerate(FFN_BLOCKS):
                        for li in range(nb):
                            gi = it0_of[b] + li
                            gbk = (gi % 2) * 2
                            W(t, f"wgL{gi % 2}", 16 * (gi // 2 + 1))
                            W(t, "sSIL", bSIL + gi - 1)
                            if li == 0 and b > 0:
                                W(t, "vDE", bVDE + 64 * b)
                            for dt in range(32):
                                for colh in range(2):
                                    ins = t.matmul(bank(gbk + colh), wgS[gi % 2][:, dt * 128:(dt + 1) * 128],
                                                   xnt[:, dt * 1024 + colh * 512:dt * 1024 + (colh + 1) * 512],
                                                   start=(dt == 0), stop=(dt == 31))
                            ins.then_inc(sems["tG"], 1)
                            cnt.add("tG", 1)
                            W(t, f"wuL{gi % 2}", 16 * (gi // 2 + 1))
                            W(t, "vIN", bVIN + gi)
                            for dt in range(32):
                                for colh in range(2):
                                    ins = t.matmul(bank(4 + colh), wuS[gi % 2][:, dt * 128:(dt + 1) * 128],
                                                   xnt[:, dt * 1024 + colh * 512:dt * 1024 + (colh + 1) * 512],
                                                   start=(dt == 0), stop=(dt == 31))
                            ins.then_inc(sems["tU"], 1)
                            cnt.add("tU", 1)
                        W(t, "vIN", bVIN + it0_of[b] + nb)
                        for r in range(64):
                            dc, tt = divmod(r, 8)
                            if tt == 0:
                                W(t, f"wdL{dc % 2}", 16 * (4 * b + dc // 2 + 1))
                            W(t, "vDE", bVDE + 64 * b + r - 1)
                            if r == 0:
                                W(t, "sSIL", bSIL + it0_of[b] + nb)
                            for li in range(nb):
                                ins = t.matmul(bank(r % 2), inter[:, li * 1024 + tt * 128:li * 1024 + (tt + 1) * 128],
                                               wdS[dc % 2][:, li * 512:(li + 1) * 512],
                                               start=(li == 0), stop=(li == nb - 1))
                            ins.then_inc(sems["tD"], 1)
                            cnt.add("tD", 1)

                @blk.scalar
                def _(sc):
                    for b, nb in enumerate(FFN_BLOCKS):
                        for li in range(nb):
                            gi = it0_of[b] + li
                            W(sc, "tG", bTG + gi + 1)
                            W(sc, "vIN", bVIN + gi - 1)
                            sc.activation(gsil[gi % 2][:, :], bank2((gi % 2) * 2), AF.Silu).then_inc(sems["sSIL"], 1)
                            cnt.add("sSIL", 1)

                @blk.vector
                def _(v):
                    for b, nb in enumerate(FFN_BLOCKS):
                        for li in range(nb):
                            gi = it0_of[b] + li
                            W(v, "tU", bTU + gi + 1)
                            W(v, "sSIL", bSIL + gi + 1)
                            v.tensor_mul(inter[:, li * 1024:(li + 1) * 1024], bank2(4),
                                         gsil[gi % 2][:, :]).then_inc(sems["vIN"], 1)
                            cnt.add("vIN", 1)
                        for r in range(64):
                            W(v, "tD", bTD + 64 * b + r + 1)
                            W(v, f"gAL{r % 2}", (bGA0 if r % 2 == 0 else bGA1) + 16 * (32 * b + r // 2))
                            v.tensor_copy(ydf[r % 2][:, :], bank(r % 2)).then_inc(sems["vDE"], 1)
                            cnt.add("vDE", 1)

    return nc



def _host_inputs(L0_attn_x, L0_attn_w_qkv, L0_attn_w_o, L0_ffn_w_gate, L0_ffn_w_up, L0_ffn_w_down):
    x = np.asarray(L0_attn_x, dtype=np.float32)
    w_qkv = np.asarray(L0_attn_w_qkv, dtype=np.float32)
    w_o = np.asarray(L0_attn_w_o, dtype=np.float32)
    w_gate = np.asarray(L0_ffn_w_gate, dtype=np.float32)
    w_up = np.asarray(L0_ffn_w_up, dtype=np.float32)
    w_down = np.asarray(L0_ffn_w_down, dtype=np.float32)

    perm = np.concatenate([np.arange(0, 128, 2), np.arange(1, 128, 2)])
    w_q = np.ascontiguousarray(w_qkv[:, :4096].reshape(D, 32, 128)[:, :, perm].reshape(D, 4096))
    w_k = np.ascontiguousarray(w_qkv[:, 4096:5120].reshape(D, 8, 128)[:, :, perm].reshape(D, 1024))
    w_v = np.ascontiguousarray(w_qkv[:, 5120:6144])

    wq_lin = np.ascontiguousarray(w_q.reshape(32, 128, 32, 128).transpose(2, 1, 0, 3).reshape(32, 128, D))
    wk_lin = np.ascontiguousarray(w_k.reshape(32, 128, 8, 128).transpose(2, 1, 0, 3).reshape(8, 128, D))
    wv_lin = np.ascontiguousarray(w_v.reshape(32, 128, 1024))
    wo_lin = np.ascontiguousarray(w_o.reshape(32, 128, 8, 512).transpose(2, 0, 1, 3))
    wg_lin = np.ascontiguousarray(w_gate.reshape(NIT, 128, 32, 128).transpose(0, 3, 2, 1).reshape(NIT, 128, D))
    wu_lin = np.ascontiguousarray(w_up.reshape(NIT, 128, 32, 128).transpose(0, 3, 2, 1).reshape(NIT, 128, D))
    wd_lin = np.ascontiguousarray(np.ascontiguousarray(w_down.T).reshape(NIT, 128, D))

    consts = np.zeros((128, 4), np.float32)
    consts[:, 0] = 1.0 / D
    consts[:, 1] = EPS
    consts[:, 2] = -1.0
    consts[:, 3] = -SHIFT
    ones_row = np.ones((1, 128), np.float32)
    ones_col = np.ones((128, 1), np.float32)
    ident = np.eye(128, dtype=np.float32)

    inv_freq = (1.0 / (10000.0 ** (np.arange(0, HD, 2, dtype=np.float64) / HD)))

    def tables(pos):
        ang = pos[None, :].astype(np.float64) * inv_freq[:, None]
        cos = np.cos(ang).astype(np.float32)
        sin = np.sin(ang).astype(np.float32)
        cos_d = np.concatenate([cos, cos], axis=0)
        sin_d = np.concatenate([-sin, sin], axis=0)
        return np.ascontiguousarray(cos_d), np.ascontiguousarray(sin_d)

    cosk, sink = tables(np.arange(S))

    shared = dict(consts=consts, ones_row=ones_row, ones_col=ones_col, ident_in=ident,
                  cosk_in=cosk, sink_in=sink, wk_lin=wk_lin, wv_lin=wv_lin, wq_lin=wq_lin,
                  wo_lin=wo_lin, wg_lin=wg_lin, wu_lin=wu_lin, wd_lin=wd_lin)

    in_maps = []
    for core in range(8):
        b, c = divmod(core, 4)
        tl = tiles_for_core(c)
        rows = np.concatenate([np.arange(t * 128, (t + 1) * 128) for t in tl])
        x_own = np.ascontiguousarray(x[b][rows])
        cosq, sinq = tables(rows)
        masks = np.zeros((32, 128, 128), np.float32)
        kk = np.arange(128)[:, None]
        qq = np.arange(128)[None, :]
        for j in range(32):
            p = (31 - j) // 4
            t = tl[p]
            masks[j] = (128 * j + kk <= 128 * t + qq).astype(np.float32)
        m = dict(shared)
        m.update(x_kv=np.ascontiguousarray(x[b]), x_own=x_own, cosq_in=cosq,
                 sinq_in=sinq, masks_in=masks)
        in_maps.append(m)
    return in_maps


def kernel(L0_attn_x, L0_attn_w_qkv, L0_attn_w_o, L0_ffn_w_gate, L0_ffn_w_up, L0_ffn_w_down):
    in_maps = _host_inputs(L0_attn_x, L0_attn_w_qkv, L0_attn_w_o,
                           L0_ffn_w_gate, L0_ffn_w_up, L0_ffn_w_down)
    nc = build_nc()
    res = run_bass_kernel_spmd(nc, in_maps, core_ids=list(range(8)))
    y = np.empty((2, S, D), np.float32)
    for core in range(8):
        b, c = divmod(core, 4)
        tl = tiles_for_core(c)
        o = res.results[core]["out"]
        for p, t in enumerate(tl):
            y[b, t * 128:(t + 1) * 128, :] = o[p * 128:(p + 1) * 128, :]
    return y


# version 04
# speedup vs baseline: 1.1909x; 1.1769x over previous
import contextlib
import math

import numpy as np
import concourse.bass as bass
import concourse.mybir as mybir
from concourse.bass_utils import run_bass_kernel_spmd

F32 = mybir.dt.float32
BF16 = mybir.dt.bfloat16
AF = mybir.ActivationFunctionType
ALU = mybir.AluOpType
AX = mybir.AxisListType

D = 4096
S = 4096
NH = 32
NG = 8
HD = 128
NIT = 86
TOWN = 1024
SCALE = 1.0 / math.sqrt(HD)
SHIFT = 20.0
EPS = 1e-5
FFN_BLOCKS = [16, 16, 16, 16, 16, 6]


def tiles_for_core(c):
    out = []
    for p in range(8):
        out.append(((31 - c) - 8 * (p // 2)) if p % 2 == 0 else ((24 + c) - 8 * (p // 2)))
    return out


def wA_of(j):
    return 128 * min(4, -(-(32 - j) // 4))


def wB_of(j):
    return 128 * min(4, -(-(16 - j) // 4))


class Cnt:
    def __init__(self):
        self.v = {}

    def __getitem__(self, k):
        return self.v.get(k, 0)

    def add(self, k, n):
        self.v[k] = self.v.get(k, 0) + n
        return self.v[k]


def build_nc(dump=False, upto="C"):
    nc = bass.Bass("TRN2", target_bir_lowering=False, detect_race_conditions=False)
    scr_kind = "ExternalOutput" if dump else "Internal"

    x_kv = nc.dram_tensor("x_kv", [S, D], F32, kind="ExternalInput").ap()
    x_own = nc.dram_tensor("x_own", [TOWN, D], F32, kind="ExternalInput").ap()
    consts = nc.dram_tensor("consts", [128, 4], F32, kind="ExternalInput").ap()
    ones_row = nc.dram_tensor("ones_row", [1, 128], F32, kind="ExternalInput").ap()
    ones_col = nc.dram_tensor("ones_col", [128, 1], F32, kind="ExternalInput").ap()
    ident_in = nc.dram_tensor("ident_in", [128, 128], F32, kind="ExternalInput").ap()
    masks_in = nc.dram_tensor("masks_in", [32, 128, 128], F32, kind="ExternalInput").ap()
    cosk_in = nc.dram_tensor("cosk_in", [128, S], F32, kind="ExternalInput").ap()
    sink_in = nc.dram_tensor("sink_in", [128, S], F32, kind="ExternalInput").ap()
    cosq_in = nc.dram_tensor("cosq_in", [128, TOWN], F32, kind="ExternalInput").ap()
    sinq_in = nc.dram_tensor("sinq_in", [128, TOWN], F32, kind="ExternalInput").ap()
    wk_lin = nc.dram_tensor("wk_lin", [NG, 128, D], F32, kind="ExternalInput").ap()
    wv_lin = nc.dram_tensor("wv_lin", [32, 128, 1024], F32, kind="ExternalInput").ap()
    wq_lin = nc.dram_tensor("wq_lin", [32, 128, D], F32, kind="ExternalInput").ap()
    wo_lin = nc.dram_tensor("wo_lin", [8, 32, 128, 512], F32, kind="ExternalInput").ap()
    wg_lin = nc.dram_tensor("wg_lin", [NIT, 128, D], F32, kind="ExternalInput").ap()
    wu_lin = nc.dram_tensor("wu_lin", [NIT, 128, D], F32, kind="ExternalInput").ap()
    wd_lin = nc.dram_tensor("wd_lin", [NIT, 128, D], F32, kind="ExternalInput").ap()

    out = nc.dram_tensor("out", [TOWN, D], F32, kind="ExternalOutput").ap()

    wkb = nc.dram_tensor("wkb", [NG, 128, D], BF16, kind="Internal").ap()
    wvb = nc.dram_tensor("wvb", [32, 128, 1024], BF16, kind="Internal").ap()
    ktr_scr = nc.dram_tensor("ktr_scr", [NG, 128, S], BF16, kind=scr_kind).ap()
    v_scr = nc.dram_tensor("v_scr", [32, 128, 1024], BF16, kind=scr_kind).ap()
    qtr_scr = nc.dram_tensor("qtr_scr", [NH, 128, TOWN], BF16, kind=scr_kind).ap()
    xnt_scr = nc.dram_tensor("xnt_scr", [32, 128, TOWN], BF16, kind=scr_kind).ap()
    if dump:
        attn_dump = nc.dram_tensor("attn_dump", [128, NH * TOWN], BF16,
                                   kind="ExternalOutput").ap()

    cnt = Cnt()

    with contextlib.ExitStack() as TOP:
        E = TOP.enter_context
        sems = {}
        semnames = ["yP0", "gP0", "gD2D", "qtb", "xsp", "xrl", "sSQ", "vVAR", "sSD",
                    "vXN", "tTR", "sTE", "tK", "sKE", "vKR", "tVg", "vVE", "tQ",
                    "sQE", "vQR", "tST", "sE", "vMK", "tPV", "sRC", "tBC", "sRB",
                    "vEV", "tY", "vY", "tG", "tU", "sSIL", "vIN", "tD", "vDE"]
        for extra in ["swL2", "swL3", "swL4", "swL5", "swL6", "swL7", "wkL2", "wvL2"]:
            semnames.append(extra)
        for pair in ["xaL", "tbL", "wkL", "wvL", "swL", "kspL", "x2L", "wqL",
                     "qswL", "qspL", "kgl", "vgl", "qgl", "woL", "yYL", "wgL",
                     "wuL", "wdL", "gAL"]:
            semnames += [pair + "0", pair + "1"]
        semnames += ["vspL0", "vspL1", "vspL2", "vspL3"]
        semnames += ["xoL0", "xoL1", "xoL2", "xoL3"]
        for name in semnames:
            sems[name] = E(nc.semaphore(name))

        def W(eng, name, val):
            if val > 0:
                eng.wait_ge(sems[name], val)

        consts_sb = E(nc.sbuf_tensor("consts_sb", [128, 4], F32))
        onesr_sb = E(nc.sbuf_tensor("onesr_sb", [1, 128], F32))
        onesc_sb = E(nc.sbuf_tensor("onesc_sb", [128, 1], BF16))
        ident_sb = E(nc.sbuf_tensor("ident_sb", [128, 128], BF16))
        masks_sb = E(nc.sbuf_tensor("masks_sb", [128, 32 * 128], BF16))
        c_rd = consts_sb[:, 0:1]
        c_eps = consts_sb[:, 1:2]
        c_m1 = consts_sb[:, 2:3]
        c_shift = consts_sb[:, 3:4]

        psf = E(nc.psum_tensor("psf", [128, 3072], F32))
        pst = E(nc.psum_tensor("pst", [128, 2048], BF16))

        def bank(i, w=512, row0=0, rows=128):
            return psf[row0:row0 + rows, i * 512:i * 512 + w]

        def bank2(i, w=1024):
            return psf[:, i * 512:i * 512 + w]

        with nc.Block("p0") as blk:
            @blk.sync
            def _(y):
                y.dma_start(consts_sb[:, :], consts).then_inc(sems["yP0"], 16)
                y.dma_start(onesr_sb[:, :], ones_row).then_inc(sems["yP0"], 16)
                cnt.add("yP0", 32)
                y.wait_ge(sems["yP0"], 32)

            @blk.gpsimd
            def _(g):
                g.dma_start(onesc_sb[:, :], ones_col).then_inc(sems["gP0"], 16)
                g.dma_start(ident_sb[:, :], ident_in).then_inc(sems["gP0"], 16)
                g.dma_start(masks_sb.ap().rearrange("p (j q) -> p j q", q=128),
                            masks_in.transpose([1, 0, 2])).then_inc(sems["gP0"], 16)
                cnt.add("gP0", 48)
                g.dma_start(wkb, wk_lin).then_inc(sems["gD2D"], 16)
                g.dma_start(wvb, wv_lin).then_inc(sems["gD2D"], 16)
                cnt.add("gD2D", 32)
                g.wait_ge(sems["gP0"], 48)

        with contextlib.ExitStack() as ST:
            EA = ST.enter_context
            xa = [EA(nc.sbuf_tensor(f"xa{i}", [128, D], F32)) for i in range(2)]
            xcen = EA(nc.sbuf_tensor("xcen", [128, D], F32))
            xn = [EA(nc.sbuf_tensor(f"xn{i}", [128, D], BF16)) for i in range(2)]
            xnt = EA(nc.sbuf_tensor("xnt", [128, 32 * 512], BF16))
            bnso = EA(nc.sbuf_tensor("bnso", [128, 48], F32))
            t_mv = EA(nc.sbuf_tensor("t_mv", [128, 4], F32))
            t_sd = EA(nc.sbuf_tensor("t_sd", [128, 2], F32))
            t_r = EA(nc.sbuf_tensor("t_r", [128, 2], F32))
            ck = [EA(nc.sbuf_tensor(f"ck{i}", [128, 512], F32)) for i in range(2)]
            sk = [EA(nc.sbuf_tensor(f"sk{i}", [128, 512], F32)) for i in range(2)]
            wkS = [EA(nc.sbuf_tensor(f"wkS{i}", [128, D], BF16)) for i in range(3)]
            wvS = [EA(nc.sbuf_tensor(f"wvS{i}", [128, 4 * 1024], BF16)) for i in range(3)]
            kqf = [EA(nc.sbuf_tensor(f"kqf{i}", [128, 512], F32)) for i in range(8)]
            kswp = [EA(nc.sbuf_tensor(f"kswp{i}", [128, 512], F32)) for i in range(8)]
            ropet = EA(nc.sbuf_tensor("ropet", [128, 512], F32))
            ropet2 = EA(nc.sbuf_tensor("ropet2", [128, 512], F32))
            ktr_t = [EA(nc.sbuf_tensor(f"ktr_t{i}", [128, 512], BF16)) for i in range(2)]
            v_bf = [EA(nc.sbuf_tensor(f"v_bf{i}", [128, 1024], BF16)) for i in range(4)]

            def wk_lidx(c, g):
                return c * 8 + g

            def wv_lidx(c, h2, grp):
                return c * 16 + h2 * 8 + grp

            def gw_A(c, g=None, h2=None, grp=None):
                if g is not None:
                    return c * 24 + g + 1
                return c * 24 + 8 + h2 * 8 + grp + 1

            with nc.Block("pA") as blk:
                @blk.gpsimd
                def _(g):
                    g.wait_ge(sems["gD2D"], 32)
                    for c in range(8):
                        for gg in range(8):
                            kidx = wk_lidx(c, gg)
                            W(g, "tK", kidx - 2)
                            g.dma_start(wkS[kidx % 3][:, :], wkb[gg]).then_inc(sems[f"wkL{kidx % 3}"], 16)
                            cnt.add(f"wkL{kidx % 3}", 16)
                        for h2 in range(2):
                            for grp in range(8):
                                vl = wv_lidx(c, h2, grp)
                                W(g, "tVg", vl - 2)
                                g.dma_start(
                                    wvS[vl % 3].ap().rearrange("p (d q) -> p d q", q=1024),
                                    wvb[4 * grp:4 * grp + 4].transpose([1, 0, 2]),
                                ).then_inc(sems[f"wvL{vl % 3}"], 16)
                                cnt.add(f"wvL{vl % 3}", 16)

                @blk.sync
                def _(y):
                    y.wait_ge(sems["yP0"], 32)
                    for c in range(9):
                        if c < 8:
                            W(y, "vKR", 8 * (c - 1))
                            y.dma_start(ck[c % 2][:, :], cosk_in[:, c * 512:(c + 1) * 512]).then_inc(sems[f"tbL{c % 2}"], 16)
                            y.dma_start(sk[c % 2][:, :], sink_in[:, c * 512:(c + 1) * 512]).then_inc(sems[f"tbL{c % 2}"], 16)
                            cnt.add(f"tbL{c % 2}", 32)
                            for tt in range(4):
                                gt = 4 * c + tt
                                W(y, "vXN", gt - 1)
                                y.dma_start(xa[gt % 2][:, :], x_kv[gt * 128:(gt + 1) * 128, :]).then_inc(sems[f"xaL{gt % 2}"], 16)
                                cnt.add(f"xaL{gt % 2}", 16)
                        if c > 0:
                            cp = c - 1
                            for gg in range(8):
                                kidx = cp * 8 + gg
                                W(y, "sKE", kidx + 1)
                                W(y, "vKR", kidx - 7)
                                y.dma_start(kswp[kidx % 8][0:64, :], kqf[kidx % 8][64:128, :]).then_inc(sems[f"swL{kidx % 8}"], 16)
                                y.dma_start(kswp[kidx % 8][64:128, :], kqf[kidx % 8][0:64, :]).then_inc(sems[f"swL{kidx % 8}"], 16)
                                cnt.add(f"swL{kidx % 8}", 32)
                                W(y, "vKR", kidx + 1)
                                y.dma_start(ktr_scr[gg, :, cp * 512:(cp + 1) * 512], ktr_t[kidx % 2][:, :]).then_inc(sems[f"kspL{kidx % 2}"], 16)
                                cnt.add(f"kspL{kidx % 2}", 16)
                            for h2 in range(2):
                                for tp in range(2):
                                    half = cp * 2 + h2
                                    vidx = 2 * half + tp
                                    slot = (half % 2) * 2 + tp
                                    W(y, "vVE", vidx + 1)
                                    ttg = cp * 4 + 2 * h2 + tp
                                    y.dma_start(v_scr[ttg], v_bf[slot][:, :]).then_inc(sems[f"vspL{slot}"], 16)
                                    cnt.add(f"vspL{slot}", 16)
                    for nm in ["kspL0", "kspL1", "vspL0", "vspL1", "vspL2", "vspL3"]:
                        y.wait_ge(sems[nm], cnt[nm])

                @blk.vector
                def _(v):
                    v.wait_ge(sems["yP0"], 32)
                    for c in range(9):
                        if c < 8:
                            for tt in range(4):
                                gt = 4 * c + tt
                                par = gt % 2
                                W(v, f"xaL{par}", 16 * (gt // 2 + 1))
                                W(v, "sSD", gt - 1)
                                for sg in range(8):
                                    v.bn_stats(bnso[:, sg * 6:(sg + 1) * 6], xa[par][:, sg * 512:(sg + 1) * 512])
                                v.bn_aggr(t_mv[:, par * 2:par * 2 + 2], bnso[:, :]).then_inc(sems["vVAR"], 1)
                                cnt.add("vVAR", 1)
                                W(v, "sSD", gt + 1)
                                v.reciprocal(t_r[:, par:par + 1], t_sd[:, par:par + 1])
                                v.tensor_scalar(xcen[:, :], xa[par][:, :], t_mv[:, par * 2:par * 2 + 1],
                                                None, ALU.subtract)
                                W(v, "tTR", 4 * (gt - 1))
                                v.tensor_scalar(xn[par][:, :], xcen[:, :], t_r[:, par:par + 1],
                                                None, ALU.mult).then_inc(sems["vXN"], 1)
                                cnt.add("vXN", 1)
                        if c > 0:
                            cp = c - 1
                            for gg in range(8):
                                kidx = cp * 8 + gg
                                W(v, "sKE", kidx + 1)
                                W(v, f"tbL{cp % 2}", 32 * (cp // 2 + 1))
                                v.tensor_mul(ropet[:, :], kqf[kidx % 8][:, :], ck[cp % 2][:, :])
                                W(v, f"swL{kidx % 8}", 32 * (kidx // 8 + 1))
                                v.tensor_mul(ropet2[:, :], kswp[kidx % 8][:, :], sk[cp % 2][:, :])
                                W(v, f"kspL{kidx % 2}", 16 * (kidx // 2))
                                v.tensor_add(ktr_t[kidx % 2][:, :], ropet[:, :], ropet2[:, :]).then_inc(sems["vKR"], 1)
                                cnt.add("vKR", 1)

                @blk.scalar
                def _(sc):
                    sc.wait_ge(sems["yP0"], 32)
                    for c in range(9):
                        if c < 8:
                            for tt in range(4):
                                gt = 4 * c + tt
                                par = gt % 2
                                W(sc, "vVAR", gt + 1)
                                W(sc, "vXN", gt - 1)
                                sc.activation(t_sd[:, par:par + 1], t_mv[:, par * 2 + 1:par * 2 + 2],
                                              AF.Sqrt, bias=c_eps).then_inc(sems["sSD"], 1)
                                cnt.add("sSD", 1)
                                for q in range(4):
                                    idx = 4 * gt + q
                                    W(sc, "tTR", idx + 1)
                                    W(sc, "tVg", 16 * c)
                                    dst = xnt.ap().rearrange("p (d t) -> p d t", t=512)[:, 8 * q:8 * q + 8, tt * 128:(tt + 1) * 128]
                                    src = pst.ap().rearrange("p (d t) -> p d t", t=128)[:, 8 * (idx % 2):8 * (idx % 2) + 8, :]
                                    sc.copy(dst, src).then_inc(sems["sTE"], 1)
                                    cnt.add("sTE", 1)
                            for gg in range(8):
                                kidx = c * 8 + gg
                                W(sc, "tK", kidx + 1)
                                W(sc, "vKR", kidx - 7)
                                W(sc, f"swL{kidx % 8}", 32 * (kidx // 8))
                                sc.copy(kqf[kidx % 8][:, :], bank(gg % 2)).then_inc(sems["sKE"], 1)
                                cnt.add("sKE", 1)
                            for h2 in range(2):
                                for tp in range(2):
                                    half = c * 2 + h2
                                    slot = (half % 2) * 2 + tp
                                    W(sc, "tVg", 8 * (half + 1))
                                    W(sc, f"vspL{slot}", 16 * (half // 2))
                                    sc.copy(v_bf[slot][:, :], bank2(2 + 2 * tp)).then_inc(sems["vVE"], 1)
                                    cnt.add("vVE", 1)

                @blk.tensor
                def _(t):
                    t.wait_ge(sems["gP0"], 48)
                    for c in range(8):
                        for tt in range(4):
                            gt = 4 * c + tt
                            W(t, "vXN", gt + 1)
                            for q in range(4):
                                idx = 4 * gt + q
                                W(t, "sTE", idx - 1)
                                for k in range(8):
                                    dt = 8 * q + k
                                    ins = t.transpose(pst[:, (idx % 2) * 1024 + k * 128:(idx % 2) * 1024 + (k + 1) * 128],
                                                      xn[gt % 2][:, dt * 128:(dt + 1) * 128],
                                                      ident_sb[:, :])
                                ins.then_inc(sems["tTR"], 1)
                                cnt.add("tTR", 1)
                        W(t, "sTE", 16 * (c + 1))
                        for gg in range(8):
                            kidx = c * 8 + gg
                            W(t, f"wkL{kidx % 3}", 16 * (kidx // 3 + 1))
                            W(t, "sKE", kidx - 1)
                            for dt in range(32):
                                ins = t.matmul(bank(gg % 2), wkS[kidx % 3][:, dt * 128:(dt + 1) * 128],
                                               xnt[:, dt * 512:(dt + 1) * 512],
                                               start=(dt == 0), stop=(dt == 31))
                            ins.then_inc(sems["tK"], 1)
                            cnt.add("tK", 1)
                        for h2 in range(2):
                            half = c * 2 + h2
                            W(t, "vVE", 2 * half)
                            for dt in range(32):
                                vl = wv_lidx(c, h2, dt // 4)
                                if dt % 4 == 0:
                                    W(t, f"wvL{vl % 3}", 16 * (vl // 3 + 1))
                                for tp in range(2):
                                    ttl = 2 * h2 + tp
                                    for colh in range(2):
                                        ins = t.matmul(bank(2 + 2 * tp + colh),
                                                       xnt[:, dt * 512 + ttl * 128:dt * 512 + (ttl + 1) * 128],
                                                       wvS[vl % 3][:, (dt % 4) * 1024 + colh * 512:(dt % 4) * 1024 + (colh + 1) * 512],
                                                       start=(dt == 0), stop=(dt == 31))
                                if dt % 4 == 3:
                                    ins.then_inc(sems["tVg"], 1)
                                    cnt.add("tVg", 1)

        with contextlib.ExitStack() as ST:
            EA = ST.enter_context
            xa = [EA(nc.sbuf_tensor(f"x2a{i}", [128, D], F32)) for i in range(2)]
            xcen = EA(nc.sbuf_tensor("x2cen", [128, D], F32))
            xn = [EA(nc.sbuf_tensor(f"x2n{i}", [128, D], BF16)) for i in range(2)]
            xnt = EA(nc.sbuf_tensor("xnt_own", [128, 32 * 1024], BF16))
            bnso = EA(nc.sbuf_tensor("bnso2", [128, 48], F32))
            t_mv = EA(nc.sbuf_tensor("t2_mv", [128, 4], F32))
            t_sd = EA(nc.sbuf_tensor("t2_sd", [128, 2], F32))
            t_r = EA(nc.sbuf_tensor("t2_r", [128, 2], F32))
            cq = EA(nc.sbuf_tensor("cq", [128, TOWN], F32))
            sq = EA(nc.sbuf_tensor("sq", [128, TOWN], F32))
            wqS = [EA(nc.sbuf_tensor(f"wqS{i}", [128, D], BF16)) for i in range(2)]
            qf = [EA(nc.sbuf_tensor(f"qf{i}", [128, TOWN], F32)) for i in range(2)]
            qswp = [EA(nc.sbuf_tensor(f"qswp{i}", [128, TOWN], F32)) for i in range(2)]
            ropet = EA(nc.sbuf_tensor("ropetq", [128, TOWN], F32))
            ropet2 = EA(nc.sbuf_tensor("ropetq2", [128, TOWN], F32))
            qtr_t = [EA(nc.sbuf_tensor(f"qtr_t{i}", [128, TOWN], BF16)) for i in range(2)]

            base = {k: cnt[k] for k in ["sSQ", "vVAR", "sSD", "vXN", "tTR", "sTE"]}

            with nc.Block("pA2") as blk:
                @blk.gpsimd
                def _(g):
                    for h in range(NH):
                        W(g, "tQ", h - 1)
                        g.dma_start(wqS[h % 2][:, :], wq_lin[h]).then_inc(sems[f"wqL{h % 2}"], 16)
                        cnt.add(f"wqL{h % 2}", 16)

                @blk.sync
                def _(y):
                    y.dma_start(cq[:, :], cosq_in).then_inc(sems["qtb"], 16)
                    y.dma_start(sq[:, :], sinq_in).then_inc(sems["qtb"], 16)
                    cnt.add("qtb", 32)
                    for gt in range(8):
                        W(y, "vXN", base["vXN"] + gt - 1)
                        y.dma_start(xa[gt % 2][:, :], x_own[gt * 128:(gt + 1) * 128, :]).then_inc(sems[f"x2L{gt % 2}"], 16)
                        cnt.add(f"x2L{gt % 2}", 16)
                    for h in range(NH):
                        W(y, "sQE", h + 1)
                        y.dma_start(qswp[h % 2][0:64, :], qf[h % 2][64:128, :]).then_inc(sems[f"qswL{h % 2}"], 16)
                        y.dma_start(qswp[h % 2][64:128, :], qf[h % 2][0:64, :]).then_inc(sems[f"qswL{h % 2}"], 16)
                        cnt.add(f"qswL{h % 2}", 32)
                        W(y, "vQR", h + 1)
                        y.dma_start(qtr_scr[h], qtr_t[h % 2][:, :]).then_inc(sems[f"qspL{h % 2}"], 16)
                        cnt.add(f"qspL{h % 2}", 16)
                    W(y, "sTE", base["sTE"] + 32)
                    y.dma_start(xnt_scr.transpose([1, 0, 2]),
                                xnt.ap().rearrange("p (d t) -> p d t", t=1024)).then_inc(sems["xsp"], 16)
                    cnt.add("xsp", 16)
                    y.wait_ge(sems["qspL0"], cnt["qspL0"])
                    y.wait_ge(sems["qspL1"], cnt["qspL1"])
                    y.wait_ge(sems["xsp"], cnt["xsp"])

                @blk.vector
                def _(v):
                    for gt in range(8):
                        par = gt % 2
                        v.wait_ge(sems[f"x2L{par}"], 16 * (gt // 2 + 1))
                        W(v, "sSD", base["sSD"] + gt - 1)
                        for sg in range(8):
                            v.bn_stats(bnso[:, sg * 6:(sg + 1) * 6], xa[par][:, sg * 512:(sg + 1) * 512])
                        v.bn_aggr(t_mv[:, par * 2:par * 2 + 2], bnso[:, :]).then_inc(sems["vVAR"], 1)
                        cnt.add("vVAR", 1)
                        W(v, "sSD", base["sSD"] + gt + 1)
                        v.reciprocal(t_r[:, par:par + 1], t_sd[:, par:par + 1])
                        v.tensor_scalar(xcen[:, :], xa[par][:, :], t_mv[:, par * 2:par * 2 + 1],
                                        None, ALU.subtract)
                        W(v, "tTR", base["tTR"] + 4 * (gt - 1))
                        v.tensor_scalar(xn[par][:, :], xcen[:, :], t_r[:, par:par + 1],
                                        None, ALU.mult).then_inc(sems["vXN"], 1)
                        cnt.add("vXN", 1)
                    for h in range(NH):
                        W(v, "sQE", h + 1)
                        W(v, "qtb", 32)
                        v.tensor_mul(ropet[:, :], qf[h % 2][:, :], cq[:, :])
                        W(v, f"qswL{h % 2}", 32 * (h // 2 + 1))
                        v.tensor_mul(ropet2[:, :], qswp[h % 2][:, :], sq[:, :])
                        W(v, f"qspL{h % 2}", 16 * (h // 2))
                        v.tensor_add(qtr_t[h % 2][:, :], ropet[:, :], ropet2[:, :]).then_inc(sems["vQR"], 1)
                        cnt.add("vQR", 1)

                @blk.scalar
                def _(sc):
                    for gt in range(8):
                        par = gt % 2
                        W(sc, "vVAR", base["vVAR"] + gt + 1)
                        W(sc, "vXN", base["vXN"] + gt - 1)
                        sc.activation(t_sd[:, par:par + 1], t_mv[:, par * 2 + 1:par * 2 + 2],
                                      AF.Sqrt, bias=c_eps).then_inc(sems["sSD"], 1)
                        cnt.add("sSD", 1)
                        for q in range(4):
                            idx = 4 * gt + q
                            W(sc, "tTR", base["tTR"] + idx + 1)
                            dst = xnt.ap().rearrange("p (d t) -> p d t", t=1024)[:, 8 * q:8 * q + 8, gt * 128:(gt + 1) * 128]
                            src = pst.ap().rearrange("p (d t) -> p d t", t=128)[:, 8 * (idx % 2):8 * (idx % 2) + 8, :]
                            sc.copy(dst, src).then_inc(sems["sTE"], 1)
                            cnt.add("sTE", 1)
                    for h in range(NH):
                        W(sc, "tQ", h + 1)
                        W(sc, "vQR", h - 1)
                        W(sc, f"qswL{h % 2}", 32 * (h // 2))
                        sc.copy(qf[h % 2][:, :], bank2((h % 2) * 2)).then_inc(sems["sQE"], 1)
                        cnt.add("sQE", 1)

                @blk.tensor
                def _(t):
                    for gt in range(8):
                        W(t, "vXN", base["vXN"] + gt + 1)
                        for q in range(4):
                            idx = 4 * gt + q
                            W(t, "sTE", base["sTE"] + idx - 1)
                            for k in range(8):
                                dt = 8 * q + k
                                ins = t.transpose(pst[:, (idx % 2) * 1024 + k * 128:(idx % 2) * 1024 + (k + 1) * 128],
                                                  xn[gt % 2][:, dt * 128:(dt + 1) * 128],
                                                  ident_sb[:, :])
                            ins.then_inc(sems["tTR"], 1)
                            cnt.add("tTR", 1)
                    W(t, "sTE", base["sTE"] + 32)
                    for h in range(NH):
                        W(t, f"wqL{h % 2}", 16 * (h // 2 + 1))
                        W(t, "sQE", h - 1)
                        for dt in range(32):
                            for colh in range(2):
                                ins = t.matmul(bank((h % 2) * 2 + colh),
                                               wqS[h % 2][:, dt * 128:(dt + 1) * 128],
                                               xnt[:, dt * 1024 + colh * 512:dt * 1024 + (colh + 1) * 512],
                                               start=(dt == 0), stop=(dt == 31))
                        ins.then_inc(sems["tQ"], 1)
                        cnt.add("tQ", 1)

        if upto == "A":
            return nc

        with contextlib.ExitStack() as STB:
            EB = STB.enter_context
            attn = EB(nc.sbuf_tensor("attn", [128, NH * TOWN], BF16))
            with contextlib.ExitStack() as ST:
                EA = ST.enter_context
                ktr_g = [EA(nc.sbuf_tensor(f"ktr_g{i}", [128, S], BF16)) for i in range(2)]
                v_g = [EA(nc.sbuf_tensor(f"v_g{i}", [128, S], BF16)) for i in range(2)]
                qtr = [EA(nc.sbuf_tensor(f"qtr{i}", [128, TOWN], BF16)) for i in range(2)]
                ptb = [EA(nc.sbuf_tensor(f"ptb{i}", [128, 512], BF16)) for i in range(4)]
                lnt = EA(nc.sbuf_tensor("lnt", [1, 512], F32))
                rcp = EA(nc.sbuf_tensor("rcp", [1, 512], F32))
                rb = EA(nc.sbuf_tensor("rb", [128, 512], F32))

                PASS = []
                for h in range(NH):
                    PASS.append((h, 0, 32))
                    PASS.append((h, 1, 16))
                e_of_pass = [0]
                for (_, _, nj) in PASS:
                    e_of_pass.append(e_of_pass[-1] + nj)

                def o_bank(pl):
                    return 2 if pl == 0 else 4

                def s_bank(pl):
                    return 3 if pl == 0 else 5

                def wof(pl, j):
                    return wA_of(j) if pl == 0 else wB_of(j)

                def masked(pl, j):
                    return (j >= 16) if pl == 0 else True

                def stbank(pl, j):
                    return (0, 1, 4)[j % 3] if pl == 0 else (0, 1, 2)[j % 3]

                def mask_count_upto(p, j):
                    n = 16 * p
                    h, pl, nj = PASS[p]
                    for jj in range(j + 1):
                        if masked(pl, jj):
                            n += 1
                    return n

                bSE = cnt["sE"]; bTST = cnt["tST"]; bTPV = cnt["tPV"]; bVMK = cnt["vMK"]
                bSRC = cnt["sRC"]; bTBC = cnt["tBC"]; bSRB = cnt["sRB"]; bVEV = cnt["vEV"]

                with nc.Block("pB") as blk:
                    @blk.sync
                    def _(y):
                        for gg in range(NG):
                            if gg >= 2:
                                W(y, "tPV", bTPV + 192 * (gg - 1))
                            y.dma_start(ktr_g[gg % 2][:, :], ktr_scr[gg]).then_inc(sems[f"kgl{gg % 2}"], 16)
                            cnt.add(f"kgl{gg % 2}", 16)
                            y.dma_start(v_g[gg % 2].ap().rearrange("p (j q) -> p j q", q=128),
                                        v_scr[:, :, gg * 128:(gg + 1) * 128].transpose([1, 0, 2])).then_inc(sems[f"vgl{gg % 2}"], 16)
                            cnt.add(f"vgl{gg % 2}", 16)
                            for hq in range(4):
                                h = gg * 4 + hq
                                if h >= 2:
                                    W(y, "tST", bTST + e_of_pass[2 * (h - 1)])
                                y.dma_start(qtr[h % 2][:, :], qtr_scr[h]).then_inc(sems[f"qgl{h % 2}"], 16)
                                cnt.add(f"qgl{h % 2}", 16)

                    @blk.scalar
                    def _(sc):
                        for p, (h, pl, nj) in enumerate(PASS):
                            for j in range(nj):
                                e = e_of_pass[p] + j
                                w = wof(pl, j)
                                W(sc, "tST", bTST + e + 1)
                                W(sc, "tPV", bTPV + e - 3)
                                sc.activation(ptb[e % 4][:, 0:w], bank(stbank(pl, j), w), AF.Exp,
                                              bias=c_shift, scale=SCALE).then_inc(sems["sE"], 1)
                                cnt.add("sE", 1)
                            W(sc, "tPV", bTPV + e_of_pass[p + 1])
                            sc.activation(lnt[:, :], bank(s_bank(pl), 512, 0, 1), AF.Ln)
                            sc.activation(rcp[:, :], lnt[:, :], AF.Exp, scale=-1.0).then_inc(sems["sRC"], 1)
                            cnt.add("sRC", 1)
                            W(sc, "tBC", bTBC + p + 1)
                            W(sc, "vEV", bVEV + p)
                            sc.copy(rb[:, :], bank(s_bank(pl))).then_inc(sems["sRB"], 1)
                            cnt.add("sRB", 1)

                    @blk.vector
                    def _(v):
                        v.wait_ge(sems["gP0"], 48)
                        for p, (h, pl, nj) in enumerate(PASS):
                            for j in range(nj):
                                if not masked(pl, j):
                                    continue
                                e = e_of_pass[p] + j
                                w = wof(pl, j)
                                W(v, "sE", bSE + e + 1)
                                v.tensor_mul(ptb[e % 4][:, w - 128:w], ptb[e % 4][:, w - 128:w],
                                             masks_sb[:, j * 128:(j + 1) * 128]).then_inc(sems["vMK"], 1)
                                cnt.add("vMK", 1)
                            W(v, "sRB", bSRB + p + 1)
                            v.tensor_mul(attn[:, h * 1024 + pl * 512:h * 1024 + (pl + 1) * 512],
                                         bank2(o_bank(pl), 512), rb[:, :]).then_inc(sems["vEV"], 1)
                            cnt.add("vEV", 1)

                    @blk.tensor
                    def _(t):
                        t.wait_ge(sems["gP0"], 48)
                        ledger = {}

                        def bank_wait(b):
                            if b in ledger:
                                W(t, ledger[b][0], ledger[b][1])

                        def emit_ST(p, j):
                            h, pl, nj = PASS[p]
                            gg = h // 4
                            e = e_of_pass[p] + j
                            w = wof(pl, j)
                            b = stbank(pl, j)
                            bank_wait(b)
                            ledger[b] = ("sE", bSE + e + 1)
                            if j == 0:
                                W(t, f"kgl{gg % 2}", 16 * (gg // 2 + 1))
                                W(t, f"qgl{h % 2}", 16 * (h // 2 + 1))
                            t.matmul(bank(b, w), ktr_g[gg % 2][:, j * 128:(j + 1) * 128],
                                     qtr[h % 2][:, pl * 512:pl * 512 + w],
                                     start=True, stop=True).then_inc(sems["tST"], 1)
                            cnt.add("tST", 1)

                        def emit_PV(p, j):
                            h, pl, nj = PASS[p]
                            gg = h // 4
                            e = e_of_pass[p] + j
                            w = wof(pl, j)
                            W(t, "sE", bSE + e + 1)
                            if masked(pl, j):
                                W(t, "vMK", bVMK + mask_count_upto(p, j))
                            if j == 0:
                                bank_wait(o_bank(pl))
                                bank_wait(s_bank(pl))
                                ledger[o_bank(pl)] = ("vEV", bVEV + p + 1)
                                ledger[s_bank(pl)] = ("sRB", bSRB + p + 1)
                                W(t, f"vgl{gg % 2}", 16 * (gg // 2 + 1))
                            t.matmul(bank2(o_bank(pl), w), v_g[gg % 2][:, j * 128:(j + 1) * 128],
                                     ptb[e % 4][:, 0:w], start=(j == 0), stop=(j == nj - 1))
                            t.matmul(bank(s_bank(pl), w, 0, 1), onesc_sb[:, :],
                                     ptb[e % 4][:, 0:w],
                                     start=(j == 0), stop=(j == nj - 1)).then_inc(sems["tPV"], 1)
                            cnt.add("tPV", 1)

                        def emit_BC(p):
                            h, pl, nj = PASS[p]
                            W(t, "sRC", bSRC + p + 1)
                            t.matmul(bank(s_bank(pl)), onesr_sb[:, :], rcp[:, :],
                                     start=True, stop=True).then_inc(sems["tBC"], 1)
                            cnt.add("tBC", 1)

                        LAG = 3
                        pend_bc = None
                        for p in range(len(PASS)):
                            h, pl, nj = PASS[p]
                            for j in range(nj):
                                emit_ST(p, j)
                                if j == 0 and pend_bc is not None:
                                    emit_BC(pend_bc)
                                    pend_bc = None
                                if j >= LAG:
                                    emit_PV(p, j - LAG)
                            for j in range(nj - LAG, nj):
                                emit_PV(p, j)
                            pend_bc = p
                        emit_BC(pend_bc)

                    if dump:
                        @blk.gpsimd
                        def _(g):
                            g.wait_ge(sems["vEV"], bVEV + 64)
                            g.dma_start(attn_dump, attn[:, :]).then_inc(sems["gAL0"], 16)
                            cnt.add("gAL0", 16)
                            g.wait_ge(sems["gAL0"], cnt["gAL0"])

            if upto == "B":
                return nc

            with contextlib.ExitStack() as ST:
                EA = ST.enter_context
                woS = [EA(nc.sbuf_tensor(f"woS{i}", [128, 32 * 512], BF16)) for i in range(2)]
                xo = [EA(nc.sbuf_tensor(f"xo{i}", [128, 512], F32)) for i in range(4)]
                yf = [EA(nc.sbuf_tensor(f"yf{i}", [128, 512], F32)) for i in range(2)]
                bTY = cnt["tY"]; bVY = cnt["vY"]

                with nc.Block("pC1") as blk:
                    @blk.gpsimd
                    def _(g):
                        for dc in range(8):
                            W(g, "tY", bTY + 8 * (dc - 1))
                            g.dma_start(woS[dc % 2].ap().rearrange("p (a q) -> p a q", q=512),
                                        wo_lin[dc].transpose([1, 0, 2])).then_inc(sems[f"woL{dc % 2}"], 16)
                            cnt.add(f"woL{dc % 2}", 16)

                    @blk.sync
                    def _(y):
                        for dc in range(8):
                            for tt in range(8):
                                qi = dc * 8 + tt
                                W(y, "vY", bVY + qi - 3)
                                y.dma_start(xo[qi % 4][:, :], x_own[tt * 128:(tt + 1) * 128, dc * 512:(dc + 1) * 512]).then_inc(sems[f"xoL{qi % 4}"], 16)
                                cnt.add(f"xoL{qi % 4}", 16)
                                W(y, "vY", bVY + qi + 1)
                                y.dma_start(out[tt * 128:(tt + 1) * 128, dc * 512:(dc + 1) * 512], yf[qi % 2][:, :]).then_inc(sems[f"yYL{qi % 2}"], 16)
                                cnt.add(f"yYL{qi % 2}", 16)
                        y.wait_ge(sems["yYL0"], cnt["yYL0"])
                        y.wait_ge(sems["yYL1"], cnt["yYL1"])

                    @blk.tensor
                    def _(t):
                        for dc in range(8):
                            W(t, f"woL{dc % 2}", 16 * (dc // 2 + 1))
                            for tt in range(8):
                                qi = dc * 8 + tt
                                W(t, "vY", bVY + qi - 1)
                                for at in range(32):
                                    ins = t.matmul(bank(qi % 2), attn[:, at * 1024 + tt * 128:at * 1024 + (tt + 1) * 128],
                                                   woS[dc % 2][:, at * 512:(at + 1) * 512],
                                                   start=(at == 0), stop=(at == 31))
                                ins.then_inc(sems["tY"], 1)
                                cnt.add("tY", 1)

                    @blk.vector
                    def _(v):
                        for dc in range(8):
                            for tt in range(8):
                                qi = dc * 8 + tt
                                W(v, "tY", bTY + qi + 1)
                                v.wait_ge(sems[f"xoL{qi % 4}"], 16 * (qi // 4 + 1))
                                W(v, f"yYL{qi % 2}", 16 * (qi // 2))
                                v.tensor_add(yf[qi % 2][:, :], bank(qi % 2), xo[qi % 4][:, :]).then_inc(sems["vY"], 1)
                                cnt.add("vY", 1)

        with contextlib.ExitStack() as ST:
            EA = ST.enter_context
            xnt = EA(nc.sbuf_tensor("xnt_c2", [128, 32 * 1024], BF16))
            inter = EA(nc.sbuf_tensor("inter", [128, 16 * 1024], BF16))
            wgS = [EA(nc.sbuf_tensor(f"wgS{i}", [128, D], BF16)) for i in range(2)]
            wuS = [EA(nc.sbuf_tensor(f"wuS{i}", [128, D], BF16)) for i in range(2)]
            wdS = [EA(nc.sbuf_tensor(f"wdS{i}", [128, 16 * 512], BF16)) for i in range(2)]
            gsil = [EA(nc.sbuf_tensor(f"gsil{i}", [128, 1024], BF16)) for i in range(2)]
            ydf = [EA(nc.sbuf_tensor(f"ydf{i}", [128, 512], F32)) for i in range(2)]
            bYY0 = cnt["yYL0"]; bYY1 = cnt["yYL1"]
            bGA0 = cnt["gAL0"]; bGA1 = cnt["gAL1"]
            bTG = cnt["tG"]; bTU = cnt["tU"]; bSIL = cnt["sSIL"]; bVIN = cnt["vIN"]
            bTD = cnt["tD"]; bVDE = cnt["vDE"]

            it0_of = [0]
            for nb in FFN_BLOCKS:
                it0_of.append(it0_of[-1] + nb)
            NB = len(FFN_BLOCKS)

            with nc.Block("pC2") as blk:
                def wgu_loads(g, b, lo=0, hi=None):
                    for li in range(lo, FFN_BLOCKS[b] if hi is None else hi):
                        gi = it0_of[b] + li
                        W(g, "sSIL", bSIL + gi - 1)
                        g.dma_start(wgS[gi % 2][:, :], wg_lin[gi]).then_inc(sems[f"wgL{gi % 2}"], 16)
                        cnt.add(f"wgL{gi % 2}", 16)
                        W(g, "vIN", bVIN + gi - 1)
                        g.dma_start(wuS[gi % 2][:, :], wu_lin[gi]).then_inc(sems[f"wuL{gi % 2}"], 16)
                        cnt.add(f"wuL{gi % 2}", 16)

                def emit_accums(g, b, dcg):
                    if b == 0 and dcg == 0:
                        g.wait_ge(sems["yYL0"], bYY0)
                        g.wait_ge(sems["yYL1"], bYY1)
                    for k in range(8):
                        r = dcg * 8 + k
                        dc, tt = divmod(r, 8)
                        W(g, "vDE", bVDE + 64 * b + r + 1)
                        g.dma_start(out[tt * 128:(tt + 1) * 128, dc * 512:(dc + 1) * 512],
                                    ydf[r % 2][:, :], accum_op=ALU.add).then_inc(sems[f"gAL{r % 2}"], 16)
                        cnt.add(f"gAL{r % 2}", 16)

                @blk.gpsimd
                def _(g):
                    wgu_loads(g, 0)
                    for b, nb in enumerate(FFN_BLOCKS):
                        for dc in range(8):
                            if dc >= 2:
                                emit_accums(g, b, dc - 2)
                            W(g, "tD", bTD + 64 * b + 8 * (dc - 1))
                            g.dma_start(wdS[dc % 2].ap().rearrange("p (i q) -> p i q", q=512)[:, 0:nb, :],
                                        wd_lin[it0_of[b]:it0_of[b] + nb, :, dc * 512:(dc + 1) * 512].transpose([1, 0, 2])).then_inc(sems[f"wdL{dc % 2}"], 16)
                            cnt.add(f"wdL{dc % 2}", 16)
                            if dc == 3 and b + 1 < NB:
                                wgu_loads(g, b + 1, 0, 2)
                        emit_accums(g, b, 6)
                        emit_accums(g, b, 7)
                        if b + 1 < NB:
                            wgu_loads(g, b + 1, 2)
                    g.wait_ge(sems["gAL0"], cnt["gAL0"])
                    g.wait_ge(sems["gAL1"], cnt["gAL1"])

                @blk.sync
                def _(y):
                    y.dma_start(xnt.ap().rearrange("p (d t) -> p d t", t=1024),
                                xnt_scr.transpose([1, 0, 2])).then_inc(sems["xrl"], 16)
                    cnt.add("xrl", 16)
                    y.wait_ge(sems["gAL0"], cnt["gAL0"])
                    y.wait_ge(sems["gAL1"], cnt["gAL1"])

                @blk.tensor
                def _(t):
                    W(t, "xrl", 16)
                    for b, nb in enumerate(FFN_BLOCKS):
                        for li in range(nb):
                            gi = it0_of[b] + li
                            gbk = (gi % 2) * 2
                            W(t, f"wgL{gi % 2}", 16 * (gi // 2 + 1))
                            W(t, "sSIL", bSIL + gi - 1)
                            if li == 0 and b > 0:
                                W(t, "vDE", bVDE + 64 * b)
                            for dt in range(32):
                                for colh in range(2):
                                    ins = t.matmul(bank(gbk + colh), wgS[gi % 2][:, dt * 128:(dt + 1) * 128],
                                                   xnt[:, dt * 1024 + colh * 512:dt * 1024 + (colh + 1) * 512],
                                                   start=(dt == 0), stop=(dt == 31))
                            ins.then_inc(sems["tG"], 1)
                            cnt.add("tG", 1)
                            W(t, f"wuL{gi % 2}", 16 * (gi // 2 + 1))
                            W(t, "vIN", bVIN + gi)
                            for dt in range(32):
                                for colh in range(2):
                                    ins = t.matmul(bank(4 + colh), wuS[gi % 2][:, dt * 128:(dt + 1) * 128],
                                                   xnt[:, dt * 1024 + colh * 512:dt * 1024 + (colh + 1) * 512],
                                                   start=(dt == 0), stop=(dt == 31))
                            ins.then_inc(sems["tU"], 1)
                            cnt.add("tU", 1)
                        W(t, "vIN", bVIN + it0_of[b] + nb)
                        for r in range(64):
                            dc, tt = divmod(r, 8)
                            if tt == 0:
                                W(t, f"wdL{dc % 2}", 16 * (4 * b + dc // 2 + 1))
                            W(t, "vDE", bVDE + 64 * b + r - 1)
                            if r == 0:
                                W(t, "sSIL", bSIL + it0_of[b] + nb)
                            for li in range(nb):
                                ins = t.matmul(bank(r % 2), inter[:, li * 1024 + tt * 128:li * 1024 + (tt + 1) * 128],
                                               wdS[dc % 2][:, li * 512:(li + 1) * 512],
                                               start=(li == 0), stop=(li == nb - 1))
                            ins.then_inc(sems["tD"], 1)
                            cnt.add("tD", 1)

                @blk.scalar
                def _(sc):
                    for b, nb in enumerate(FFN_BLOCKS):
                        for li in range(nb):
                            gi = it0_of[b] + li
                            W(sc, "tG", bTG + gi + 1)
                            W(sc, "vIN", bVIN + gi - 1)
                            sc.activation(gsil[gi % 2][:, :], bank2((gi % 2) * 2), AF.Silu).then_inc(sems["sSIL"], 1)
                            cnt.add("sSIL", 1)

                @blk.vector
                def _(v):
                    for b, nb in enumerate(FFN_BLOCKS):
                        for li in range(nb):
                            gi = it0_of[b] + li
                            W(v, "tU", bTU + gi + 1)
                            W(v, "sSIL", bSIL + gi + 1)
                            v.tensor_mul(inter[:, li * 1024:(li + 1) * 1024], bank2(4),
                                         gsil[gi % 2][:, :]).then_inc(sems["vIN"], 1)
                            cnt.add("vIN", 1)
                        for r in range(64):
                            W(v, "tD", bTD + 64 * b + r + 1)
                            W(v, f"gAL{r % 2}", (bGA0 if r % 2 == 0 else bGA1) + 16 * (32 * b + r // 2))
                            v.tensor_copy(ydf[r % 2][:, :], bank(r % 2)).then_inc(sems["vDE"], 1)
                            cnt.add("vDE", 1)

    return nc



def _host_inputs(L0_attn_x, L0_attn_w_qkv, L0_attn_w_o, L0_ffn_w_gate, L0_ffn_w_up, L0_ffn_w_down):
    x = np.asarray(L0_attn_x, dtype=np.float32)
    w_qkv = np.asarray(L0_attn_w_qkv, dtype=np.float32)
    w_o = np.asarray(L0_attn_w_o, dtype=np.float32)
    w_gate = np.asarray(L0_ffn_w_gate, dtype=np.float32)
    w_up = np.asarray(L0_ffn_w_up, dtype=np.float32)
    w_down = np.asarray(L0_ffn_w_down, dtype=np.float32)

    perm = np.concatenate([np.arange(0, 128, 2), np.arange(1, 128, 2)])
    w_q = np.ascontiguousarray(w_qkv[:, :4096].reshape(D, 32, 128)[:, :, perm].reshape(D, 4096))
    w_k = np.ascontiguousarray(w_qkv[:, 4096:5120].reshape(D, 8, 128)[:, :, perm].reshape(D, 1024))
    w_v = np.ascontiguousarray(w_qkv[:, 5120:6144])

    wq_lin = np.ascontiguousarray(w_q.reshape(32, 128, 32, 128).transpose(2, 1, 0, 3).reshape(32, 128, D))
    wk_lin = np.ascontiguousarray(w_k.reshape(32, 128, 8, 128).transpose(2, 1, 0, 3).reshape(8, 128, D))
    wv_lin = np.ascontiguousarray(w_v.reshape(32, 128, 1024))
    wo_lin = np.ascontiguousarray(w_o.reshape(32, 128, 8, 512).transpose(2, 0, 1, 3))
    wg_lin = np.ascontiguousarray(w_gate.reshape(NIT, 128, 32, 128).transpose(0, 3, 2, 1).reshape(NIT, 128, D))
    wu_lin = np.ascontiguousarray(w_up.reshape(NIT, 128, 32, 128).transpose(0, 3, 2, 1).reshape(NIT, 128, D))
    wd_lin = np.ascontiguousarray(np.ascontiguousarray(w_down.T).reshape(NIT, 128, D))

    consts = np.zeros((128, 4), np.float32)
    consts[:, 0] = 1.0 / D
    consts[:, 1] = EPS
    consts[:, 2] = -1.0
    consts[:, 3] = -SHIFT
    ones_row = np.ones((1, 128), np.float32)
    ones_col = np.ones((128, 1), np.float32)
    ident = np.eye(128, dtype=np.float32)

    inv_freq = (1.0 / (10000.0 ** (np.arange(0, HD, 2, dtype=np.float64) / HD)))

    def tables(pos):
        ang = pos[None, :].astype(np.float64) * inv_freq[:, None]
        cos = np.cos(ang).astype(np.float32)
        sin = np.sin(ang).astype(np.float32)
        cos_d = np.concatenate([cos, cos], axis=0)
        sin_d = np.concatenate([-sin, sin], axis=0)
        return np.ascontiguousarray(cos_d), np.ascontiguousarray(sin_d)

    cosk, sink = tables(np.arange(S))

    shared = dict(consts=consts, ones_row=ones_row, ones_col=ones_col, ident_in=ident,
                  cosk_in=cosk, sink_in=sink, wk_lin=wk_lin, wv_lin=wv_lin, wq_lin=wq_lin,
                  wo_lin=wo_lin, wg_lin=wg_lin, wu_lin=wu_lin, wd_lin=wd_lin)

    in_maps = []
    for core in range(8):
        b, c = divmod(core, 4)
        tl = tiles_for_core(c)
        rows = np.concatenate([np.arange(t * 128, (t + 1) * 128) for t in tl])
        x_own = np.ascontiguousarray(x[b][rows])
        cosq, sinq = tables(rows)
        masks = np.zeros((32, 128, 128), np.float32)
        kk = np.arange(128)[:, None]
        qq = np.arange(128)[None, :]
        for j in range(32):
            p = (31 - j) // 4
            t = tl[p]
            masks[j] = (128 * j + kk <= 128 * t + qq).astype(np.float32)
        m = dict(shared)
        m.update(x_kv=np.ascontiguousarray(x[b]), x_own=x_own, cosq_in=cosq,
                 sinq_in=sinq, masks_in=masks)
        in_maps.append(m)
    return in_maps


def kernel(L0_attn_x, L0_attn_w_qkv, L0_attn_w_o, L0_ffn_w_gate, L0_ffn_w_up, L0_ffn_w_down):
    in_maps = _host_inputs(L0_attn_x, L0_attn_w_qkv, L0_attn_w_o,
                           L0_ffn_w_gate, L0_ffn_w_up, L0_ffn_w_down)
    nc = build_nc()
    res = run_bass_kernel_spmd(nc, in_maps, core_ids=list(range(8)))
    y = np.empty((2, S, D), np.float32)
    for core in range(8):
        b, c = divmod(core, 4)
        tl = tiles_for_core(c)
        o = res.results[core]["out"]
        for p, t in enumerate(tl):
            y[b, t * 128:(t + 1) * 128, :] = o[p * 128:(p + 1) * 128, :]
    return y


# version 06
# speedup vs baseline: 1.1934x; 1.0022x over previous
import contextlib
import math

import numpy as np
import concourse.bass as bass
import concourse.mybir as mybir
from concourse.bass_utils import run_bass_kernel_spmd

F32 = mybir.dt.float32
BF16 = mybir.dt.bfloat16
AF = mybir.ActivationFunctionType
ALU = mybir.AluOpType
AX = mybir.AxisListType

D = 4096
S = 4096
NH = 32
NG = 8
HD = 128
NIT = 86
TOWN = 1024
SCALE = 1.0 / math.sqrt(HD)
SHIFT = 20.0
EPS = 1e-5
FFN_BLOCKS = [16, 16, 16, 16, 16, 6]


def tiles_for_core(c):
    out = []
    for p in range(8):
        out.append(((31 - c) - 8 * (p // 2)) if p % 2 == 0 else ((24 + c) - 8 * (p // 2)))
    return out


def wA_of(j):
    return 128 * min(4, -(-(32 - j) // 4))


def wB_of(j):
    return 128 * min(4, -(-(16 - j) // 4))


class Cnt:
    def __init__(self):
        self.v = {}

    def __getitem__(self, k):
        return self.v.get(k, 0)

    def add(self, k, n):
        self.v[k] = self.v.get(k, 0) + n
        return self.v[k]


def build_nc(dump=False, upto="C"):
    nc = bass.Bass("TRN2", target_bir_lowering=False, detect_race_conditions=False)
    scr_kind = "ExternalOutput" if dump else "Internal"

    x_kv = nc.dram_tensor("x_kv", [S, D], F32, kind="ExternalInput").ap()
    x_own = nc.dram_tensor("x_own", [TOWN, D], F32, kind="ExternalInput").ap()
    consts = nc.dram_tensor("consts", [128, 4], F32, kind="ExternalInput").ap()
    ones_row = nc.dram_tensor("ones_row", [1, 128], F32, kind="ExternalInput").ap()
    ones_col = nc.dram_tensor("ones_col", [128, 1], F32, kind="ExternalInput").ap()
    ident_in = nc.dram_tensor("ident_in", [128, 128], F32, kind="ExternalInput").ap()
    masks_in = nc.dram_tensor("masks_in", [32, 128, 128], F32, kind="ExternalInput").ap()
    cosk_in = nc.dram_tensor("cosk_in", [128, S], F32, kind="ExternalInput").ap()
    sink_in = nc.dram_tensor("sink_in", [128, S], F32, kind="ExternalInput").ap()
    cosq_in = nc.dram_tensor("cosq_in", [128, TOWN], F32, kind="ExternalInput").ap()
    sinq_in = nc.dram_tensor("sinq_in", [128, TOWN], F32, kind="ExternalInput").ap()
    wk_lin = nc.dram_tensor("wk_lin", [NG, 128, D], F32, kind="ExternalInput").ap()
    wv_lin = nc.dram_tensor("wv_lin", [32, 128, 1024], F32, kind="ExternalInput").ap()
    wq_lin = nc.dram_tensor("wq_lin", [32, 128, D], F32, kind="ExternalInput").ap()
    wo_lin = nc.dram_tensor("wo_lin", [8, 32, 128, 512], F32, kind="ExternalInput").ap()
    wg_lin = nc.dram_tensor("wg_lin", [NIT, 128, D], F32, kind="ExternalInput").ap()
    wu_lin = nc.dram_tensor("wu_lin", [NIT, 128, D], F32, kind="ExternalInput").ap()
    wd_lin = nc.dram_tensor("wd_lin", [NIT, 128, D], F32, kind="ExternalInput").ap()

    out = nc.dram_tensor("out", [TOWN, D], F32, kind="ExternalOutput").ap()

    wkb = nc.dram_tensor("wkb", [NG, 128, D], BF16, kind="Internal").ap()
    wvb = nc.dram_tensor("wvb", [32, 128, 1024], BF16, kind="Internal").ap()
    ktr_scr = nc.dram_tensor("ktr_scr", [NG, 128, S], BF16, kind=scr_kind).ap()
    v_scr = nc.dram_tensor("v_scr", [32, 128, 1024], BF16, kind=scr_kind).ap()
    qtr_scr = nc.dram_tensor("qtr_scr", [NH, 128, TOWN], BF16, kind=scr_kind).ap()
    xnt_scr = nc.dram_tensor("xnt_scr", [32, 128, TOWN], BF16, kind=scr_kind).ap()
    if dump:
        attn_dump = nc.dram_tensor("attn_dump", [128, NH * TOWN], BF16,
                                   kind="ExternalOutput").ap()

    cnt = Cnt()

    with contextlib.ExitStack() as TOP:
        E = TOP.enter_context
        sems = {}
        semnames = ["yP0", "gP0", "gD2D", "qtb", "xsp", "xrl", "sSQ", "vVAR", "sSD",
                    "vXN", "tTR", "sTE", "tK", "sKE", "vKR", "tVg", "vVE", "tQ",
                    "sQE", "vQR", "tST", "sE", "vMK", "tPV", "sRC", "tBC", "sRB",
                    "vEV", "tY", "vY", "tG", "tU", "sSIL", "vIN", "tD", "vDE"]
        for extra in ["swL2", "swL3", "swL4", "swL5", "swL6", "swL7", "wkL2", "wvL2"]:
            semnames.append(extra)
        for pair in ["xaL", "tbL", "wkL", "wvL", "swL", "kspL", "x2L", "wqL",
                     "qswL", "qspL", "kgl", "vgl", "qgl", "woL", "yYL", "wgL",
                     "wuL", "wdL", "gAL"]:
            semnames += [pair + "0", pair + "1"]
        semnames += ["vspL0", "vspL1", "vspL2", "vspL3", "gAL2", "gAL3"]
        semnames += ["xoL0", "xoL1", "xoL2", "xoL3"]
        for name in semnames:
            sems[name] = E(nc.semaphore(name))

        def W(eng, name, val):
            if val > 0:
                eng.wait_ge(sems[name], val)

        consts_sb = E(nc.sbuf_tensor("consts_sb", [128, 4], F32))
        onesr_sb = E(nc.sbuf_tensor("onesr_sb", [1, 128], F32))
        onesc_sb = E(nc.sbuf_tensor("onesc_sb", [128, 1], BF16))
        ident_sb = E(nc.sbuf_tensor("ident_sb", [128, 128], BF16))
        masks_sb = E(nc.sbuf_tensor("masks_sb", [128, 32 * 128], BF16))
        c_rd = consts_sb[:, 0:1]
        c_eps = consts_sb[:, 1:2]
        c_m1 = consts_sb[:, 2:3]
        c_shift = consts_sb[:, 3:4]

        psf = E(nc.psum_tensor("psf", [128, 3072], F32))
        pst = E(nc.psum_tensor("pst", [128, 2048], BF16))

        def bank(i, w=512, row0=0, rows=128):
            return psf[row0:row0 + rows, i * 512:i * 512 + w]

        def bank2(i, w=1024):
            return psf[:, i * 512:i * 512 + w]

        with nc.Block("p0") as blk:
            @blk.sync
            def _(y):
                y.dma_start(consts_sb[:, :], consts).then_inc(sems["yP0"], 16)
                y.dma_start(onesr_sb[:, :], ones_row).then_inc(sems["yP0"], 16)
                cnt.add("yP0", 32)
                y.wait_ge(sems["yP0"], 32)

            @blk.gpsimd
            def _(g):
                g.dma_start(onesc_sb[:, :], ones_col).then_inc(sems["gP0"], 16)
                g.dma_start(ident_sb[:, :], ident_in).then_inc(sems["gP0"], 16)
                g.dma_start(masks_sb.ap().rearrange("p (j q) -> p j q", q=128),
                            masks_in.transpose([1, 0, 2])).then_inc(sems["gP0"], 16)
                cnt.add("gP0", 48)
                g.dma_start(wkb, wk_lin).then_inc(sems["gD2D"], 16)
                g.dma_start(wvb, wv_lin).then_inc(sems["gD2D"], 16)
                cnt.add("gD2D", 32)
                g.wait_ge(sems["gP0"], 48)

        with contextlib.ExitStack() as ST:
            EA = ST.enter_context
            xa = [EA(nc.sbuf_tensor(f"xa{i}", [128, D], F32)) for i in range(2)]
            xcen = EA(nc.sbuf_tensor("xcen", [128, D], F32))
            xn = [EA(nc.sbuf_tensor(f"xn{i}", [128, D], BF16)) for i in range(2)]
            xnt = EA(nc.sbuf_tensor("xnt", [128, 32 * 512], BF16))
            bnso = EA(nc.sbuf_tensor("bnso", [128, 48], F32))
            t_mv = EA(nc.sbuf_tensor("t_mv", [128, 4], F32))
            t_sd = EA(nc.sbuf_tensor("t_sd", [128, 2], F32))
            t_r = EA(nc.sbuf_tensor("t_r", [128, 2], F32))
            ck = [EA(nc.sbuf_tensor(f"ck{i}", [128, 512], F32)) for i in range(2)]
            sk = [EA(nc.sbuf_tensor(f"sk{i}", [128, 512], F32)) for i in range(2)]
            wkS = [EA(nc.sbuf_tensor(f"wkS{i}", [128, D], BF16)) for i in range(3)]
            wvS = [EA(nc.sbuf_tensor(f"wvS{i}", [128, 4 * 1024], BF16)) for i in range(3)]
            kqf = [EA(nc.sbuf_tensor(f"kqf{i}", [128, 512], F32)) for i in range(8)]
            kswp = [EA(nc.sbuf_tensor(f"kswp{i}", [128, 512], F32)) for i in range(8)]
            ropet = EA(nc.sbuf_tensor("ropet", [128, 512], F32))
            ropet2 = EA(nc.sbuf_tensor("ropet2", [128, 512], F32))
            ktr_t = [EA(nc.sbuf_tensor(f"ktr_t{i}", [128, 512], BF16)) for i in range(2)]
            v_bf = [EA(nc.sbuf_tensor(f"v_bf{i}", [128, 1024], BF16)) for i in range(4)]

            def wk_lidx(c, g):
                return c * 8 + g

            def wv_lidx(c, h2, grp):
                return c * 16 + h2 * 8 + grp

            def gw_A(c, g=None, h2=None, grp=None):
                if g is not None:
                    return c * 24 + g + 1
                return c * 24 + 8 + h2 * 8 + grp + 1

            with nc.Block("pA") as blk:
                @blk.gpsimd
                def _(g):
                    g.wait_ge(sems["gD2D"], 32)
                    for c in range(8):
                        for gg in range(8):
                            kidx = wk_lidx(c, gg)
                            W(g, "tK", kidx - 2)
                            g.dma_start(wkS[kidx % 3][:, :], wkb[gg]).then_inc(sems[f"wkL{kidx % 3}"], 16)
                            cnt.add(f"wkL{kidx % 3}", 16)
                        for h2 in range(2):
                            for grp in range(8):
                                vl = wv_lidx(c, h2, grp)
                                W(g, "tVg", vl - 2)
                                g.dma_start(
                                    wvS[vl % 3].ap().rearrange("p (d q) -> p d q", q=1024),
                                    wvb[4 * grp:4 * grp + 4].transpose([1, 0, 2]),
                                ).then_inc(sems[f"wvL{vl % 3}"], 16)
                                cnt.add(f"wvL{vl % 3}", 16)

                @blk.sync
                def _(y):
                    y.wait_ge(sems["yP0"], 32)
                    for c in range(9):
                        if c < 8:
                            W(y, "vKR", 8 * (c - 1))
                            y.dma_start(ck[c % 2][:, :], cosk_in[:, c * 512:(c + 1) * 512]).then_inc(sems[f"tbL{c % 2}"], 16)
                            y.dma_start(sk[c % 2][:, :], sink_in[:, c * 512:(c + 1) * 512]).then_inc(sems[f"tbL{c % 2}"], 16)
                            cnt.add(f"tbL{c % 2}", 32)
                            for tt in range(4):
                                gt = 4 * c + tt
                                W(y, "vXN", gt - 1)
                                y.dma_start(xa[gt % 2][:, :], x_kv[gt * 128:(gt + 1) * 128, :]).then_inc(sems[f"xaL{gt % 2}"], 16)
                                cnt.add(f"xaL{gt % 2}", 16)
                        if c > 0:
                            cp = c - 1
                            for gg in range(8):
                                kidx = cp * 8 + gg
                                W(y, "sKE", kidx + 1)
                                W(y, "vKR", kidx - 7)
                                y.dma_start(kswp[kidx % 8][0:64, :], kqf[kidx % 8][64:128, :]).then_inc(sems[f"swL{kidx % 8}"], 16)
                                y.dma_start(kswp[kidx % 8][64:128, :], kqf[kidx % 8][0:64, :]).then_inc(sems[f"swL{kidx % 8}"], 16)
                                cnt.add(f"swL{kidx % 8}", 32)
                                W(y, "vKR", kidx + 1)
                                y.dma_start(ktr_scr[gg, :, cp * 512:(cp + 1) * 512], ktr_t[kidx % 2][:, :]).then_inc(sems[f"kspL{kidx % 2}"], 16)
                                cnt.add(f"kspL{kidx % 2}", 16)
                            for h2 in range(2):
                                for tp in range(2):
                                    half = cp * 2 + h2
                                    vidx = 2 * half + tp
                                    slot = (half % 2) * 2 + tp
                                    W(y, "vVE", vidx + 1)
                                    ttg = cp * 4 + 2 * h2 + tp
                                    y.dma_start(v_scr[ttg], v_bf[slot][:, :]).then_inc(sems[f"vspL{slot}"], 16)
                                    cnt.add(f"vspL{slot}", 16)
                    for nm in ["kspL0", "kspL1", "vspL0", "vspL1", "vspL2", "vspL3"]:
                        y.wait_ge(sems[nm], cnt[nm])

                @blk.vector
                def _(v):
                    v.wait_ge(sems["yP0"], 32)
                    for c in range(9):
                        if c < 8:
                            for tt in range(4):
                                gt = 4 * c + tt
                                par = gt % 2
                                W(v, f"xaL{par}", 16 * (gt // 2 + 1))
                                W(v, "sSD", gt - 1)
                                for sg in range(8):
                                    v.bn_stats(bnso[:, sg * 6:(sg + 1) * 6], xa[par][:, sg * 512:(sg + 1) * 512])
                                v.bn_aggr(t_mv[:, par * 2:par * 2 + 2], bnso[:, :]).then_inc(sems["vVAR"], 1)
                                cnt.add("vVAR", 1)
                                W(v, "sSD", gt + 1)
                                v.reciprocal(t_r[:, par:par + 1], t_sd[:, par:par + 1])
                                v.tensor_scalar(xcen[:, :], xa[par][:, :], t_mv[:, par * 2:par * 2 + 1],
                                                None, ALU.subtract)
                                W(v, "tTR", 4 * (gt - 1))
                                v.tensor_scalar(xn[par][:, :], xcen[:, :], t_r[:, par:par + 1],
                                                None, ALU.mult).then_inc(sems["vXN"], 1)
                                cnt.add("vXN", 1)
                        if c > 0:
                            cp = c - 1
                            for gg in range(8):
                                kidx = cp * 8 + gg
                                W(v, "sKE", kidx + 1)
                                W(v, f"tbL{cp % 2}", 32 * (cp // 2 + 1))
                                v.tensor_mul(ropet[:, :], kqf[kidx % 8][:, :], ck[cp % 2][:, :])
                                W(v, f"swL{kidx % 8}", 32 * (kidx // 8 + 1))
                                v.tensor_mul(ropet2[:, :], kswp[kidx % 8][:, :], sk[cp % 2][:, :])
                                W(v, f"kspL{kidx % 2}", 16 * (kidx // 2))
                                v.tensor_add(ktr_t[kidx % 2][:, :], ropet[:, :], ropet2[:, :]).then_inc(sems["vKR"], 1)
                                cnt.add("vKR", 1)

                @blk.scalar
                def _(sc):
                    sc.wait_ge(sems["yP0"], 32)
                    for c in range(9):
                        if c < 8:
                            for tt in range(4):
                                gt = 4 * c + tt
                                par = gt % 2
                                W(sc, "vVAR", gt + 1)
                                W(sc, "vXN", gt - 1)
                                sc.activation(t_sd[:, par:par + 1], t_mv[:, par * 2 + 1:par * 2 + 2],
                                              AF.Sqrt, bias=c_eps).then_inc(sems["sSD"], 1)
                                cnt.add("sSD", 1)
                                for q in range(4):
                                    idx = 4 * gt + q
                                    W(sc, "tTR", idx + 1)
                                    W(sc, "tVg", 16 * c)
                                    dst = xnt.ap().rearrange("p (d t) -> p d t", t=512)[:, 8 * q:8 * q + 8, tt * 128:(tt + 1) * 128]
                                    src = pst.ap().rearrange("p (d t) -> p d t", t=128)[:, 8 * (idx % 2):8 * (idx % 2) + 8, :]
                                    sc.copy(dst, src).then_inc(sems["sTE"], 1)
                                    cnt.add("sTE", 1)
                            for gg in range(8):
                                kidx = c * 8 + gg
                                W(sc, "tK", kidx + 1)
                                W(sc, "vKR", kidx - 7)
                                W(sc, f"swL{kidx % 8}", 32 * (kidx // 8))
                                sc.copy(kqf[kidx % 8][:, :], bank(gg % 2)).then_inc(sems["sKE"], 1)
                                cnt.add("sKE", 1)
                            for h2 in range(2):
                                for tp in range(2):
                                    half = c * 2 + h2
                                    slot = (half % 2) * 2 + tp
                                    W(sc, "tVg", 8 * (half + 1))
                                    W(sc, f"vspL{slot}", 16 * (half // 2))
                                    sc.copy(v_bf[slot][:, :], bank2(2 + 2 * tp)).then_inc(sems["vVE"], 1)
                                    cnt.add("vVE", 1)

                @blk.tensor
                def _(t):
                    t.wait_ge(sems["gP0"], 48)
                    for c in range(8):
                        for tt in range(4):
                            gt = 4 * c + tt
                            W(t, "vXN", gt + 1)
                            for q in range(4):
                                idx = 4 * gt + q
                                W(t, "sTE", idx - 1)
                                for k in range(8):
                                    dt = 8 * q + k
                                    ins = t.transpose(pst[:, (idx % 2) * 1024 + k * 128:(idx % 2) * 1024 + (k + 1) * 128],
                                                      xn[gt % 2][:, dt * 128:(dt + 1) * 128],
                                                      ident_sb[:, :])
                                ins.then_inc(sems["tTR"], 1)
                                cnt.add("tTR", 1)
                        W(t, "sTE", 16 * (c + 1))
                        for gg in range(8):
                            kidx = c * 8 + gg
                            W(t, f"wkL{kidx % 3}", 16 * (kidx // 3 + 1))
                            W(t, "sKE", kidx - 1)
                            for dt in range(32):
                                ins = t.matmul(bank(gg % 2), wkS[kidx % 3][:, dt * 128:(dt + 1) * 128],
                                               xnt[:, dt * 512:(dt + 1) * 512],
                                               start=(dt == 0), stop=(dt == 31))
                            ins.then_inc(sems["tK"], 1)
                            cnt.add("tK", 1)
                        for h2 in range(2):
                            half = c * 2 + h2
                            W(t, "vVE", 2 * half)
                            for dt in range(32):
                                vl = wv_lidx(c, h2, dt // 4)
                                if dt % 4 == 0:
                                    W(t, f"wvL{vl % 3}", 16 * (vl // 3 + 1))
                                for tp in range(2):
                                    ttl = 2 * h2 + tp
                                    for colh in range(2):
                                        ins = t.matmul(bank(2 + 2 * tp + colh),
                                                       xnt[:, dt * 512 + ttl * 128:dt * 512 + (ttl + 1) * 128],
                                                       wvS[vl % 3][:, (dt % 4) * 1024 + colh * 512:(dt % 4) * 1024 + (colh + 1) * 512],
                                                       start=(dt == 0), stop=(dt == 31))
                                if dt % 4 == 3:
                                    ins.then_inc(sems["tVg"], 1)
                                    cnt.add("tVg", 1)

        with contextlib.ExitStack() as ST:
            EA = ST.enter_context
            xa = [EA(nc.sbuf_tensor(f"x2a{i}", [128, D], F32)) for i in range(2)]
            xcen = EA(nc.sbuf_tensor("x2cen", [128, D], F32))
            xn = [EA(nc.sbuf_tensor(f"x2n{i}", [128, D], BF16)) for i in range(2)]
            xnt = EA(nc.sbuf_tensor("xnt_own", [128, 32 * 1024], BF16))
            bnso = EA(nc.sbuf_tensor("bnso2", [128, 48], F32))
            t_mv = EA(nc.sbuf_tensor("t2_mv", [128, 4], F32))
            t_sd = EA(nc.sbuf_tensor("t2_sd", [128, 2], F32))
            t_r = EA(nc.sbuf_tensor("t2_r", [128, 2], F32))
            cq = EA(nc.sbuf_tensor("cq", [128, TOWN], F32))
            sq = EA(nc.sbuf_tensor("sq", [128, TOWN], F32))
            wqS = [EA(nc.sbuf_tensor(f"wqS{i}", [128, D], BF16)) for i in range(2)]
            qf = [EA(nc.sbuf_tensor(f"qf{i}", [128, TOWN], F32)) for i in range(2)]
            qswp = [EA(nc.sbuf_tensor(f"qswp{i}", [128, TOWN], F32)) for i in range(2)]
            ropet = EA(nc.sbuf_tensor("ropetq", [128, TOWN], F32))
            ropet2 = EA(nc.sbuf_tensor("ropetq2", [128, TOWN], F32))
            qtr_t = [EA(nc.sbuf_tensor(f"qtr_t{i}", [128, TOWN], BF16)) for i in range(2)]

            base = {k: cnt[k] for k in ["sSQ", "vVAR", "sSD", "vXN", "tTR", "sTE"]}

            with nc.Block("pA2") as blk:
                @blk.gpsimd
                def _(g):
                    for h in range(NH):
                        W(g, "tQ", h - 1)
                        g.dma_start(wqS[h % 2][:, :], wq_lin[h]).then_inc(sems[f"wqL{h % 2}"], 16)
                        cnt.add(f"wqL{h % 2}", 16)

                @blk.sync
                def _(y):
                    y.dma_start(cq[:, :], cosq_in).then_inc(sems["qtb"], 16)
                    y.dma_start(sq[:, :], sinq_in).then_inc(sems["qtb"], 16)
                    cnt.add("qtb", 32)
                    for gt in range(8):
                        W(y, "vXN", base["vXN"] + gt - 1)
                        y.dma_start(xa[gt % 2][:, :], x_own[gt * 128:(gt + 1) * 128, :]).then_inc(sems[f"x2L{gt % 2}"], 16)
                        cnt.add(f"x2L{gt % 2}", 16)
                    for h in range(NH):
                        W(y, "sQE", h + 1)
                        y.dma_start(qswp[h % 2][0:64, :], qf[h % 2][64:128, :]).then_inc(sems[f"qswL{h % 2}"], 16)
                        y.dma_start(qswp[h % 2][64:128, :], qf[h % 2][0:64, :]).then_inc(sems[f"qswL{h % 2}"], 16)
                        cnt.add(f"qswL{h % 2}", 32)
                        W(y, "vQR", h + 1)
                        y.dma_start(qtr_scr[h], qtr_t[h % 2][:, :]).then_inc(sems[f"qspL{h % 2}"], 16)
                        cnt.add(f"qspL{h % 2}", 16)
                    W(y, "sTE", base["sTE"] + 32)
                    y.dma_start(xnt_scr.transpose([1, 0, 2]),
                                xnt.ap().rearrange("p (d t) -> p d t", t=1024)).then_inc(sems["xsp"], 16)
                    cnt.add("xsp", 16)
                    y.wait_ge(sems["qspL0"], cnt["qspL0"])
                    y.wait_ge(sems["qspL1"], cnt["qspL1"])
                    y.wait_ge(sems["xsp"], cnt["xsp"])

                @blk.vector
                def _(v):
                    for gt in range(8):
                        par = gt % 2
                        v.wait_ge(sems[f"x2L{par}"], 16 * (gt // 2 + 1))
                        W(v, "sSD", base["sSD"] + gt - 1)
                        for sg in range(8):
                            v.bn_stats(bnso[:, sg * 6:(sg + 1) * 6], xa[par][:, sg * 512:(sg + 1) * 512])
                        v.bn_aggr(t_mv[:, par * 2:par * 2 + 2], bnso[:, :]).then_inc(sems["vVAR"], 1)
                        cnt.add("vVAR", 1)
                        W(v, "sSD", base["sSD"] + gt + 1)
                        v.reciprocal(t_r[:, par:par + 1], t_sd[:, par:par + 1])
                        v.tensor_scalar(xcen[:, :], xa[par][:, :], t_mv[:, par * 2:par * 2 + 1],
                                        None, ALU.subtract)
                        W(v, "tTR", base["tTR"] + 4 * (gt - 1))
                        v.tensor_scalar(xn[par][:, :], xcen[:, :], t_r[:, par:par + 1],
                                        None, ALU.mult).then_inc(sems["vXN"], 1)
                        cnt.add("vXN", 1)
                    for h in range(NH):
                        W(v, "sQE", h + 1)
                        W(v, "qtb", 32)
                        v.tensor_mul(ropet[:, :], qf[h % 2][:, :], cq[:, :])
                        W(v, f"qswL{h % 2}", 32 * (h // 2 + 1))
                        v.tensor_mul(ropet2[:, :], qswp[h % 2][:, :], sq[:, :])
                        W(v, f"qspL{h % 2}", 16 * (h // 2))
                        v.tensor_add(qtr_t[h % 2][:, :], ropet[:, :], ropet2[:, :]).then_inc(sems["vQR"], 1)
                        cnt.add("vQR", 1)

                @blk.scalar
                def _(sc):
                    for gt in range(8):
                        par = gt % 2
                        W(sc, "vVAR", base["vVAR"] + gt + 1)
                        W(sc, "vXN", base["vXN"] + gt - 1)
                        sc.activation(t_sd[:, par:par + 1], t_mv[:, par * 2 + 1:par * 2 + 2],
                                      AF.Sqrt, bias=c_eps).then_inc(sems["sSD"], 1)
                        cnt.add("sSD", 1)
                        for q in range(4):
                            idx = 4 * gt + q
                            W(sc, "tTR", base["tTR"] + idx + 1)
                            dst = xnt.ap().rearrange("p (d t) -> p d t", t=1024)[:, 8 * q:8 * q + 8, gt * 128:(gt + 1) * 128]
                            src = pst.ap().rearrange("p (d t) -> p d t", t=128)[:, 8 * (idx % 2):8 * (idx % 2) + 8, :]
                            sc.copy(dst, src).then_inc(sems["sTE"], 1)
                            cnt.add("sTE", 1)
                    for h in range(NH):
                        W(sc, "tQ", h + 1)
                        W(sc, "vQR", h - 1)
                        W(sc, f"qswL{h % 2}", 32 * (h // 2))
                        sc.copy(qf[h % 2][:, :], bank2((h % 2) * 2)).then_inc(sems["sQE"], 1)
                        cnt.add("sQE", 1)

                @blk.tensor
                def _(t):
                    for gt in range(8):
                        W(t, "vXN", base["vXN"] + gt + 1)
                        for q in range(4):
                            idx = 4 * gt + q
                            W(t, "sTE", base["sTE"] + idx - 1)
                            for k in range(8):
                                dt = 8 * q + k
                                ins = t.transpose(pst[:, (idx % 2) * 1024 + k * 128:(idx % 2) * 1024 + (k + 1) * 128],
                                                  xn[gt % 2][:, dt * 128:(dt + 1) * 128],
                                                  ident_sb[:, :])
                            ins.then_inc(sems["tTR"], 1)
                            cnt.add("tTR", 1)
                    W(t, "sTE", base["sTE"] + 32)
                    for h in range(NH):
                        W(t, f"wqL{h % 2}", 16 * (h // 2 + 1))
                        W(t, "sQE", h - 1)
                        for dt in range(32):
                            for colh in range(2):
                                ins = t.matmul(bank((h % 2) * 2 + colh),
                                               wqS[h % 2][:, dt * 128:(dt + 1) * 128],
                                               xnt[:, dt * 1024 + colh * 512:dt * 1024 + (colh + 1) * 512],
                                               start=(dt == 0), stop=(dt == 31))
                        ins.then_inc(sems["tQ"], 1)
                        cnt.add("tQ", 1)

        if upto == "A":
            return nc

        with contextlib.ExitStack() as STB:
            EB = STB.enter_context
            attn = EB(nc.sbuf_tensor("attn", [128, NH * TOWN], BF16))
            with contextlib.ExitStack() as ST:
                EA = ST.enter_context
                ktr_g = [EA(nc.sbuf_tensor(f"ktr_g{i}", [128, S], BF16)) for i in range(2)]
                v_g = [EA(nc.sbuf_tensor(f"v_g{i}", [128, S], BF16)) for i in range(2)]
                qtr = [EA(nc.sbuf_tensor(f"qtr{i}", [128, TOWN], BF16)) for i in range(2)]
                ptb = [EA(nc.sbuf_tensor(f"ptb{i}", [128, 512], BF16)) for i in range(4)]
                lnt = EA(nc.sbuf_tensor("lnt", [1, 512], F32))
                rcp = EA(nc.sbuf_tensor("rcp", [1, 512], F32))
                rb = EA(nc.sbuf_tensor("rb", [128, 512], F32))

                PASS = []
                for h in range(NH):
                    PASS.append((h, 0, 32))
                    PASS.append((h, 1, 16))
                e_of_pass = [0]
                for (_, _, nj) in PASS:
                    e_of_pass.append(e_of_pass[-1] + nj)

                def o_bank(pl):
                    return 2 if pl == 0 else 4

                def s_bank(pl):
                    return 3 if pl == 0 else 5

                def wof(pl, j):
                    return wA_of(j) if pl == 0 else wB_of(j)

                def masked(pl, j):
                    return (j >= 16) if pl == 0 else True

                def stbank(pl, j):
                    if j < 6:
                        return (0, 1)[j % 2]
                    return ((0, 1, 4) if pl == 0 else (0, 1, 2))[(j - 6) % 3]

                def mask_count_upto(p, j):
                    n = 16 * p
                    h, pl, nj = PASS[p]
                    for jj in range(j + 1):
                        if masked(pl, jj):
                            n += 1
                    return n

                bSE = cnt["sE"]; bTST = cnt["tST"]; bTPV = cnt["tPV"]; bVMK = cnt["vMK"]
                bSRC = cnt["sRC"]; bTBC = cnt["tBC"]; bSRB = cnt["sRB"]; bVEV = cnt["vEV"]

                with nc.Block("pB") as blk:
                    @blk.sync
                    def _(y):
                        for gg in range(NG):
                            if gg >= 2:
                                W(y, "tPV", bTPV + 192 * (gg - 1))
                            y.dma_start(ktr_g[gg % 2][:, :], ktr_scr[gg]).then_inc(sems[f"kgl{gg % 2}"], 16)
                            cnt.add(f"kgl{gg % 2}", 16)
                            y.dma_start(v_g[gg % 2].ap().rearrange("p (j q) -> p j q", q=128),
                                        v_scr[:, :, gg * 128:(gg + 1) * 128].transpose([1, 0, 2])).then_inc(sems[f"vgl{gg % 2}"], 16)
                            cnt.add(f"vgl{gg % 2}", 16)
                            for hq in range(4):
                                h = gg * 4 + hq
                                if h >= 2:
                                    W(y, "tST", bTST + e_of_pass[2 * (h - 1)])
                                y.dma_start(qtr[h % 2][:, :], qtr_scr[h]).then_inc(sems[f"qgl{h % 2}"], 16)
                                cnt.add(f"qgl{h % 2}", 16)

                    @blk.scalar
                    def _(sc):
                        for p, (h, pl, nj) in enumerate(PASS):
                            for j in range(nj):
                                e = e_of_pass[p] + j
                                w = wof(pl, j)
                                W(sc, "tST", bTST + e + 1)
                                W(sc, "tPV", bTPV + e - 3)
                                sc.activation(ptb[e % 4][:, 0:w], bank(stbank(pl, j), w), AF.Exp,
                                              bias=c_shift, scale=SCALE).then_inc(sems["sE"], 1)
                                cnt.add("sE", 1)
                            W(sc, "tPV", bTPV + e_of_pass[p + 1])
                            sc.activation(lnt[:, :], bank(s_bank(pl), 512, 0, 1), AF.Ln)
                            sc.activation(rcp[:, :], lnt[:, :], AF.Exp, scale=-1.0).then_inc(sems["sRC"], 1)
                            cnt.add("sRC", 1)
                            W(sc, "tBC", bTBC + p + 1)
                            W(sc, "vEV", bVEV + p)
                            sc.copy(rb[:, :], bank(s_bank(pl))).then_inc(sems["sRB"], 1)
                            cnt.add("sRB", 1)

                    @blk.vector
                    def _(v):
                        v.wait_ge(sems["gP0"], 48)
                        for p, (h, pl, nj) in enumerate(PASS):
                            for j in range(nj):
                                if not masked(pl, j):
                                    continue
                                e = e_of_pass[p] + j
                                w = wof(pl, j)
                                W(v, "sE", bSE + e + 1)
                                v.tensor_mul(ptb[e % 4][:, w - 128:w], ptb[e % 4][:, w - 128:w],
                                             masks_sb[:, j * 128:(j + 1) * 128]).then_inc(sems["vMK"], 1)
                                cnt.add("vMK", 1)
                            W(v, "sRB", bSRB + p + 1)
                            v.tensor_mul(attn[:, h * 1024 + pl * 512:h * 1024 + (pl + 1) * 512],
                                         bank2(o_bank(pl), 512), rb[:, :]).then_inc(sems["vEV"], 1)
                            cnt.add("vEV", 1)

                    @blk.tensor
                    def _(t):
                        t.wait_ge(sems["gP0"], 48)
                        ledger = {}

                        def bank_wait(b):
                            if b in ledger:
                                W(t, ledger[b][0], ledger[b][1])

                        def emit_ST(p, j):
                            h, pl, nj = PASS[p]
                            gg = h // 4
                            e = e_of_pass[p] + j
                            w = wof(pl, j)
                            b = stbank(pl, j)
                            bank_wait(b)
                            ledger[b] = ("sE", bSE + e + 1)
                            if j == 0:
                                W(t, f"kgl{gg % 2}", 16 * (gg // 2 + 1))
                                W(t, f"qgl{h % 2}", 16 * (h // 2 + 1))
                            t.matmul(bank(b, w), ktr_g[gg % 2][:, j * 128:(j + 1) * 128],
                                     qtr[h % 2][:, pl * 512:pl * 512 + w],
                                     start=True, stop=True).then_inc(sems["tST"], 1)
                            cnt.add("tST", 1)

                        def emit_PV(p, j):
                            h, pl, nj = PASS[p]
                            gg = h // 4
                            e = e_of_pass[p] + j
                            w = wof(pl, j)
                            W(t, "sE", bSE + e + 1)
                            if masked(pl, j):
                                W(t, "vMK", bVMK + mask_count_upto(p, j))
                            if j == 0:
                                bank_wait(o_bank(pl))
                                bank_wait(s_bank(pl))
                                ledger[o_bank(pl)] = ("vEV", bVEV + p + 1)
                                ledger[s_bank(pl)] = ("sRB", bSRB + p + 1)
                                W(t, f"vgl{gg % 2}", 16 * (gg // 2 + 1))
                            t.matmul(bank2(o_bank(pl), w), v_g[gg % 2][:, j * 128:(j + 1) * 128],
                                     ptb[e % 4][:, 0:w], start=(j == 0), stop=(j == nj - 1))
                            t.matmul(bank(s_bank(pl), w, 0, 1), onesc_sb[:, :],
                                     ptb[e % 4][:, 0:w],
                                     start=(j == 0), stop=(j == nj - 1)).then_inc(sems["tPV"], 1)
                            cnt.add("tPV", 1)

                        def emit_BC(p):
                            h, pl, nj = PASS[p]
                            W(t, "sRC", bSRC + p + 1)
                            t.matmul(bank(s_bank(pl)), onesr_sb[:, :], rcp[:, :],
                                     start=True, stop=True).then_inc(sems["tBC"], 1)
                            cnt.add("tBC", 1)

                        LAG = 3
                        pend_bc = None
                        for p in range(len(PASS)):
                            h, pl, nj = PASS[p]
                            for j in range(nj):
                                emit_ST(p, j)
                                if j == 0 and pend_bc is not None:
                                    emit_BC(pend_bc)
                                    pend_bc = None
                                if j >= LAG:
                                    emit_PV(p, j - LAG)
                            for j in range(nj - LAG, nj):
                                emit_PV(p, j)
                            pend_bc = p
                        emit_BC(pend_bc)

                    if dump:
                        @blk.gpsimd
                        def _(g):
                            g.wait_ge(sems["vEV"], bVEV + 64)
                            g.dma_start(attn_dump, attn[:, :]).then_inc(sems["gAL0"], 16)
                            cnt.add("gAL0", 16)
                            g.wait_ge(sems["gAL0"], cnt["gAL0"])

            if upto == "B":
                return nc

            with contextlib.ExitStack() as ST:
                EA = ST.enter_context
                woS = [EA(nc.sbuf_tensor(f"woS{i}", [128, 32 * 512], BF16)) for i in range(2)]
                xo = [EA(nc.sbuf_tensor(f"xo{i}", [128, 512], F32)) for i in range(4)]
                yf = [EA(nc.sbuf_tensor(f"yf{i}", [128, 512], F32)) for i in range(2)]
                bTY = cnt["tY"]; bVY = cnt["vY"]

                with nc.Block("pC1") as blk:
                    @blk.gpsimd
                    def _(g):
                        for dc in range(8):
                            W(g, "tY", bTY + 8 * (dc - 1))
                            g.dma_start(woS[dc % 2].ap().rearrange("p (a q) -> p a q", q=512),
                                        wo_lin[dc].transpose([1, 0, 2])).then_inc(sems[f"woL{dc % 2}"], 16)
                            cnt.add(f"woL{dc % 2}", 16)

                    @blk.sync
                    def _(y):
                        for dc in range(8):
                            for tt in range(8):
                                qi = dc * 8 + tt
                                W(y, "vY", bVY + qi - 3)
                                y.dma_start(xo[qi % 4][:, :], x_own[tt * 128:(tt + 1) * 128, dc * 512:(dc + 1) * 512]).then_inc(sems[f"xoL{qi % 4}"], 16)
                                cnt.add(f"xoL{qi % 4}", 16)
                                W(y, "vY", bVY + qi + 1)
                                y.dma_start(out[tt * 128:(tt + 1) * 128, dc * 512:(dc + 1) * 512], yf[qi % 2][:, :]).then_inc(sems[f"yYL{qi % 2}"], 16)
                                cnt.add(f"yYL{qi % 2}", 16)
                        y.wait_ge(sems["yYL0"], cnt["yYL0"])
                        y.wait_ge(sems["yYL1"], cnt["yYL1"])

                    @blk.tensor
                    def _(t):
                        for dc in range(8):
                            W(t, f"woL{dc % 2}", 16 * (dc // 2 + 1))
                            for tt in range(8):
                                qi = dc * 8 + tt
                                W(t, "vY", bVY + qi - 1)
                                for at in range(32):
                                    ins = t.matmul(bank(qi % 2), attn[:, at * 1024 + tt * 128:at * 1024 + (tt + 1) * 128],
                                                   woS[dc % 2][:, at * 512:(at + 1) * 512],
                                                   start=(at == 0), stop=(at == 31))
                                ins.then_inc(sems["tY"], 1)
                                cnt.add("tY", 1)

                    @blk.vector
                    def _(v):
                        for dc in range(8):
                            for tt in range(8):
                                qi = dc * 8 + tt
                                W(v, "tY", bTY + qi + 1)
                                v.wait_ge(sems[f"xoL{qi % 4}"], 16 * (qi // 4 + 1))
                                W(v, f"yYL{qi % 2}", 16 * (qi // 2))
                                v.tensor_add(yf[qi % 2][:, :], bank(qi % 2), xo[qi % 4][:, :]).then_inc(sems["vY"], 1)
                                cnt.add("vY", 1)

        with contextlib.ExitStack() as ST:
            EA = ST.enter_context
            xnt = EA(nc.sbuf_tensor("xnt_c2", [128, 32 * 1024], BF16))
            inter = EA(nc.sbuf_tensor("inter", [128, 16 * 1024], BF16))
            wgS = [EA(nc.sbuf_tensor(f"wgS{i}", [128, D], BF16)) for i in range(2)]
            wuS = [EA(nc.sbuf_tensor(f"wuS{i}", [128, D], BF16)) for i in range(2)]
            wdS = [EA(nc.sbuf_tensor(f"wdS{i}", [128, 16 * 512], BF16)) for i in range(2)]
            gsil = [EA(nc.sbuf_tensor(f"gsil{i}", [128, 1024], BF16)) for i in range(2)]
            ydf = [EA(nc.sbuf_tensor(f"ydf{i}", [128, 512], F32)) for i in range(4)]
            bYY0 = cnt["yYL0"]; bYY1 = cnt["yYL1"]
            bGA = [cnt["gAL0"], cnt["gAL1"], cnt["gAL2"], cnt["gAL3"]]
            bTG = cnt["tG"]; bTU = cnt["tU"]; bSIL = cnt["sSIL"]; bVIN = cnt["vIN"]
            bTD = cnt["tD"]; bVDE = cnt["vDE"]

            it0_of = [0]
            for nb in FFN_BLOCKS:
                it0_of.append(it0_of[-1] + nb)
            NB = len(FFN_BLOCKS)

            with nc.Block("pC2") as blk:
                def wgu_loads(g, b, lo=0, hi=None):
                    for li in range(lo, FFN_BLOCKS[b] if hi is None else hi):
                        gi = it0_of[b] + li
                        W(g, "sSIL", bSIL + gi - 1)
                        g.dma_start(wgS[gi % 2][:, :], wg_lin[gi]).then_inc(sems[f"wgL{gi % 2}"], 16)
                        cnt.add(f"wgL{gi % 2}", 16)
                        W(g, "vIN", bVIN + gi - 1)
                        g.dma_start(wuS[gi % 2][:, :], wu_lin[gi]).then_inc(sems[f"wuL{gi % 2}"], 16)
                        cnt.add(f"wuL{gi % 2}", 16)

                def emit_accums(g, b, dcg):
                    if b == 0 and dcg == 0:
                        g.wait_ge(sems["yYL0"], bYY0)
                        g.wait_ge(sems["yYL1"], bYY1)
                    for k in range(8):
                        r = dcg * 8 + k
                        dc, tt = divmod(r, 8)
                        W(g, "vDE", bVDE + 64 * b + r + 1)
                        g.dma_start(out[tt * 128:(tt + 1) * 128, dc * 512:(dc + 1) * 512],
                                    ydf[r % 4][:, :], accum_op=ALU.add).then_inc(sems[f"gAL{r % 4}"], 16)
                        cnt.add(f"gAL{r % 4}", 16)

                @blk.gpsimd
                def _(g):
                    wgu_loads(g, 0)
                    for b, nb in enumerate(FFN_BLOCKS):
                        for dc in range(8):
                            if dc >= 2:
                                emit_accums(g, b, dc - 2)
                            W(g, "tD", bTD + 64 * b + 8 * (dc - 1))
                            g.dma_start(wdS[dc % 2].ap().rearrange("p (i q) -> p i q", q=512)[:, 0:nb, :],
                                        wd_lin[it0_of[b]:it0_of[b] + nb, :, dc * 512:(dc + 1) * 512].transpose([1, 0, 2])).then_inc(sems[f"wdL{dc % 2}"], 16)
                            cnt.add(f"wdL{dc % 2}", 16)
                            if dc == 3 and b + 1 < NB:
                                wgu_loads(g, b + 1, 0, 2)
                        emit_accums(g, b, 6)
                        emit_accums(g, b, 7)
                        if b + 1 < NB:
                            wgu_loads(g, b + 1, 2)
                    for nm in ["gAL0", "gAL1", "gAL2", "gAL3"]:
                        g.wait_ge(sems[nm], cnt[nm])

                @blk.sync
                def _(y):
                    y.dma_start(xnt.ap().rearrange("p (d t) -> p d t", t=1024),
                                xnt_scr.transpose([1, 0, 2])).then_inc(sems["xrl"], 16)
                    cnt.add("xrl", 16)
                    for nm in ["gAL0", "gAL1", "gAL2", "gAL3"]:
                        y.wait_ge(sems[nm], cnt[nm])

                @blk.tensor
                def _(t):
                    W(t, "xrl", 16)
                    for b, nb in enumerate(FFN_BLOCKS):
                        for li in range(nb):
                            gi = it0_of[b] + li
                            gbk = (gi % 2) * 2
                            W(t, f"wgL{gi % 2}", 16 * (gi // 2 + 1))
                            W(t, "sSIL", bSIL + gi - 1)
                            if li == 0 and b > 0:
                                W(t, "vDE", bVDE + 64 * b)
                            for dt in range(32):
                                for colh in range(2):
                                    ins = t.matmul(bank(gbk + colh), wgS[gi % 2][:, dt * 128:(dt + 1) * 128],
                                                   xnt[:, dt * 1024 + colh * 512:dt * 1024 + (colh + 1) * 512],
                                                   start=(dt == 0), stop=(dt == 31))
                            ins.then_inc(sems["tG"], 1)
                            cnt.add("tG", 1)
                            W(t, f"wuL{gi % 2}", 16 * (gi // 2 + 1))
                            W(t, "vIN", bVIN + gi)
                            for dt in range(32):
                                for colh in range(2):
                                    ins = t.matmul(bank(4 + colh), wuS[gi % 2][:, dt * 128:(dt + 1) * 128],
                                                   xnt[:, dt * 1024 + colh * 512:dt * 1024 + (colh + 1) * 512],
                                                   start=(dt == 0), stop=(dt == 31))
                            ins.then_inc(sems["tU"], 1)
                            cnt.add("tU", 1)
                        W(t, "vIN", bVIN + it0_of[b] + nb)
                        for r in range(64):
                            dc, tt = divmod(r, 8)
                            if tt == 0:
                                W(t, f"wdL{dc % 2}", 16 * (4 * b + dc // 2 + 1))
                            W(t, "vDE", bVDE + 64 * b + r - 1)
                            if r == 0:
                                W(t, "sSIL", bSIL + it0_of[b] + nb)
                            for li in range(nb):
                                ins = t.matmul(bank(r % 2), inter[:, li * 1024 + tt * 128:li * 1024 + (tt + 1) * 128],
                                               wdS[dc % 2][:, li * 512:(li + 1) * 512],
                                               start=(li == 0), stop=(li == nb - 1))
                            ins.then_inc(sems["tD"], 1)
                            cnt.add("tD", 1)

                @blk.scalar
                def _(sc):
                    for b, nb in enumerate(FFN_BLOCKS):
                        for li in range(nb):
                            gi = it0_of[b] + li
                            W(sc, "tG", bTG + gi + 1)
                            W(sc, "vIN", bVIN + gi - 1)
                            sc.activation(gsil[gi % 2][:, :], bank2((gi % 2) * 2), AF.Silu).then_inc(sems["sSIL"], 1)
                            cnt.add("sSIL", 1)

                @blk.vector
                def _(v):
                    for b, nb in enumerate(FFN_BLOCKS):
                        for li in range(nb):
                            gi = it0_of[b] + li
                            W(v, "tU", bTU + gi + 1)
                            W(v, "sSIL", bSIL + gi + 1)
                            v.tensor_mul(inter[:, li * 1024:(li + 1) * 1024], bank2(4),
                                         gsil[gi % 2][:, :]).then_inc(sems["vIN"], 1)
                            cnt.add("vIN", 1)
                        for r in range(64):
                            W(v, "tD", bTD + 64 * b + r + 1)
                            W(v, f"gAL{r % 4}", bGA[r % 4] + 16 * (16 * b + r // 4))
                            v.tensor_copy(ydf[r % 4][:, :], bank(r % 2)).then_inc(sems["vDE"], 1)
                            cnt.add("vDE", 1)

    return nc



def _host_inputs(L0_attn_x, L0_attn_w_qkv, L0_attn_w_o, L0_ffn_w_gate, L0_ffn_w_up, L0_ffn_w_down):
    x = np.asarray(L0_attn_x, dtype=np.float32)
    w_qkv = np.asarray(L0_attn_w_qkv, dtype=np.float32)
    w_o = np.asarray(L0_attn_w_o, dtype=np.float32)
    w_gate = np.asarray(L0_ffn_w_gate, dtype=np.float32)
    w_up = np.asarray(L0_ffn_w_up, dtype=np.float32)
    w_down = np.asarray(L0_ffn_w_down, dtype=np.float32)

    perm = np.concatenate([np.arange(0, 128, 2), np.arange(1, 128, 2)])
    w_q = np.ascontiguousarray(w_qkv[:, :4096].reshape(D, 32, 128)[:, :, perm].reshape(D, 4096))
    w_k = np.ascontiguousarray(w_qkv[:, 4096:5120].reshape(D, 8, 128)[:, :, perm].reshape(D, 1024))
    w_v = np.ascontiguousarray(w_qkv[:, 5120:6144])

    wq_lin = np.ascontiguousarray(w_q.reshape(32, 128, 32, 128).transpose(2, 1, 0, 3).reshape(32, 128, D))
    wk_lin = np.ascontiguousarray(w_k.reshape(32, 128, 8, 128).transpose(2, 1, 0, 3).reshape(8, 128, D))
    wv_lin = np.ascontiguousarray(w_v.reshape(32, 128, 1024))
    wo_lin = np.ascontiguousarray(w_o.reshape(32, 128, 8, 512).transpose(2, 0, 1, 3))
    wg_lin = np.ascontiguousarray(w_gate.reshape(NIT, 128, 32, 128).transpose(0, 3, 2, 1).reshape(NIT, 128, D))
    wu_lin = np.ascontiguousarray(w_up.reshape(NIT, 128, 32, 128).transpose(0, 3, 2, 1).reshape(NIT, 128, D))
    wd_lin = np.ascontiguousarray(np.ascontiguousarray(w_down.T).reshape(NIT, 128, D))

    consts = np.zeros((128, 4), np.float32)
    consts[:, 0] = 1.0 / D
    consts[:, 1] = EPS
    consts[:, 2] = -1.0
    consts[:, 3] = -SHIFT
    ones_row = np.ones((1, 128), np.float32)
    ones_col = np.ones((128, 1), np.float32)
    ident = np.eye(128, dtype=np.float32)

    inv_freq = (1.0 / (10000.0 ** (np.arange(0, HD, 2, dtype=np.float64) / HD)))

    def tables(pos):
        ang = pos[None, :].astype(np.float64) * inv_freq[:, None]
        cos = np.cos(ang).astype(np.float32)
        sin = np.sin(ang).astype(np.float32)
        cos_d = np.concatenate([cos, cos], axis=0)
        sin_d = np.concatenate([-sin, sin], axis=0)
        return np.ascontiguousarray(cos_d), np.ascontiguousarray(sin_d)

    cosk, sink = tables(np.arange(S))

    shared = dict(consts=consts, ones_row=ones_row, ones_col=ones_col, ident_in=ident,
                  cosk_in=cosk, sink_in=sink, wk_lin=wk_lin, wv_lin=wv_lin, wq_lin=wq_lin,
                  wo_lin=wo_lin, wg_lin=wg_lin, wu_lin=wu_lin, wd_lin=wd_lin)

    in_maps = []
    for core in range(8):
        b, c = divmod(core, 4)
        tl = tiles_for_core(c)
        rows = np.concatenate([np.arange(t * 128, (t + 1) * 128) for t in tl])
        x_own = np.ascontiguousarray(x[b][rows])
        cosq, sinq = tables(rows)
        masks = np.zeros((32, 128, 128), np.float32)
        kk = np.arange(128)[:, None]
        qq = np.arange(128)[None, :]
        for j in range(32):
            p = (31 - j) // 4
            t = tl[p]
            masks[j] = (128 * j + kk <= 128 * t + qq).astype(np.float32)
        m = dict(shared)
        m.update(x_kv=np.ascontiguousarray(x[b]), x_own=x_own, cosq_in=cosq,
                 sinq_in=sinq, masks_in=masks)
        in_maps.append(m)
    return in_maps


def kernel(L0_attn_x, L0_attn_w_qkv, L0_attn_w_o, L0_ffn_w_gate, L0_ffn_w_up, L0_ffn_w_down):
    in_maps = _host_inputs(L0_attn_x, L0_attn_w_qkv, L0_attn_w_o,
                           L0_ffn_w_gate, L0_ffn_w_up, L0_ffn_w_down)
    nc = build_nc()
    res = run_bass_kernel_spmd(nc, in_maps, core_ids=list(range(8)))
    y = np.empty((2, S, D), np.float32)
    for core in range(8):
        b, c = divmod(core, 4)
        tl = tiles_for_core(c)
        o = res.results[core]["out"]
        for p, t in enumerate(tl):
            y[b, t * 128:(t + 1) * 128, :] = o[p * 128:(p + 1) * 128, :]
    return y
